# version 27
# baseline (speedup 1.0000x reference)
import numpy as np
import ml_dtypes
from contextlib import ExitStack

import concourse.bass as bass
import concourse.mybir as mybir
from concourse.bass_utils import run_bass_kernel_spmd

F32 = mybir.dt.float32
BF16 = mybir.dt.bfloat16
FP16 = mybir.dt.float16
ALU = mybir.AluOpType
ACT = mybir.ActivationFunctionType
AX = mybir.AxisListType.X

NCORES = 8
SEQ = 2048
D = 1024
KC = D // 128
NH = 16
HD = 64
INNER = 4096
IKC = INNER // 128

TPA = 256
TPB = 256
TPC = 512
NTA = SEQ // TPA
NTB = SEQ // TPB
NTC = SEQ // TPC

EPS_LN = 1e-5
EPS_ATT = 1e-5
MAGIC = float(np.float32(12582912.0))

ROUND_LADDER = True

bfdt = ml_dtypes.bfloat16


class B:
    def __init__(self):
        self.nc = bass.Bass("TRN2", target_bir_lowering=False)
        self.stack = ExitStack()
        self.sems = {}
        self.cnt = {}
        self.waited = {}
        self.last = {}
        self.lastinc = {}
        self.mkdummy = {}
        self.pools = {}
        self.dirty = {}
        self.engines = {}

    def sem(self, name):
        h = self.stack.enter_context(self.nc.semaphore(name))
        self.sems[name] = h
        self.cnt[name] = 0
        return h

    def pool(self, prefix, n):
        names = []
        for i in range(n):
            self.sem(f"{prefix}{i}")
            names.append(f"{prefix}{i}")
        self.pools[prefix] = {"names": names, "idx": 0}
        return prefix

    def dma(self, qeng, qname, dst, src, prefix, transpose=False):
        p = self.pools[prefix]
        sname = p["names"][p["idx"]]
        p["idx"] = (p["idx"] + 1) % len(p["names"])
        self.wait(qeng, qname, sname)
        if transpose:
            inst = qeng.dma_start_transpose(dst, src)
        else:
            inst = qeng.dma_start(dst, src)
        self.inc(inst, sname, 16)
        return (sname, self.cnt[sname])

    def wait_tok(self, eng, engname, tok):
        if tok is not None:
            self.wait(eng, engname, tok[0], tok[1])

    def wait_pool(self, eng, engname, prefix):
        for sname in self.pools[prefix]["names"]:
            self.wait(eng, engname, sname)

    def sbuf(self, name, shape, dtype):
        return self.stack.enter_context(self.nc.sbuf_tensor(name, shape, dtype))

    def psum(self, name, shape, dtype):
        return self.stack.enter_context(self.nc.psum_tensor(name, shape, dtype))

    def inc(self, inst, sem, k=1):
        inst.then_inc(self.sems[sem], k)
        self.cnt[sem] += k
        self.lastinc[inst] = sem

    def wait(self, eng, engname, sem, val=None):
        if val is None:
            val = self.cnt[sem]
        if val <= 0:
            return
        val = min(val, self.cnt[sem])
        key = (engname, sem)
        if self.waited.get(key, -1) >= val:
            return
        eng.wait_ge(self.sems[sem], val)
        self.waited[key] = val

    def op(self, engname, inst, sem=None, k=1):
        self.last[engname] = inst
        if sem is not None:
            self.inc(inst, sem, k)
        return inst

    @staticmethod
    def _span(ap):
        try:
            dims = ap.ap
            lo = ap.offset
            hi = lo + 1
            for (stride, size) in dims[1:]:
                hi += abs(stride) * (size - 1)
            pb = ap.base_partition
            return (ap.tensor.name, pb, pb + dims[0][1], lo, hi)
        except Exception:
            return (ap.tensor.name, 0, 128, 0, 1 << 40)

    @staticmethod
    def _overlap(a, b):
        return (a[0] == b[0] and a[1] < b[2] and b[1] < a[2]
                and a[3] < b[4] and b[3] < a[4])

    def pre(self, engname, reads=(), writes=()):
        st = self.dirty.get(engname)
        if not isinstance(st, tuple):
            self.dirty[engname] = st = ([], [])
        tw, trd = st
        rs = [self._span(ap) for ap in reads]
        ws = [self._span(ap) for ap in writes]
        hazard = (tw or trd) and (
            (not rs and not ws)
            or any(self._overlap(t, s) for t in tw for s in rs + ws)
            or any(self._overlap(t, s) for t in trd for s in ws))
        if hazard:
            self.engines[engname].drain()
            del tw[:]
            del trd[:]
        tw.extend(ws)
        trd.extend(rs)

    def fence(self, engname, sem):
        inst = self.last.get(engname)
        assert inst is not None, f"fence on idle engine {engname}"
        have = self.lastinc.get(inst)
        if have is None:
            self.inc(inst, sem, 1)
            return
        if have == sem:
            return
        inst2 = self.mkdummy[engname]()
        self.last[engname] = inst2
        self.inc(inst2, sem, 1)


def build_program():
    b = B()
    nc = b.nc
    T, V, S, G, SY = nc.tensor, nc.vector, nc.scalar, nc.gpsimd, nc.sync

    def din(name, shape, dtype=F32):
        return nc.dram_tensor(name, shape, dtype, kind="ExternalInput").ap()

    def dint(name, shape, dtype):
        return nc.dram_tensor(name, shape, dtype, kind="Internal").ap()

    x_in = din("x", [2 * SEQ, D])
    out_hbm = nc.dram_tensor("out", [2 * SEQ, D], F32, kind="ExternalOutput").ap()

    wzt = din("wzt", [D, D]); wht = din("wht", [D, D]); wg0 = din("wg0", [D, D])
    w0pt = din("w0pt", [D, D]); a10 = din("a10", [D, D]); a20 = din("a20", [D, D])
    wqt = din("wqt", [D, D]); wkt = din("wkt", [D, D])
    wvt = din("wvt", [D, D]); wot = din("wot", [D, D])
    wgr = din("wgr", [D, INNER]); wur = din("wur", [D, INNER]); wdr = din("wdr", [INNER, D])
    bias_in = din("biases", [128, 5, KC])
    ident_f = din("ident_f", [128, 128], F32)
    ident_b = din("ident_b", [128, 128], BF16)
    maskd = din("maskd", [128, 128], BF16)
    ones_b = din("ones_b", [128, 128], BF16)
    ones_f = din("ones_f", [128, 128], F32)
    cvals = din("cvals", [128, 8], F32)
    zeros_in = din("zeros", [128, 2 * NH * HD], F32)

    x1a = dint("x1a", [128, KC, 2 * SEQ], F32)
    x1b = dint("x1b", [128, KC, 2 * SEQ], F32)
    xqs = dint("xqs", [128, KC, 2 * SEQ], BF16)
    yqs = dint("yqs", [2 * SEQ, INNER], BF16)
    wqg = dint("wqg", [128, KC * INNER], BF16)
    wqu = dint("wqu", [128, KC * INNER], BF16)
    wqd = dint("wqd", [128, IKC * D], BF16)
    wzh = dint("wzh", [128, KC * D], BF16); wzl = dint("wzl", [128, KC * D], BF16)
    whh = dint("whh", [128, KC * D], BF16); whl = dint("whl", [128, KC * D], BF16)
    wgh = dint("wgh", [128, KC * D], BF16); wgl = dint("wgl", [128, KC * D], BF16)
    wch = dint("wch", [128, KC * D], BF16); wcl = dint("wcl", [128, KC * D], BF16)

    for sname in ["TS", "VS", "SS", "GS"]:
        b.sem(sname)
    LW = b.pool("LW", 4)
    LX = b.pool("LX", 6)
    STP = b.pool("ST", 6)
    LS = b.pool("LS", 3)
    LVQ = b.pool("LV", 3)

    GLB = b.sbuf("GLB", [128, 16], F32)
    BIAS = b.sbuf("BIAS", [128, 5, KC], F32)
    BZN = b.sbuf("BZN", [128, KC], F32)
    IDF = b.sbuf("IDF", [128, 128], F32)
    IDB = b.sbuf("IDB", [128, 128], BF16)
    MSK = b.sbuf("MSK", [128, 128], BF16)
    ONB = b.sbuf("ONB", [128, 128], BF16)
    ONF = b.sbuf("ONF", [128, 128], F32)
    CVS = b.sbuf("CVS", [128, 8], F32)
    AMX = b.sbuf("AMX", [128, 2 * SEQ // 128], F32)
    AMY = b.sbuf("AMY", [128, 2 * SEQ // 128], F32)
    SST = b.sbuf("SST", [128, 2 * KC], F32)

    PB = [b.psum(f"pb{i}", [128, 512], F32) for i in range(6)]
    PT = [b.psum(f"pt{i}", [128, 1024], BF16) for i in range(2)]
    DUM = b.sbuf("DUM", [128, 2], F32)
    b.engines = {"t": T, "v": V, "s": S, "g": G}

    def _dummy(en, mk):
        def f():
            b.engines[en].drain()
            d = b.dirty.get(en)
            if isinstance(d, tuple):
                del d[0][:]
                del d[1][:]
            else:
                b.dirty[en] = ([], [])
            return mk()
        return f
    b.mkdummy = {
        "t": _dummy("t", lambda: T.nop()),
        "v": _dummy("v", lambda: V.engine_nop()),
        "g": _dummy("g", lambda: G.engine_nop()),
        "s": _dummy("s", lambda: S.copy(DUM[0:1, 0:1], DUM[0:1, 1:2])),
    }

    def _aps(*xs):
        return [x for x in xs if hasattr(x, "tensor")]

    def act(out, in_, func, bias=0.0, scale=1.0, accum_out=None, sem=None):
        wr = _aps(out, accum_out)
        b.pre("s", _aps(in_, bias, scale), wr)
        i = b.op("s", S.activation(out, in_, func, bias=bias, scale=scale,
                                   accum_out=accum_out), sem)
        return i

    def vop(inst, sem=None):
        return b.op("v", inst, sem)

    def vts(out, in0, s1, s2, op0, op1=None, sem=None):
        b.pre("v", _aps(in0, s1, s2), [out])
        if op1 is None:
            return vop(V.tensor_scalar(out, in0, s1, None, op0), sem)
        return vop(V.tensor_scalar(out, in0, s1, s2, op0, op1), sem)

    def vtt(out, a_, b_, opx, sem=None):
        b.pre("v", _aps(a_, b_), [out])
        return vop(V.tensor_tensor(out, a_, b_, opx), sem)

    def vstt(out, in0, scal, in1, op0, op1, sem=None):
        b.pre("v", _aps(in0, scal, in1), [out])
        return vop(V.scalar_tensor_tensor(out, in0, scal, in1, op0, op1), sem)

    def vcopy(out, in_, sem=None):
        b.pre("v", _aps(in_), [out])
        return vop(V.tensor_copy(out, in_), sem)

    def vred(out, in_, opx, absval=False, sem=None):
        b.pre("v", _aps(in_), [out])
        return vop(V.tensor_reduce(out, in_, AX, opx,
                                   apply_absolute_value=absval), sem)

    def scopy(out, in_, sem=None):
        b.pre("s", _aps(in_), [out])
        i = b.op("s", S.copy(out, in_), sem)
        return i

    def mm(out, lhsT, rhs, start, stop, sem=None, tile_position=None):
        return b.op("t", T.matmul(out, lhsT, rhs, start=start, stop=stop,
                                  tile_position=tile_position), sem)

    def tr(out, in_, ident, sem=None):
        return b.op("t", T.transpose(out, in_, ident), sem)

    def ldx(dst, src):
        return b.dma(SY, "sy", dst, src, LX)

    _ldq = ((SY, "sy", LX), (G, "g", LW), (S, "s", LS))

    def ldq(i, dst, src):
        q = _ldq[i % 3]
        return b.dma(q[0], q[1], dst, src, q[2])

    def ldw(dst, src):
        return b.dma(G, "g", dst, src, LW)

    def stx(dst, src):
        return b.dma(SY, "sy", dst, src, STP)

    def barrier():
        for en, sname in (("t", "TS"), ("v", "VS"), ("s", "SS"), ("g", "GS")):
            if b.last.get(en) is not None:
                b.fence(en, sname)
        vals = {sn: b.cnt[sn] for sn in ("TS", "VS", "SS", "GS")}
        for eng, en in ((T, "t"), (V, "v"), (S, "s"), (G, "g"), (SY, "sy")):
            for sn in ("TS", "VS", "SS", "GS"):
                b.wait(eng, en, sn, vals[sn])
            for pool in (LW, LX, STP):
                b.wait_pool(eng, en, pool)

    class Arena:
        _uid = [0]

        def __init__(self):
            self.stack = ExitStack()

        def _mk(self, n, dtype):
            Arena._uid[0] += 1
            t = self.stack.enter_context(
                nc.sbuf_tensor(f"ar{Arena._uid[0]}", [128, n], dtype))
            return t[:, :]

        def f32(self, n):
            return self._mk(n, F32)

        def bf16(self, n):
            return self._mk(n, BF16)

        def fp16(self, n):
            return self._mk(n, FP16)

        def w(self, n):
            return self._mk(n, BF16)

        def close(self):
            self.stack.close()

    def r3(ap2d, c, n):
        return ap2d.rearrange("p (c n) -> p c n", c=c, n=n)

    def emit_round(dst, u32, uh, th):
        if not ROUND_LADDER:
            vts(dst, u32, MAGIC - 0.5, MAGIC + 128.0, ALU.add, ALU.subtract)
            return
        vcopy(uh, u32)
        for c in (128.0, 64.0, 32.0, 16.0, 8.0, 4.0, 2.0, 1.0):
            vts(th, uh, c, -c, ALU.is_ge, ALU.mult)
            vtt(uh, uh, th, ALU.add)
        vstt(dst, u32, 128.0, uh, ALU.subtract, ALU.subtract)

    def round_sv_steps(dst, u32, uh, th):
        first = True
        for c in (128.0, 64.0, 32.0, 16.0, 8.0, 4.0, 2.0, 1.0):
            src = u32 if first else uh
            b.wait(S, "s", "VS")
            act(th, src, ACT.Sign, bias=CVS[:, 2:3], sem="SS")
            yield
            b.wait(V, "v", "SS")
            vstt(uh, th, -0.5 * c, src, ALU.mult, ALU.add, sem="VS")
            first = False
            yield
        vstt(dst, u32, 0.5, uh, ALU.subtract, ALU.subtract)
        yield

    def emit_round_sv(dst, u32, uh, th):
        for _ in round_sv_steps(dst, u32, uh, th):
            pass

    def emit_wq_prep():
        ar = Arena()
        NWB = 4
        wbuf = [ar.f32(4096) for _ in range(NWB)]
        sums = ar.f32(IKC)
        tot = ar.f32(2)
        sg = [ar.bf16(4096), ar.bf16(4096)]
        wab = [ar.bf16(4096), ar.bf16(4096)]
        wabl = [ar.bf16(4096), ar.bf16(4096)]
        wab32 = [ar.f32(4096), ar.f32(4096)]
        mg = [ar.bf16(4096), ar.bf16(4096)]
        cons = {}
        gl = [0]
        for (src, dst, nch, width, widx) in (
                (wgr, wqg, KC, INNER, 0), (wur, wqu, KC, INNER, 1),
                (wdr, wqd, IKC, D, 2)):
            src3 = src.rearrange("(c p) o -> p c o", p=128)
            dst3 = r3(dst, nch, width)
            mmtok = {}
            for k in range(nch):
                g_ = gl[0]
                d_ = k % 2
                q = _ldq[g_ % 3]
                b.wait_tok(q[0], q[1], cons.get(g_ - NWB))
                ltok = ldq(g_, wbuf[g_ % NWB][:, 0:width], src3[:, k, :])
                b.wait_tok(S, "s", ltok)
                if k >= 2:
                    b.wait(S, "s", "TS", mmtok.get(k - 2, 0))
                b.wait(S, "s", "VS")
                act(wab32[d_][:, 0:width], wbuf[g_ % NWB][:, 0:width], ACT.Abs,
                    sem="SS")
                cons[g_] = ("SS", b.cnt["SS"])
                gl[0] += 1
                b.wait(V, "v", "SS")
                if k >= 2:
                    b.wait(V, "v", "TS", mmtok.get(k - 2, 0))
                vcopy(wab[d_][:, 0:width], wab32[d_][:, 0:width])
                vtt(wabl[d_][:, 0:width], wab32[d_][:, 0:width], wab[d_][:, 0:width],
                    ALU.subtract, sem="VS")
                b.wait(T, "t", "SS")
                b.wait(T, "t", "VS")
                for cs in range(width // 512):
                    mm(PB[0][0:1, 0:512], ONB[:, 0:1],
                       wab[d_][:, cs * 512:(cs + 1) * 512],
                       (k == 0 and cs == 0), False)
                    mm(PB[0][0:1, 0:512], ONB[:, 0:1],
                       wabl[d_][:, cs * 512:(cs + 1) * 512],
                       False, (k == nch - 1 and cs == width // 512 - 1))
                b.fence("t", "TS")
                mmtok[k] = b.cnt["TS"]
            b.wait(V, "v", "TS")
            vred(tot[0:1, 0:1], PB[0][0:1, 0:512], ALU.add)
            vts(tot[0:1, 1:2], tot[0:1, 0:1], 1.0 / (nch * 128 * width), 1e-5,
                ALU.mult, ALU.max, sem="VS")
            b.wait(T, "t", "VS")
            mm(PB[1][:, 0:1], ONF[0:1, 0:128], tot[0:1, 1:2], True, True, sem="TS")
            b.wait(V, "v", "TS")
            vcopy(GLB[:, widx:widx + 1], PB[1][:, 0:1])
            vts(GLB[:, 3 + widx:4 + widx], PB[1][:, 0:1], 0.5, None, ALU.mult, sem="VS")
            vtok = {}
            for k in range(nch):
                g_ = gl[0]
                d_ = k % 2
                wb = wbuf[g_ % NWB][:, 0:width]
                q = _ldq[g_ % 3]
                b.wait_tok(q[0], q[1], cons.get(g_ - NWB))
                ltok = ldq(g_, wb, src3[:, k, :])
                b.wait_tok(S, "s", ltok)
                b.wait(S, "s", "VS", vtok.get(k - 2, 0) if k >= 2 else None)
                act(sg[d_][:, 0:width], wb, ACT.Sign)
                act(wab32[d_][:, 0:width], wb, ACT.Abs, sem="SS")
                cons[g_] = ("SS", b.cnt["SS"])
                gl[0] += 1
                b.wait_pool(V, "v", LX)
                for _sn in b.pools[STP]["names"]:
                    b.wait(V, "v", _sn)
                m_ = mg[k % 2][:, 0:width]
                b.wait(V, "v", "SS")
                vts(m_, wab32[d_][:, 0:width], GLB[:, 3 + widx:4 + widx], None, ALU.is_ge)
                vtt(m_, m_, sg[d_][:, 0:width], ALU.mult, sem="VS")
                vtok[k] = b.cnt["VS"]
                b.wait(SY, "sy", "VS")
                stx(dst3[:, k, :], m_)
        barrier()
        ar.close()

    def emit_w0_prep():
        ar = Arena()
        stag = [ar.f32(D) for _ in range(2)]
        hib = [ar.bf16(D) for _ in range(2)]
        lob = [ar.bf16(D) for _ in range(2)]
        for (src, dsth, dstl) in ((wzt, wzh, wzl), (wht, whh, whl), (wg0, wgh, wgl)):
            src3 = src.rearrange("(c p) o -> p c o", p=128)
            dh3, dl3 = r3(dsth, KC, D), r3(dstl, KC, D)
            for k in range(KC):
                cu = k % 2
                q = _ldq[k % 3]
                b.wait(q[0], q[1], "SS")
                b.wait(q[0], q[1], "VS")
                ltok = ldq(k, stag[cu][:, :], src3[:, k, :])
                b.wait_tok(S, "s", ltok)
                for _sn in b.pools[STP]["names"]:
                    b.wait(S, "s", _sn)
                scopy(hib[cu][:, :], stag[cu][:, :], sem="SS")
                b.wait(V, "v", "SS")
                vtt(lob[cu][:, :], stag[cu][:, :], hib[cu][:, :], ALU.subtract, sem="VS")
                b.wait(SY, "sy", "SS")
                b.wait(SY, "sy", "VS")
                stx(dh3[:, k, :], hib[cu][:, :])
                stx(dl3[:, k, :], lob[cu][:, :])
        barrier()
        ar.close()
        arM = Arena()
        M1H = r3(arM.bf16(KC * D), KC, D)
        M1L = r3(arM.bf16(KC * D), KC, D)
        ar1 = Arena()
        A1H = r3(ar1.bf16(KC * D), KC, D); A1L = r3(ar1.bf16(KC * D), KC, D)
        WTH = r3(ar1.bf16(KC * D), KC, D); WTL = r3(ar1.bf16(KC * D), KC, D)
        st1 = [ar1.f32(D) for _ in range(2)]
        for (src, dh, dl) in ((a10, A1H, A1L), (w0pt, WTH, WTL)):
            src3 = src.rearrange("(c p) o -> p c o", p=128)
            for k in range(KC):
                cu = k % 2
                q = _ldq[k % 3]
                b.wait(q[0], q[1], "SS")
                b.wait(q[0], q[1], "VS")
                ltok = ldq(k, st1[cu][:, :], src3[:, k, :])
                b.wait_tok(S, "s", ltok)
                scopy(dh[:, k, :], st1[cu][:, :], sem="SS")
                b.wait(V, "v", "SS")
                vtt(dl[:, k, :], st1[cu][:, :], dh[:, k, :], ALU.subtract, sem="VS")
        b.wait(T, "t", "SS")
        b.wait(T, "t", "VS")
        for cc in range(KC):
            for rn in range(2):
                pp = PB[0] if rn == 0 else PB[1]
                b.wait(T, "t", "SS")
                b.wait(T, "t", "VS")
                for j in range(KC):
                    la = A1H[:, j, cc * 128:(cc + 1) * 128]
                    ll = A1L[:, j, cc * 128:(cc + 1) * 128]
                    rh = WTH[:, j, rn * 512:(rn + 1) * 512]
                    rl = WTL[:, j, rn * 512:(rn + 1) * 512]
                    mm(pp[:, 0:512], la, rh, j == 0, False)
                    mm(pp[:, 0:512], ll, rh, False, False)
                    mm(pp[:, 0:512], la, rl, False, j == KC - 1)
                b.fence("t", "TS")
                b.wait(S, "s", "TS")
                scopy(M1H[:, cc, rn * 512:(rn + 1) * 512], pp[:, 0:512], sem="SS")
                b.wait(V, "v", "SS")
                b.wait(V, "v", "TS")
                vtt(M1L[:, cc, rn * 512:(rn + 1) * 512], pp[:, 0:512],
                    M1H[:, cc, rn * 512:(rn + 1) * 512], ALU.subtract, sem="VS")
        barrier()
        ar1.close()
        ar2 = Arena()
        A2H = r3(ar2.bf16(KC * D), KC, D); A2L = r3(ar2.bf16(KC * D), KC, D)
        st2 = [ar2.f32(D) for _ in range(2)]
        och = [ar2.bf16(D) for _ in range(2)]
        ocl = [ar2.bf16(D) for _ in range(2)]
        src3 = a20.rearrange("(c p) o -> p c o", p=128)
        for k in range(KC):
            cu = k % 2
            q = _ldq[k % 3]
            b.wait(q[0], q[1], "SS")
            b.wait(q[0], q[1], "VS")
            ltok = ldq(k, st2[cu][:, :], src3[:, k, :])
            b.wait_tok(S, "s", ltok)
            scopy(A2H[:, k, :], st2[cu][:, :], sem="SS")
            b.wait(V, "v", "SS")
            vtt(A2L[:, k, :], st2[cu][:, :], A2H[:, k, :], ALU.subtract, sem="VS")
        b.wait(T, "t", "SS")
        b.wait(T, "t", "VS")
        wch3, wcl3 = r3(wch, KC, D), r3(wcl, KC, D)
        ot_tok = {}
        for rc in range(KC):
            cu = rc % 2
            for on in range(2):
                pp = PB[0] if on == 0 else PB[1]
                b.wait(T, "t", "SS")
                b.wait(T, "t", "VS")
                for cc in range(KC):
                    la = M1H[:, cc, rc * 128:(rc + 1) * 128]
                    ll = M1L[:, cc, rc * 128:(rc + 1) * 128]
                    rh = A2H[:, cc, on * 512:(on + 1) * 512]
                    rl = A2L[:, cc, on * 512:(on + 1) * 512]
                    mm(pp[:, 0:512], la, rh, cc == 0, False)
                    mm(pp[:, 0:512], ll, rh, False, False)
                    mm(pp[:, 0:512], la, rl, False, cc == KC - 1)
                b.fence("t", "TS")
                b.wait(S, "s", "TS")
                b.wait_tok(S, "s", ot_tok.get((rc - 2, 0)))
                b.wait_tok(V, "v", ot_tok.get((rc - 2, 1)))
                scopy(och[cu][:, on * 512:(on + 1) * 512], pp[:, 0:512], sem="SS")
                b.wait(V, "v", "SS")
                b.wait(V, "v", "TS")
                vtt(ocl[cu][:, on * 512:(on + 1) * 512], pp[:, 0:512],
                    och[cu][:, on * 512:(on + 1) * 512], ALU.subtract, sem="VS")
            b.wait(SY, "sy", "SS")
            b.wait(SY, "sy", "VS")
            ot_tok[(rc, 0)] = stx(wch3[:, rc, :], och[cu][:, :])
            ot_tok[(rc, 1)] = stx(wcl3[:, rc, :], ocl[cu][:, :])
        barrier()
        ar2.close()
        arM.close()

    def emit_seg_a():
        ar = Arena()
        WZH = r3(ar.w(KC * D), KC, D); WZL = r3(ar.w(KC * D), KC, D)
        WHH = r3(ar.w(KC * D), KC, D); WHL = r3(ar.w(KC * D), KC, D)
        WGH = r3(ar.w(KC * D), KC, D); WGL = r3(ar.w(KC * D), KC, D)
        WCH = r3(ar.w(KC * D), KC, D); WCL = r3(ar.w(KC * D), KC, D)
        for wi, (dstw, srcw) in enumerate(((WZH, wzh), (WZL, wzl), (WHH, whh),
                                          (WHL, whl), (WGH, wgh), (WGL, wgl),
                                          (WCH, wch), (WCL, wcl))):
            ldq(wi + 1, dstw[:, :, :], r3(srcw, KC, D))

        NS = TPA // 128
        XT = [r3(ar.f32(NS * D), NS, D) for _ in range(2)]
        XNT = r3(ar.f32(NS * D), NS, D)
        XF = [r3(ar.f32(KC * TPA), KC, TPA) for _ in range(2)]
        XNFH = [r3(ar.bf16(KC * TPA), KC, TPA) for _ in range(2)]
        AA = r3(ar.f32(KC * TPA), KC, TPA)
        BB = r3(ar.f32(KC * TPA), KC, TPA)
        HF = BB
        HBH = r3(ar.bf16(KC * TPA), KC, TPA)
        ZS = ar.f32(TPA)
        GSc = ar.f32(TPA)
        FS = ar.f32(TPA)
        ATS = ar.f32(TPA)
        STT = ar.f32(8)

        def mm3(pp, wh_, wl_, xh_, o, n):
            for k in range(KC):
                mm(pp[:, 0:n], wh_[:, k, o * 128:(o + 1) * 128], xh_[:, k, :],
                   k == 0, False)
                mm(pp[:, 0:n], wl_[:, k, o * 128:(o + 1) * 128], xh_[:, k, :],
                   False, k == KC - 1)
            b.fence("t", "TS")

        x3 = x_in.rearrange("(s p) d -> p s d", p=128)
        b.dma(G, "g", SST[:, :], zeros_in[:, 0:2 * KC], LW)
        b.wait_pool(V, "v", LX)
        vts(BZN[:, :], BIAS[:, 0, :], -1.0, None, ALU.mult, sem="VS")
        b.wait_pool(G, "g", LW)
        b.wait_pool(T, "t", LW)
        b.wait_pool(T, "t", LS)
        b.wait_pool(T, "t", LX)
        b.wait(V, "v", "GS")

        tile_guard = {}
        st_toks = {}
        prep_done = {}
        scan_tok = {}
        hbs_tok = {}
        gate_tok = {}

        NTA2 = 2 * NTA

        def tile_prep(j):
            cu = j % 2
            p_ = j % 2
            sub0 = p_ * (SEQ // 128) + (j // 2) * NS
            if j >= 2:
                b.wait(SY, "sy", "TS", tile_guard[j - 2])
            ldx(XT[cu][:, :, :], x3[:, sub0:sub0 + NS, :])
            yield
            b.wait_pool(V, "v", LX)
            if j >= 1:
                b.wait(V, "v", "TS", tile_guard[j - 1])
            for sbi in range(NS):
                xs = XT[cu][:, sbi, :]
                sq = XNT[:, sbi, :]
                vred(STT[:, 0:1], xs, ALU.add)
                vstt(sq, xs, 0.0, xs, ALU.add, ALU.mult)
                vred(STT[:, 1:2], sq, ALU.add)
                vts(STT[:, 2:3], STT[:, 0:1], 1.0 / D, None, ALU.mult)
                vts(STT[:, 3:4], STT[:, 1:2], 1.0 / D, EPS_LN, ALU.mult, ALU.add)
                vtt(STT[:, 4:5], STT[:, 2:3], STT[:, 2:3], ALU.mult)
                vstt(STT[:, 3:4], STT[:, 4:5], -1.0, STT[:, 3:4], ALU.mult, ALU.add)
                vts(STT[:, 4:5], STT[:, 3:4], 0.0, 1.0, ALU.mult, ALU.add)
                for _ in range(5):
                    vtt(STT[:, 5:6], STT[:, 4:5], STT[:, 4:5], ALU.mult)
                    vtt(STT[:, 5:6], STT[:, 5:6], STT[:, 3:4], ALU.mult)
                    vts(STT[:, 5:6], STT[:, 5:6], -0.5, 1.5, ALU.mult, ALU.add)
                    vtt(STT[:, 4:5], STT[:, 4:5], STT[:, 5:6], ALU.mult)
                vts(XNT[:, sbi, :], xs, STT[:, 2:3], STT[:, 4:5],
                    ALU.subtract, ALU.mult)
                yield
            b.fence("v", "VS")
            ln_vs = b.cnt["VS"]
            b.wait(T, "t", "VS", ln_vs)
            pbtok = {}
            for sbi in range(NS):
                for k in range(KC):
                    bk = k % 2
                    pbx = PB[4 + bk]
                    if bk in pbtok:
                        b.wait(T, "t", "SS", pbtok[bk][0])
                    tr(pbx[0:128, 0:128], XNT[:, sbi, k * 128:(k + 1) * 128], IDF[:, :])
                    b.fence("t", "TS")
                    trts = b.cnt["TS"]
                    b.wait(S, "s", "TS", trts)
                    scopy(XNFH[cu][:, k, sbi * 128:(sbi + 1) * 128],
                          pbx[0:128, 0:128], sem="SS")
                    pbtok[bk] = (b.cnt["SS"], b.cnt["SS"])
                    if k % 4 == 3:
                        yield
            if j >= 2:
                b.wait_tok(V, "v", st_toks.get(j - 2))
            pbt2 = {}
            for sbi in range(NS):
                for k in range(KC):
                    bk = k % 2
                    pbx = PB[4 + bk]
                    if bk in pbt2:
                        b.wait(T, "t", "VS", pbt2[bk])
                    elif pbtok:
                        b.wait(T, "t", "SS", pbtok[bk][0])
                    tr(pbx[0:128, 0:128], XT[cu][:, sbi, k * 128:(k + 1) * 128],
                       IDF[:, :])
                    b.fence("t", "TS")
                    b.wait(V, "v", "TS", b.cnt["TS"])
                    vcopy(XF[cu][:, k, sbi * 128:(sbi + 1) * 128],
                          pbx[0:128, 0:128], sem="VS")
                    pbt2[bk] = b.cnt["VS"]
                    if k % 4 == 3:
                        yield
            tile_guard[j] = b.cnt["TS"]
            prep_done[j] = (b.cnt["SS"], b.cnt["VS"])
            yield

        prep = [None]

        def ppump(n):
            g = prep[0]
            if g is None:
                return
            try:
                for _ in range(n):
                    next(g)
            except StopIteration:
                prep[0] = None

        for _ in tile_prep(0):
            pass
        for i in range(NTA2):
            cur = i % 2
            p_ = i % 2
            ti = i // 2
            if i + 1 < NTA2:
                prep[0] = tile_prep(i + 1)
            zev = {}
            for o in range(KC):
                pz = PB[0] if o % 2 == 0 else PB[1]
                ph = PB[2] if o % 2 == 0 else PB[3]
                if o >= 2:
                    b.wait(T, "t", "SS", zev[o - 2][0])
                    b.wait(T, "t", "VS", zev[o - 2][1])
                elif i >= 1:
                    b.wait(T, "t", "SS", gate_tok[i - 1][0])
                    b.wait(T, "t", "VS", gate_tok[i - 1][1])
                mm3(pz, WZH, WZL, XNFH[cur], o, TPA)
                mm3(ph, WHH, WHL, XNFH[cur], o, TPA)
                b.wait(S, "s", "TS", b.cnt["TS"] - 1)
                if i >= 1:
                    b.wait(S, "s", "VS", scan_tok[i - 1])
                if o >= 1:
                    b.wait(S, "s", "VS", zev[o - 1][1])
                act(ZS[:, :], pz[:, 0:TPA], ACT.Sigmoid, bias=BIAS[:, 0, o:o + 1], sem="SS")
                act(AA[:, o, :], pz[:, 0:TPA], ACT.Sigmoid, bias=BZN[:, o:o + 1],
                    scale=-1.0, sem="SS")
                b.wait(V, "v", "TS", b.cnt["TS"])
                b.wait(V, "v", "SS", b.cnt["SS"])
                if i >= 1:
                    b.wait(V, "v", "SS", hbs_tok[i - 1][0])
                vstt(BB[:, o, :], ph[:, 0:TPA], BIAS[:, 1, o:o + 1], ZS[:, :],
                     ALU.add, ALU.mult, sem="VS")
                zev[o] = (b.cnt["SS"], b.cnt["VS"])
                ppump(1)
            if i >= 1:
                b.wait(V, "v", "SS", hbs_tok[i - 1][0])
            for k in range(KC):
                sk = p_ * KC + k
                b.pre("v", [AA[:, k, :], BB[:, k, :], SST[:, sk:sk + 1]], [HF[:, k, :]])
                vop(V.tensor_tensor_scan(HF[:, k, :], AA[:, k, :],
                                         BB[:, k, :], SST[:, sk:sk + 1],
                                         ALU.mult, ALU.add))
                vcopy(SST[:, sk:sk + 1], HF[:, k, TPA - 1:TPA])
            b.fence("v", "VS")
            vscan = b.cnt["VS"]
            scan_tok[i] = vscan
            ppump(2)
            b.wait(S, "s", "VS", vscan)
            if i >= 1:
                b.wait(S, "s", "TS", gate_tok[i - 1][2])
                b.wait(V, "v", "TS", gate_tok[i - 1][2])
            for k in range(KC):
                scopy(HBH[:, k, :], HF[:, k, :], sem="SS")
            hbs_tok[i] = (b.cnt["SS"], b.cnt["VS"])
            gev = {}
            for o in range(KC):
                pg = PB[0] if o % 2 == 0 else PB[1]
                pf = PB[2] if o % 2 == 0 else PB[3]
                if o >= 2:
                    b.wait(T, "t", "SS", gev[o - 2][0])
                else:
                    b.wait(T, "t", "SS", hbs_tok[i][0])
                    b.wait(T, "t", "VS", hbs_tok[i][1])
                mm3(pg, WGH, WGL, HBH, o, TPA)
                mm3(pf, WCH, WCL, XNFH[cur], o, TPA)
                b.wait(S, "s", "TS", b.cnt["TS"] - 1)
                if o >= 1:
                    b.wait(S, "s", "VS", gev[o - 1][1])
                act(GSc[:, :], pg[:, 0:TPA], ACT.Sigmoid, bias=BIAS[:, 2, o:o + 1], sem="SS")
                b.wait(S, "s", "TS", b.cnt["TS"])
                act(FS[:, :], pf[:, 0:TPA], ACT.Tanh, bias=BIAS[:, 3, o:o + 1], sem="SS")
                b.wait(V, "v", "SS", b.cnt["SS"] - 1)
                vtt(ATS[:, :], GSc[:, :], HF[:, o, :], ALU.mult)
                vtt(XF[cur][:, o, :], XF[cur][:, o, :], ATS[:, :], ALU.add)
                b.wait(V, "v", "SS", b.cnt["SS"])
                vtt(XF[cur][:, o, :], XF[cur][:, o, :], FS[:, :], ALU.add, sem="VS")
                gev[o] = (b.cnt["SS"], b.cnt["VS"])
                ppump(3)
            gate_tok[i] = (b.cnt["SS"], b.cnt["VS"], b.cnt["TS"])
            ppump(100)
            b.wait(SY, "sy", "VS")
            toff = p_ * SEQ + ti * TPA
            st_toks[i] = stx(x1a[:, :, toff:toff + TPA], XF[cur][:, :, :])
        barrier()
        ar.close()

    def emit_seg_b():
        ar = Arena()
        WQ = r3(ar.w(KC * D), KC, D); WK = r3(ar.w(KC * D), KC, D)
        WV = r3(ar.w(KC * D), KC, D); WO = r3(ar.w(KC * D), KC, D)
        for (dstw, srcw) in ((WQ, wqt), (WK, wkt), (WV, wvt), (WO, wot)):
            ldw(dstw[:, :, :], srcw.rearrange("(c p) o -> p c o", p=128))

        X1 = [r3(ar.f32(KC * TPB), KC, TPB) for _ in range(2)]
        X1B = r3(ar.bf16(KC * TPB), KC, TPB)
        X1BL = r3(ar.bf16(KC * TPB), KC, TPB)
        SQ = r3(ar.bf16(KC * TPB), KC, TPB)
        SQL = r3(ar.bf16(KC * TPB), KC, TPB)
        SQF = ar.f32(TPB)
        VARR = ar.f32(TPB)
        XN1 = r3(ar.bf16(KC * TPB), KC, TPB)
        PQ = r3(ar.bf16(KC * TPB), KC, TPB)
        PK = r3(ar.bf16(KC * TPB), KC, TPB)
        PKT = [r3(ar.bf16(D), KC, 128) for _ in range(2)]
        VEX = [r3(ar.bf16(NH * HD), NH, HD) for _ in range(2)]
        OBF = r3(ar.bf16(KC * TPB), KC, TPB)
        AT0 = ar.bf16(2 * TPB)
        AT1 = ar.bf16(256)
        QM = ar.f32(TPB)
        EX = ar.f32(TPB)
        RBS = ar.f32(TPB)
        DENS = ar.f32(2 * TPB)
        SEXT = r3(ar.f32(2 * NH * HD), 2 * NH, HD)
        SZP = r3(ar.f32(2 * NH), 2 * NH, 1)
        SEB = r3(ar.bf16(2 * NH * HD), 2 * NH, HD)
        SZB = r3(ar.bf16(2 * NH), 2 * NH, 1)
        MRW = ar.f32(TPB)
        RRW = ar.f32(TPB)
        STR = ar.f32(TPB)
        U32 = ar.f32(KC * TPB)
        UH = ar.f32(KC * TPB)
        TH = ar.f32(KC * TPB)
        XN1F = r3(ar.f32(KC * TPB), KC, TPB)
        XQ2 = ar.bf16(KC * TPB)
        XQ = r3(XQ2, KC, TPB)
        CM = ar.f32(TPB)
        ABB = ar.f32(2 * TPB)
        AMT = ar.f32(4)

        zs2 = zeros_in.rearrange("p (h e) -> p h e", h=2 * NH)
        b.dma(G, "g", SEXT[:, :, :], zs2[:, :, :], LW)
        b.dma(G, "g", SZP.rearrange("p h e -> p (h e)"), zeros_in[:, 0:2 * NH], LW)
        b.dma(G, "g", SEB[:, :, :], zs2[:, :, :], LW)
        b.dma(G, "g", SZB.rearrange("p h e -> p (h e)"), zeros_in[:, 0:2 * NH], LW)
        b.wait_pool(T, "t", LW)
        b.wait_pool(T, "t", LS)
        b.wait_pool(T, "t", LX)
        b.wait_pool(S, "s", LW)
        b.wait_pool(V, "v", LW)

        tguard = {}
        xq_tok = {}
        x1b_tok = {}
        xnfree = {}
        sgen = [None]

        def stats_gen(ci_, cur_, t0_):
            if ci_ >= 2:
                b.wait(SY, "sy", "VS", tguard[ci_ - 2])
                b.wait_tok(SY, "sy", x1b_tok.get(ci_ - 2))
            ldx(X1[cur_][:, :, :], x1a[:, :, t0_:t0_ + TPB])
            yield
            b.wait_pool(V, "v", LX)
            for k in range(KC):
                vcopy(X1B[:, k, :], X1[cur_][:, k, :])
                vtt(X1BL[:, k, :], X1[cur_][:, k, :], X1B[:, k, :], ALU.subtract)
                if k % 2 == 1:
                    yield
            b.fence("v", "VS")
            b.wait(T, "t", "VS", b.cnt["VS"])
            for k in range(KC):
                mm(PB[3][0:1, 256:256 + TPB], ONB[:, 0:1], X1B[:, k, :],
                   k == 0, False)
                mm(PB[3][0:1, 256:256 + TPB], ONB[:, 0:1], X1BL[:, k, :],
                   False, k == KC - 1)
            b.fence("t", "TS")
            yield
            for k in range(KC):
                b.wait(S, "s", "VS", b.cnt["VS"])
                act(SQF[:, :], X1[cur_][:, k, :], ACT.Square)
                scopy(SQ[:, k, :], SQF[:, :], sem="SS")
                b.wait(V, "v", "SS", b.cnt["SS"])
                vtt(SQL[:, k, :], SQF[:, :], SQ[:, k, :], ALU.subtract, sem="VS")
                yield
            b.wait(T, "t", "SS", b.cnt["SS"])
            b.wait(T, "t", "VS", b.cnt["VS"])
            for k in range(KC):
                mm(PB[4][0:1, 256:256 + TPB], ONB[:, 0:1], SQ[:, k, :],
                   k == 0, False)
                mm(PB[4][0:1, 256:256 + TPB], ONB[:, 0:1], SQL[:, k, :],
                   False, k == KC - 1)
            b.fence("t", "TS")
            yield
            b.wait(V, "v", "TS", b.cnt["TS"])
            vts(MRW[0:1, :], PB[3][0:1, 256:256 + TPB], 1.0 / D, None, ALU.mult)
            vts(RRW[0:1, :], PB[4][0:1, 256:256 + TPB], 1.0 / D, None, ALU.mult)
            vtt(STR[0:1, :], MRW[0:1, :], MRW[0:1, :], ALU.mult)
            vstt(VARR[0:1, :], STR[0:1, :], -1.0, RRW[0:1, :], ALU.mult, ALU.add)
            vts(VARR[0:1, :], VARR[0:1, :], EPS_LN, None, ALU.add, sem="VS")
            yield
            b.wait(S, "s", "VS", b.cnt["VS"])
            act(RRW[0:1, :], VARR[0:1, :], ACT.Ln)
            act(RRW[0:1, :], RRW[0:1, :], ACT.Exp, scale=-0.5, sem="SS")
            b.wait(V, "v", "SS", b.cnt["SS"])
            for _ in range(2):
                vtt(STR[0:1, :], RRW[0:1, :], RRW[0:1, :], ALU.mult)
                vtt(STR[0:1, :], STR[0:1, :], VARR[0:1, :], ALU.mult)
                vts(STR[0:1, :], STR[0:1, :], -0.5, 1.5, ALU.mult, ALU.add)
                vtt(RRW[0:1, :], RRW[0:1, :], STR[0:1, :], ALU.mult)
            vtt(MRW[0:1, :], MRW[0:1, :], RRW[0:1, :], ALU.mult, sem="VS")
            yield
            b.wait(T, "t", "VS", b.cnt["VS"])
            mm(PB[5][:, 256:256 + TPB], ONF[0:1, 0:128], RRW[0:1, :], True, True)
            b.fence("t", "TS")
            mm(PB[4][:, 256:256 + TPB], ONF[0:1, 0:128], MRW[0:1, :], True, True)
            b.fence("t", "TS")
            bc_ts = b.cnt["TS"]
            yield
            b.wait(V, "v", "TS", bc_ts)
            if ci_ >= 1:
                b.wait(V, "v", "SS", xnfree[ci_ - 1][0])
            for k in range(KC):
                vtt(QM[:, :], X1[cur_][:, k, :], PB[5][:, 256:256 + TPB], ALU.mult)
                vtt(XN1F[:, k, :], QM[:, :], PB[4][:, 256:256 + TPB], ALU.subtract)
                if k % 2 == 1:
                    yield
            b.fence("v", "VS")
            xn_vs = b.cnt["VS"]
            b.wait(S, "s", "VS", xn_vs)
            if ci_ >= 1:
                b.wait(S, "s", "TS", xnfree[ci_ - 1][1])
            for k in range(KC):
                scopy(XN1[:, k, :], XN1F[:, k, :])
                if k % 2 == 1:
                    yield
            b.fence("s", "SS")
            yield

        def spump(n):
            g = sgen[0]
            if g is None:
                return
            try:
                for _ in range(n):
                    next(g)
            except StopIteration:
                sgen[0] = None

        for _ in stats_gen(0, 0, 0):
            pass
        for ci in range(2 * NTB):
            cur = ci % 2
            p_ = ci % 2
            t0 = p_ * SEQ + (ci // 2) * TPB
            spump(100)
            b.wait(T, "t", "SS")
            tile_vs0 = b.cnt["VS"]
            tile_ss0 = b.cnt["SS"]
            qev = {}
            qn = 0
            for (wmat, dst) in ((WQ, PQ), (WK, PK)):
                for o in range(KC):
                    pp = PB[qn % 4]
                    if qn >= 4:
                        b.wait(T, "t", "VS", qev[qn - 4])
                    else:
                        b.wait(T, "t", "VS", tile_vs0)
                        b.wait(T, "t", "SS", tile_ss0)
                        if qn == 0:
                            b.wait(S, "s", "VS", b.cnt["VS"])
                    for k in range(KC):
                        mm(pp[:, 0:TPB], wmat[:, k, o * 128:(o + 1) * 128], XN1[:, k, :],
                           k == 0, k == KC - 1)
                    b.fence("t", "TS")
                    b.wait(S, "s", "TS", b.cnt["TS"])
                    if qn >= 1:
                        b.wait(S, "s", "VS", qev[qn - 1])
                    act(QM[:, :], pp[:, 0:TPB], ACT.Relu, scale=-1.0)
                    act(EX[:, :], QM[:, :], ACT.Exp, scale=-1.0, sem="SS")
                    b.wait(V, "v", "SS", b.cnt["SS"])
                    vstt(dst[:, o, :], pp[:, 0:TPB], 0.0, EX[:, :], ALU.max, ALU.add,
                         sem="VS")
                    qev[qn] = b.cnt["VS"]
                    qn += 1
            vev = {}
            vn = 0
            for sbi in range(TPB // 128):
                for nh2 in range(2):
                    pp = PB[vn % 4]
                    if vn == 0:
                        b.wait(T, "t", "VS", qev[qn - 4 + 0])
                        b.wait(T, "t", "VS", qev[qn - 4 + 1])
                        b.wait(T, "t", "VS", qev[qn - 4 + 2])
                        b.wait(T, "t", "VS", qev[qn - 4 + 3])
                    for k in range(KC):
                        mm(pp[:, 0:512], XN1[:, k, sbi * 128:(sbi + 1) * 128],
                           WV[:, k, nh2 * 512:(nh2 + 1) * 512], k == 0, k == KC - 1)
                    b.fence("t", "TS")
                    b.wait(S, "s", "TS", b.cnt["TS"])
                    scopy(VEX[sbi][:, nh2 * 8:(nh2 + 1) * 8, :],
                          pp[:, 0:512].rearrange("p (h e) -> p h e", h=8), sem="SS")
                    vev[vn] = b.cnt["SS"]
                    vn += 1
            for j in range(TPB // 128):
                for k in range(KC):
                    b.wait(T, "t", "VS")
                    tr(PT[k % 2][0:128, 0:128], PK[:, k, j * 128:(j + 1) * 128], IDB[:, :])
                    b.fence("t", "TS")
                    b.wait(S, "s", "TS")
                    scopy(PKT[j][:, k, :], PT[k % 2][0:128, 0:128], sem="SS")
                    b.wait(T, "t", "SS", b.cnt["SS"] - 1)
            b.wait(S, "s", "VS")
            act(CM[:, :], XN1F[:, 0, :], ACT.Abs)
            vg = {}
            for k in range(1, KC):
                if k >= 3:
                    b.wait(S, "s", "VS", vg[k - 2])
                act(ABB[:, (k % 2) * TPB:(k % 2 + 1) * TPB], XN1F[:, k, :], ACT.Abs, sem="SS")
                b.wait(V, "v", "SS")
                vtt(CM[:, :], CM[:, :], ABB[:, (k % 2) * TPB:(k % 2 + 1) * TPB], ALU.max, sem="VS")
                vg[k] = b.cnt["VS"]
            for sbi in range(TPB // 128):
                b.wait(T, "t", "VS")
                tr(PB[4][0:128, 0:128], CM[:, sbi * 128:(sbi + 1) * 128], IDF[:, :])
                b.fence("t", "TS")
                b.wait(V, "v", "TS")
                vred(AMT[:, sbi:sbi + 1], PB[4][0:128, 0:128], ALU.max)
                vcopy(AMX[:, t0 // 128 + sbi:t0 // 128 + sbi + 1], AMT[:, sbi:sbi + 1])
                b.pre("v", [AMT[:, sbi:sbi + 1]], [AMT[:, 2:3]])
                vop(V.reciprocal(AMT[:, 2:3], AMT[:, sbi:sbi + 1]))
                vts(AMT[:, 3:4], AMT[:, 2:3], 127.0, None, ALU.mult, sem="VS")
                b.wait(T, "t", "VS")
                mm(PB[5][0:1, 0:128], AMT[:, 3:4], IDF[:, 0:128], True, True, sem="TS")
                b.wait(V, "v", "TS")
                vcopy(RBS[0:1, sbi * 128:(sbi + 1) * 128], PB[5][0:1, 0:128], sem="VS")
            b.wait(T, "t", "VS")
            mm(PB[5][:, 0:TPB], ONF[0:1, 0:128], RBS[0:1, 0:TPB], True, True, sem="TS")
            b.wait(V, "v", "TS")
            b.wait_tok(V, "v", xq_tok.get(ci - 1))
            for k in range(KC):
                vtt(U32[:, k * TPB:(k + 1) * TPB], XN1F[:, k, :], PB[5][:, 0:TPB],
                    ALU.mult)
            vts(U32[:, :], U32[:, :], 128.5, None, ALU.add, sem="VS")

            def xq_gen(ci_, t0_):
                for _ in round_sv_steps(XQ2[:, :], U32[:, :], UH[:, :], TH[:, :]):
                    yield
                b.fence("v", "VS")
                b.wait(SY, "sy", "VS")
                xq_tok[ci_] = stx(xqs[:, :, t0_:t0_ + TPB], XQ[:, :, :])
                yield
            xpend = [xq_gen(ci, t0)]

            def xpump(n):
                g = xpend[0]
                if g is None:
                    return
                try:
                    for _ in range(n):
                        next(g)
                except StopIteration:
                    xpend[0] = None
            xnfree[ci] = (b.cnt["SS"], b.cnt["TS"])
            if ci + 1 < 2 * NTB:
                nci = ci + 1
                sgen[0] = stats_gen(nci, nci % 2, (nci % 2) * SEQ + (nci // 2) * TPB)
            b.wait(S, "s", "VS")
            scopy(SEB[:, :, :], SEXT[:, :, :])
            scopy(SZB[:, :, :], SZP[:, :, :], sem="SS")
            b.wait(T, "t", "SS")
            pair_tok = {}
            att_vs0 = b.cnt["VS"]
            att_ss0 = b.cnt["SS"]
            for pr in range(KC):
                hA, hB = 2 * pr, 2 * pr + 1
                sA, sB = p_ * NH + hA, p_ * NH + hB
                if pr >= 1:
                    b.wait(T, "t", "VS", pair_tok[pr - 1][0])
                    b.wait(T, "t", "SS", pair_tok[pr - 1][1])
                else:
                    b.wait(T, "t", "VS", att_vs0)
                    b.wait(T, "t", "SS", att_ss0)
                for bas, apb in ((0, PB[0]), (64, PB[1])):
                    mm(apb[0:128, 0:TPB], PK[bas:bas + 64, pr, 0:128],
                       PQ[bas:bas + 64, pr, 0:TPB], True, True)
                    b.fence("t", "TS")
                    mm(apb[0:128, TPB:TPB + 128], PK[bas:bas + 64, pr, 128:256],
                       PQ[bas:bas + 64, pr, 128:256], True, True)
                    b.fence("t", "TS")
                spb = PB[5]
                for j in range(2):
                    mm(spb[:, 0:128], PKT[j][:, pr, :],
                       VEX[j][:, hA:hB + 1, :].rearrange("p h e -> p (h e)"),
                       j == 0, j == 1)
                b.fence("t", "TS")
                for j in range(2):
                    mm(spb[:, 128:129], PKT[j][:, pr, :], ONB[:, 0:1], j == 0, j == 1)
                b.fence("t", "TS")
                b.wait(V, "v", "TS", b.cnt["TS"])
                for off, apb in ((0, PB[0]), (1, PB[1])):
                    vtt(AT0[0:128, off * TPB:off * TPB + 128], apb[0:128, 0:128],
                        MSK[:, :], ALU.mult)
                    vcopy(AT0[0:128, off * TPB + 128:off * TPB + TPB],
                          apb[0:128, 128:256])
                    vtt(AT1[0:128, off * 128:(off + 1) * 128],
                        apb[0:128, TPB:TPB + 128], MSK[:, :], ALU.mult)
                b.fence("v", "VS")
                vtt(SEXT[0:64, sA, :], SEXT[0:64, sA, :], spb[0:64, 0:64], ALU.add)
                vtt(SEXT[64:128, sB, :], SEXT[64:128, sB, :], spb[64:128, 64:128], ALU.add)
                vtt(SZP[0:64, sA, :], SZP[0:64, sA, :], spb[0:64, 128:129], ALU.add)
                vtt(SZP[64:128, sB, :], SZP[64:128, sB, :], spb[64:128, 128:129],
                    ALU.add, sem="VS")
                xpump(3)
                spump(4)
                mask_vs = b.cnt["VS"]
                b.wait(T, "t", "VS", mask_vs)
                for off, hh in ((0, sA), (1, sB)):
                    co = off * TPB
                    mm(PB[2][0:1, co:co + TPB], ONB[:, 0:1],
                       AT0[0:128, co:co + TPB], True, False)
                    mm(PB[2][0:1, co + 128:co + TPB], ONB[:, 0:1],
                       AT1[0:128, off * 128:(off + 1) * 128], False, False)
                    mm(PB[2][0:1, co:co + TPB], SZB[:, hh, :], PQ[:, pr, :],
                       False, True)
                    b.fence("t", "TS")
                b.wait(S, "s", "TS", b.cnt["TS"])
                act(DENS[0:1, 0:TPB], PB[2][0:1, 0:TPB], ACT.Identity,
                    bias=CVS[0:1, 1:2])
                act(DENS[0:1, TPB:2 * TPB], PB[2][0:1, TPB:2 * TPB], ACT.Identity,
                    bias=CVS[0:1, 1:2], sem="SS")
                b.wait(T, "t", "SS", b.cnt["SS"])
                opb = PB[4]
                for off, hh, sh, bas in ((0, hA, sA, 0), (1, hB, sB, 64)):
                    co = off * TPB
                    mm(PB[3][bas:bas + 64, 0:TPB], ONF[0:1, 0:64],
                       DENS[0:1, co:co + TPB], True, True, tile_position=(0, bas))
                    b.fence("t", "TS")
                    mm(opb[bas:bas + 64, 0:TPB], SEB[:, sh, :], PQ[:, pr, :],
                       True, False, tile_position=(0, bas))
                    mm(opb[bas:bas + 64, 0:TPB], VEX[0][:, hh, :],
                       AT0[0:128, co:co + TPB], False, False, tile_position=(0, bas))
                    mm(opb[bas:bas + 64, 128:TPB], VEX[1][:, hh, :],
                       AT1[0:128, off * 128:(off + 1) * 128], False, True,
                       tile_position=(0, bas))
                    b.fence("t", "TS")
                b.wait(V, "v", "TS", b.cnt["TS"])
                b.pre("v", [PB[3][:, 0:TPB]], [RBS[:, 0:TPB]])
                vop(V.reciprocal(RBS[:, 0:TPB], PB[3][:, 0:TPB]), "VS")
                vtt(OBF[:, pr, :], PB[4][:, 0:TPB], RBS[:, 0:TPB], ALU.mult, sem="VS")
                pair_tok[pr] = (b.cnt["VS"], b.cnt["SS"])
                xpump(1)
                spump(1)
            att_vs = b.cnt["VS"]
            att_ss2 = b.cnt["SS"]
            wev = {}
            for o in range(KC):
                pp = PB[o % 4]
                if o >= 4:
                    b.wait(T, "t", "VS", wev[o - 4])
                else:
                    b.wait(T, "t", "VS", att_vs)
                    b.wait(T, "t", "SS", att_ss2)
                for k in range(KC):
                    mm(pp[:, 0:TPB], WO[:, k, o * 128:(o + 1) * 128], OBF[:, k, :],
                       k == 0, k == KC - 1)
                b.fence("t", "TS")
                b.wait(V, "v", "TS", b.cnt["TS"])
                vstt(X1[cur][:, o, :], pp[:, 0:TPB], BIAS[:, 4, o:o + 1],
                     X1[cur][:, o, :], ALU.add, ALU.add, sem="VS")
                wev[o] = b.cnt["VS"]
                xpump(2)
                spump(3)
            xpump(50)
            spump(100)
            tguard[ci] = b.cnt["VS"]
            b.wait(SY, "sy", "VS")
            x1b_tok[ci] = stx(x1b[:, :, t0:t0 + TPB], X1[cur][:, :, :])
        barrier()
        ar.close()

    def emit_seg_c1():
        ar = Arena()
        WGA = r3(ar.w(KC * INNER), KC, INNER)
        WUP = r3(ar.w(KC * INNER), KC, INNER)
        wg4 = r3(wqg, KC, INNER)
        wu4 = r3(wqu, KC, INNER)
        ldq(1, WGA[:, 0:KC // 2, :], wg4[:, 0:KC // 2, :])
        ldq(2, WGA[:, KC // 2:KC, :], wg4[:, KC // 2:KC, :])
        ldq(1, WUP[:, 0:KC // 2, :], wu4[:, 0:KC // 2, :])
        ldq(2, WUP[:, KC // 2:KC, :], wu4[:, KC // 2:KC, :])

        NSUB = TPC // 128
        XQ = [r3(ar.bf16(KC * TPC), KC, TPC) for _ in range(2)]
        GSI = ar.bf16(1024)
        YB = [ar.f32(INNER), ar.f32(INNER)]
        b.dbg = getattr(b, "dbg", {})
        b.dbg["YB1"] = YB[0].tensor.name
        U32 = ar.f32(INNER // 2)
        UH = ar.f32(INNER // 2)
        TH = ar.bf16(INNER // 2)
        YQ = ar.bf16(INNER // 2)
        USG = ar.f32(12)
        yq3 = yqs.rearrange("(s p) i -> p s i", p=128)
        b.wait_pool(T, "t", LW)
        b.wait_pool(T, "t", LS)

        tguard = {}
        yq_tok = {}
        pend = [None]
        evac = {}
        ogg = [0]

        def quant_gen(qb, srow):
            c0 = 6 * qb
            vred(USG[:, c0 + 3:c0 + 4], YB[qb][:, :], ALU.max, absval=True)
            vcopy(AMY[:, srow:srow + 1], USG[:, c0 + 3:c0 + 4])
            b.pre("v", [USG[:, c0 + 3:c0 + 4]], [USG[:, c0 + 5:c0 + 6]])
            vop(V.reciprocal(USG[:, c0 + 5:c0 + 6], USG[:, c0 + 3:c0 + 4]))
            vts(USG[:, c0 + 4:c0 + 5], USG[:, c0 + 5:c0 + 6], 127.0, None,
                ALU.mult, sem="VS")
            yield
            for hf in range(2):
                sl = slice(hf * (INNER // 2), (hf + 1) * (INNER // 2))
                b.wait_tok(V, "v", yq_tok.get("last"))
                vts(U32[:, :], YB[qb][:, sl], USG[:, c0 + 4:c0 + 5], 128.5,
                    ALU.mult, ALU.add, sem="VS")
                yield
                for _ in round_sv_steps(YQ[:, :], U32[:, :], UH[:, :], TH[:, :]):
                    yield
                b.fence("v", "VS")
                b.wait(SY, "sy", "VS")
                yq_tok["last"] = stx(yq3[:, srow, sl], YQ[:, :])
                yield

        def pump(n):
            g = pend[0]
            if g is None:
                return
            try:
                for _ in range(n):
                    next(g)
            except StopIteration:
                pend[0] = None

        for ti in range(2 * NTC):
            cur = ti % 2
            t0 = (ti % 2) * SEQ + (ti // 2) * TPC
            if ti >= 2:
                b.wait(SY, "sy", "TS", tguard[ti - 2])
            ldx(XQ[cur][:, :, :], xqs[:, :, t0:t0 + TPC])
            b.wait_pool(T, "t", LX)
            for sbi in range(NSUB):
                srow = t0 // 128 + sbi
                qb = (ti * NSUB + sbi) % 2
                c0 = 6 * qb
                vts(USG[:, c0:c0 + 1], AMX[:, srow:srow + 1], 1.0 / 127.0,
                    None, ALU.mult)
                vtt(USG[:, c0 + 1:c0 + 2], USG[:, c0:c0 + 1], GLB[:, 0:1], ALU.mult)
                vtt(USG[:, c0 + 2:c0 + 3], USG[:, c0:c0 + 1], GLB[:, 1:2],
                    ALU.mult, sem="VS")
                scl_tok = b.cnt["VS"]
                for og in range(8):
                    obase = og * 512
                    g_ = ogg[0]
                    pg_ = PB[(g_ % 3) * 2]
                    pu_ = PB[(g_ % 3) * 2 + 1]
                    if g_ >= 3:
                        b.wait(T, "t", "SS", evac[g_ - 3][0])
                        b.wait(T, "t", "VS", evac[g_ - 3][1])
                    for k in range(KC):
                        mm(pg_[:, 0:512], XQ[cur][:, k, sbi * 128:(sbi + 1) * 128],
                           WGA[:, k, obase:obase + 512], k == 0, k == KC - 1)
                    b.fence("t", "TS")
                    for k in range(KC):
                        mm(pu_[:, 0:512], XQ[cur][:, k, sbi * 128:(sbi + 1) * 128],
                           WUP[:, k, obase:obase + 512], k == 0, k == KC - 1)
                    b.fence("t", "TS")
                    b.wait(S, "s", "TS", b.cnt["TS"] - 1)
                    b.wait(S, "s", "VS", scl_tok)
                    if g_ >= 2:
                        b.wait(S, "s", "VS", evac[g_ - 2][1])
                    act(GSI[:, (og % 2) * 512:(og % 2 + 1) * 512], pg_[:, 0:512],
                        ACT.Silu, scale=USG[:, c0 + 1:c0 + 2], sem="SS")
                    b.wait(V, "v", "SS", b.cnt["SS"])
                    b.wait(V, "v", "TS", b.cnt["TS"])
                    vstt(YB[qb][:, obase:obase + 512], pu_[:, 0:512],
                         USG[:, c0 + 2:c0 + 3],
                         GSI[:, (og % 2) * 512:(og % 2 + 1) * 512], ALU.mult, ALU.mult,
                         sem="VS")
                    evac[g_] = (b.cnt["SS"], b.cnt["VS"])
                    ogg[0] += 1
                    pump(5)
                pump(50)
                pend[0] = quant_gen(qb, srow)
                pump(2)
            tguard[ti] = b.cnt["TS"]
        pump(50)
        barrier()
        ar.close()

    def emit_seg_c2():
        ar = Arena()
        WD = r3(ar.w(IKC * D), IKC, D)
        wd4 = r3(wqd, IKC, D)
        ldq(1, WD[:, 0:IKC // 2, :], wd4[:, 0:IKC // 2, :])
        ldq(2, WD[:, IKC // 2:IKC, :], wd4[:, IKC // 2:IKC, :])
        YQF = [r3(ar.bf16(IKC * TPC), IKC, TPC) for _ in range(2)]
        X1T = [r3(ar.f32(KC * TPC), KC, TPC) for _ in range(2)]
        RB = ar.f32(TPC)
        RROW = ar.f32(TPC)
        OF = r3(ar.f32(KC * TPC), KC, TPC)
        b.dbg = getattr(b, "dbg", {})
        b.dbg["OF1"] = OF.tensor.name
        b.dbg["X1T1"] = X1T[1].tensor.name
        b.dbg["RB1"] = RB.tensor.name
        OT = [ar.f32(D) for _ in range(TPC // 128)]
        USD = ar.f32(2)
        out3 = out_hbm.rearrange("(s p) d -> p s d", p=128)
        b.wait_pool(T, "t", LW)
        b.wait_pool(T, "t", LS)

        tguard = {}
        ot_tok = {}
        for ti in range(2 * NTC):
            cur = ti % 2
            t0 = (ti % 2) * SEQ + (ti // 2) * TPC
            if ti >= 2:
                b.wait(SY, "sy", "TS", tguard[ti - 2][0])
                b.wait(SY, "sy", "VS", tguard[ti - 2][1])
            for k in range(IKC):
                b.dma(SY, "sy", YQF[cur][:, k, :],
                      yqs[t0:t0 + TPC, k * 128:(k + 1) * 128], LX, transpose=True)
            ldx(X1T[cur][:, :, :], x1b[:, :, t0:t0 + TPC])
            NSUB = TPC // 128
            for sbi in range(NSUB):
                srow = t0 // 128 + sbi
                b.wait(V, "v", "TS")
                b.wait(V, "v", "SS")
                vts(USD[:, 0:1], AMY[:, srow:srow + 1], 1.0 / 127.0, None, ALU.mult)
                vtt(USD[:, 1:2], USD[:, 0:1], GLB[:, 2:3], ALU.mult, sem="VS")
                b.wait(T, "t", "VS")
                mm(PB[4][0:1, 0:128], USD[:, 1:2], IDF[:, 0:128], True, True, sem="TS")
                b.wait(V, "v", "TS")
                vcopy(RROW[0:1, sbi * 128:(sbi + 1) * 128], PB[4][0:1, 0:128], sem="VS")
            b.wait(T, "t", "VS")
            mm(PB[4][:, 0:TPC], ONF[0:1, 0:128], RROW[0:1, 0:TPC], True, True, sem="TS")
            b.wait(S, "s", "TS")
            scopy(RB[:, :], PB[4][:, 0:TPC], sem="SS")
            b.wait_pool(T, "t", LX)
            b.wait_pool(T, "t", LW)
            b.wait_pool(T, "t", LS)
            c2vs0 = b.cnt["VS"]
            oev = {}
            for o in range(KC):
                pp = PB[0] if o % 2 == 0 else PB[1]
                if o >= 2:
                    b.wait(T, "t", "VS", oev[o - 2])
                else:
                    b.wait(T, "t", "VS", c2vs0)
                for k in range(IKC):
                    mm(pp[:, 0:TPC], WD[:, k, o * 128:(o + 1) * 128], YQF[cur][:, k, :],
                       k == 0, k == IKC - 1)
                b.fence("t", "TS")
                b.wait(V, "v", "TS", b.cnt["TS"])
                b.wait(V, "v", "SS", b.cnt["SS"])
                vtt(OF[:, o, :], pp[:, 0:TPC], RB[:, :], ALU.mult)
                vtt(OF[:, o, :], OF[:, o, :], X1T[cur][:, o, :], ALU.add, sem="VS")
                oev[o] = b.cnt["VS"]
            tg_ts = b.cnt["TS"]
            b.wait(T, "t", "SS")
            for sbi in range(TPC // 128):
                if ti >= 1:
                    b.wait_tok(V, "v", ot_tok.get((ti - 1, sbi)))
                for k in range(KC):
                    pbx = PB[4 + (k % 2)]
                    b.wait(T, "t", "VS")
                    tr(pbx[0:128, 0:128], OF[:, k, sbi * 128:(sbi + 1) * 128], IDF[:, :])
                    b.fence("t", "TS")
                    b.wait(V, "v", "TS")
                    vcopy(OT[sbi][:, k * 128:(k + 1) * 128], pbx[0:128, 0:128], sem="VS")
                b.wait(SY, "sy", "VS")
                ot_tok[(ti, sbi)] = stx(out3[:, t0 // 128 + sbi, :],
                                        OT[sbi][:, :])
            tguard[ti] = (tg_ts, b.cnt["VS"])
        barrier()
        ar.close()

    b.dma(G, "g", DUM[:, :], zeros_in[:, 0:2], LW)
    b.pre("g", (), [DUM[:, :]])
    b.op("g", G.engine_nop(), "GS")
    ldx(BIAS[:, :, :], bias_in)
    ldx(IDF[:, :], ident_f)
    ldx(IDB[:, :], ident_b)
    ldx(MSK[:, :], maskd)
    ldx(ONB[:, :], ones_b)
    ldx(ONF[:, :], ones_f)
    ldx(CVS[:, :], cvals)
    for eng, en in ((T, "t"), (V, "v"), (S, "s"), (G, "g")):
        b.wait_pool(eng, en, LX)

    emit_wq_prep()
    emit_w0_prep()
    emit_seg_a()
    emit_seg_b()
    emit_seg_c1()
    emit_seg_c2()

    b.wait_pool(SY, "sy", STP)
    b.stack.close()
    return b


_cached = {}


def _build():
    if "b" not in _cached:
        _cached["b"] = build_program()
    return _cached["b"]


def kernel(L0_attn_x, L0_attn_Wz, L0_attn_bz, L0_attn_Wh, L0_attn_bh, L0_attn_Wg,
           L0_attn_bg, L0_ffn_weight, L0_ffn_aux_weight1, L0_ffn_aux_weight2,
           L0_ffn_bias, L1_attn_Wq, L1_attn_Wk, L1_attn_Wv, L1_attn_Wo, L1_attn_bo,
           L1_ffn_w_gate, L1_ffn_w_up, L1_ffn_w_down):
    x = np.ascontiguousarray(np.asarray(L0_attn_x, np.float32))

    def t(a):
        return np.ascontiguousarray(np.asarray(a, np.float32).T)

    def c(a):
        return np.ascontiguousarray(np.asarray(a, np.float32))

    wzt_h = np.ascontiguousarray(
        np.transpose(np.asarray(L0_attn_Wz, np.float32), (1, 0, 2)).reshape(D, D))
    wht_h = np.ascontiguousarray(
        np.transpose(np.asarray(L0_attn_Wh, np.float32), (1, 0, 2)).reshape(D, D))
    biases = np.zeros((128, 5, KC), np.float32)
    biases[:, 0, :] = np.asarray(L0_attn_bz, np.float32).reshape(KC, 128).T
    biases[:, 1, :] = np.asarray(L0_attn_bh, np.float32).reshape(KC, 128).T
    biases[:, 2, :] = np.asarray(L0_attn_bg, np.float32).reshape(KC, 128).T
    biases[:, 3, :] = np.asarray(L0_ffn_bias, np.float32).reshape(KC, 128).T
    biases[:, 4, :] = np.asarray(L1_attn_bo, np.float32).reshape(KC, 128).T

    common = dict(
        wzt=wzt_h, wht=wht_h, wg0=c(L0_attn_Wg), w0pt=t(L0_ffn_weight),
        a10=c(L0_ffn_aux_weight1), a20=c(L0_ffn_aux_weight2),
        wqt=t(L1_attn_Wq), wkt=t(L1_attn_Wk), wvt=t(L1_attn_Wv), wot=t(L1_attn_Wo),
        wgr=t(L1_ffn_w_gate), wur=t(L1_ffn_w_up), wdr=t(L1_ffn_w_down),
        biases=biases,
        ident_f=np.eye(128, dtype=np.float32),
        ident_b=np.eye(128, dtype=bfdt),
        maskd=np.ascontiguousarray(np.tril(np.ones((128, 128), np.float32)).T).astype(bfdt),
        ones_b=np.ones((128, 128), dtype=bfdt),
        ones_f=np.ones((128, 128), dtype=np.float32),
        cvals=np.tile(np.array([EPS_LN, EPS_ATT, -128.0, 0, 0, 0, 0, 0], np.float32), (128, 1)),
        zeros=np.zeros((128, 2 * NH * HD), np.float32),
    )

    in_maps = []
    for ci in range(NCORES):
        m = dict(common)
        m["x"] = np.ascontiguousarray(x[2 * ci:2 * ci + 2].reshape(2 * SEQ, D))
        in_maps.append(m)

    bb = _build()
    res = run_bass_kernel_spmd(bb.nc, in_maps, core_ids=list(range(NCORES)))
    out = np.empty((16, SEQ, D), np.float32)
    for ci in range(NCORES):
        out[2 * ci:2 * ci + 2] = np.asarray(res.results[ci]["out"]).reshape(2, SEQ, D)
    return out


# version 28
# speedup vs baseline: 1.0024x; 1.0024x over previous
import numpy as np
import ml_dtypes
from contextlib import ExitStack

import concourse.bass as bass
import concourse.mybir as mybir
from concourse.bass_utils import run_bass_kernel_spmd

F32 = mybir.dt.float32
BF16 = mybir.dt.bfloat16
FP16 = mybir.dt.float16
ALU = mybir.AluOpType
ACT = mybir.ActivationFunctionType
AX = mybir.AxisListType.X

NCORES = 8
SEQ = 2048
D = 1024
KC = D // 128
NH = 16
HD = 64
INNER = 4096
IKC = INNER // 128

TPA = 256
TPB = 256
TPC = 512
NTA = SEQ // TPA
NTB = SEQ // TPB
NTC = SEQ // TPC

EPS_LN = 1e-5
EPS_ATT = 1e-5
MAGIC = float(np.float32(12582912.0))

ROUND_LADDER = True

bfdt = ml_dtypes.bfloat16


class B:
    def __init__(self):
        self.nc = bass.Bass("TRN2", target_bir_lowering=False)
        self.stack = ExitStack()
        self.sems = {}
        self.cnt = {}
        self.waited = {}
        self.last = {}
        self.lastinc = {}
        self.mkdummy = {}
        self.pools = {}
        self.dirty = {}
        self.engines = {}

    def sem(self, name):
        h = self.stack.enter_context(self.nc.semaphore(name))
        self.sems[name] = h
        self.cnt[name] = 0
        return h

    def pool(self, prefix, n):
        names = []
        for i in range(n):
            self.sem(f"{prefix}{i}")
            names.append(f"{prefix}{i}")
        self.pools[prefix] = {"names": names, "idx": 0}
        return prefix

    def dma(self, qeng, qname, dst, src, prefix, transpose=False):
        p = self.pools[prefix]
        sname = p["names"][p["idx"]]
        p["idx"] = (p["idx"] + 1) % len(p["names"])
        self.wait(qeng, qname, sname)
        if transpose:
            inst = qeng.dma_start_transpose(dst, src)
        else:
            inst = qeng.dma_start(dst, src)
        self.inc(inst, sname, 16)
        return (sname, self.cnt[sname])

    def wait_tok(self, eng, engname, tok):
        if tok is not None:
            self.wait(eng, engname, tok[0], tok[1])

    def wait_pool(self, eng, engname, prefix):
        for sname in self.pools[prefix]["names"]:
            self.wait(eng, engname, sname)

    def sbuf(self, name, shape, dtype):
        return self.stack.enter_context(self.nc.sbuf_tensor(name, shape, dtype))

    def psum(self, name, shape, dtype):
        return self.stack.enter_context(self.nc.psum_tensor(name, shape, dtype))

    def inc(self, inst, sem, k=1):
        inst.then_inc(self.sems[sem], k)
        self.cnt[sem] += k
        self.lastinc[inst] = sem

    def wait(self, eng, engname, sem, val=None):
        if val is None:
            val = self.cnt[sem]
        if val <= 0:
            return
        val = min(val, self.cnt[sem])
        key = (engname, sem)
        if self.waited.get(key, -1) >= val:
            return
        eng.wait_ge(self.sems[sem], val)
        self.waited[key] = val

    def op(self, engname, inst, sem=None, k=1):
        self.last[engname] = inst
        if sem is not None:
            self.inc(inst, sem, k)
        return inst

    @staticmethod
    def _span(ap):
        try:
            dims = ap.ap
            lo = ap.offset
            hi = lo + 1
            for (stride, size) in dims[1:]:
                hi += abs(stride) * (size - 1)
            pb = ap.base_partition
            return (ap.tensor.name, pb, pb + dims[0][1], lo, hi)
        except Exception:
            return (ap.tensor.name, 0, 128, 0, 1 << 40)

    @staticmethod
    def _overlap(a, b):
        return (a[0] == b[0] and a[1] < b[2] and b[1] < a[2]
                and a[3] < b[4] and b[3] < a[4])

    def pre(self, engname, reads=(), writes=()):
        st = self.dirty.get(engname)
        if not isinstance(st, tuple):
            self.dirty[engname] = st = ([], [])
        tw, trd = st
        rs = [self._span(ap) for ap in reads]
        ws = [self._span(ap) for ap in writes]
        hazard = (tw or trd) and (
            (not rs and not ws)
            or any(self._overlap(t, s) for t in tw for s in rs + ws)
            or any(self._overlap(t, s) for t in trd for s in ws))
        if hazard:
            self.engines[engname].drain()
            del tw[:]
            del trd[:]
        tw.extend(ws)
        trd.extend(rs)

    def fence(self, engname, sem):
        inst = self.last.get(engname)
        assert inst is not None, f"fence on idle engine {engname}"
        have = self.lastinc.get(inst)
        if have is None:
            self.inc(inst, sem, 1)
            return
        if have == sem:
            return
        inst2 = self.mkdummy[engname]()
        self.last[engname] = inst2
        self.inc(inst2, sem, 1)


def build_program():
    b = B()
    nc = b.nc
    T, V, S, G, SY = nc.tensor, nc.vector, nc.scalar, nc.gpsimd, nc.sync

    def din(name, shape, dtype=F32):
        return nc.dram_tensor(name, shape, dtype, kind="ExternalInput").ap()

    def dint(name, shape, dtype):
        return nc.dram_tensor(name, shape, dtype, kind="Internal").ap()

    x_in = din("x", [2 * SEQ, D])
    out_hbm = nc.dram_tensor("out", [2 * SEQ, D], F32, kind="ExternalOutput").ap()

    wzt = din("wzt", [D, D]); wht = din("wht", [D, D]); wg0 = din("wg0", [D, D])
    w0pt = din("w0pt", [D, D]); a10 = din("a10", [D, D]); a20 = din("a20", [D, D])
    wqt = din("wqt", [D, D]); wkt = din("wkt", [D, D])
    wvt = din("wvt", [D, D]); wot = din("wot", [D, D])
    wgr = din("wgr", [D, INNER]); wur = din("wur", [D, INNER]); wdr = din("wdr", [INNER, D])
    bias_in = din("biases", [128, 5, KC])
    ident_f = din("ident_f", [128, 128], F32)
    ident_b = din("ident_b", [128, 128], BF16)
    maskd = din("maskd", [128, 128], BF16)
    ones_b = din("ones_b", [128, 128], BF16)
    ones_f = din("ones_f", [128, 128], F32)
    cvals = din("cvals", [128, 8], F32)
    zeros_in = din("zeros", [128, 2 * NH * HD], F32)

    x1a = dint("x1a", [128, KC, 2 * SEQ], F32)
    x1b = dint("x1b", [128, KC, 2 * SEQ], F32)
    xqs = dint("xqs", [128, KC, 2 * SEQ], BF16)
    yqs = dint("yqs", [2 * SEQ, INNER], BF16)
    wqg = dint("wqg", [128, KC * INNER], BF16)
    wqu = dint("wqu", [128, KC * INNER], BF16)
    wqd = dint("wqd", [128, IKC * D], BF16)
    wzh = dint("wzh", [128, KC * D], BF16); wzl = dint("wzl", [128, KC * D], BF16)
    whh = dint("whh", [128, KC * D], BF16); whl = dint("whl", [128, KC * D], BF16)
    wgh = dint("wgh", [128, KC * D], BF16); wgl = dint("wgl", [128, KC * D], BF16)
    wch = dint("wch", [128, KC * D], BF16); wcl = dint("wcl", [128, KC * D], BF16)

    for sname in ["TS", "VS", "SS", "GS"]:
        b.sem(sname)
    LW = b.pool("LW", 4)
    LX = b.pool("LX", 6)
    STP = b.pool("ST", 6)
    LS = b.pool("LS", 3)
    LVQ = b.pool("LV", 3)

    GLB = b.sbuf("GLB", [128, 16], F32)
    BIAS = b.sbuf("BIAS", [128, 5, KC], F32)
    BZN = b.sbuf("BZN", [128, KC], F32)
    IDF = b.sbuf("IDF", [128, 128], F32)
    IDB = b.sbuf("IDB", [128, 128], BF16)
    MSK = b.sbuf("MSK", [128, 128], BF16)
    ONB = b.sbuf("ONB", [128, 128], BF16)
    ONF = b.sbuf("ONF", [128, 128], F32)
    CVS = b.sbuf("CVS", [128, 8], F32)
    AMX = b.sbuf("AMX", [128, 2 * SEQ // 128], F32)
    AMY = b.sbuf("AMY", [128, 2 * SEQ // 128], F32)
    SST = b.sbuf("SST", [128, 2 * KC], F32)

    PB = [b.psum(f"pb{i}", [128, 512], F32) for i in range(6)]
    PT = [b.psum(f"pt{i}", [128, 1024], BF16) for i in range(2)]
    DUM = b.sbuf("DUM", [128, 2], F32)
    b.engines = {"t": T, "v": V, "s": S, "g": G}

    def _dummy(en, mk):
        def f():
            b.engines[en].drain()
            d = b.dirty.get(en)
            if isinstance(d, tuple):
                del d[0][:]
                del d[1][:]
            else:
                b.dirty[en] = ([], [])
            return mk()
        return f
    b.mkdummy = {
        "t": _dummy("t", lambda: T.nop()),
        "v": _dummy("v", lambda: V.engine_nop()),
        "g": _dummy("g", lambda: G.engine_nop()),
        "s": _dummy("s", lambda: S.copy(DUM[0:1, 0:1], DUM[0:1, 1:2])),
    }

    def _aps(*xs):
        return [x for x in xs if hasattr(x, "tensor")]

    def act(out, in_, func, bias=0.0, scale=1.0, accum_out=None, sem=None):
        wr = _aps(out, accum_out)
        b.pre("s", _aps(in_, bias, scale), wr)
        i = b.op("s", S.activation(out, in_, func, bias=bias, scale=scale,
                                   accum_out=accum_out), sem)
        return i

    def vop(inst, sem=None):
        return b.op("v", inst, sem)

    def vts(out, in0, s1, s2, op0, op1=None, sem=None):
        b.pre("v", _aps(in0, s1, s2), [out])
        if op1 is None:
            return vop(V.tensor_scalar(out, in0, s1, None, op0), sem)
        return vop(V.tensor_scalar(out, in0, s1, s2, op0, op1), sem)

    def vtt(out, a_, b_, opx, sem=None):
        b.pre("v", _aps(a_, b_), [out])
        return vop(V.tensor_tensor(out, a_, b_, opx), sem)

    def vstt(out, in0, scal, in1, op0, op1, sem=None):
        b.pre("v", _aps(in0, scal, in1), [out])
        return vop(V.scalar_tensor_tensor(out, in0, scal, in1, op0, op1), sem)

    def vcopy(out, in_, sem=None):
        b.pre("v", _aps(in_), [out])
        return vop(V.tensor_copy(out, in_), sem)

    def vred(out, in_, opx, absval=False, sem=None):
        b.pre("v", _aps(in_), [out])
        return vop(V.tensor_reduce(out, in_, AX, opx,
                                   apply_absolute_value=absval), sem)

    def scopy(out, in_, sem=None):
        b.pre("s", _aps(in_), [out])
        i = b.op("s", S.copy(out, in_), sem)
        return i

    def mm(out, lhsT, rhs, start, stop, sem=None, tile_position=None):
        return b.op("t", T.matmul(out, lhsT, rhs, start=start, stop=stop,
                                  tile_position=tile_position), sem)

    def tr(out, in_, ident, sem=None):
        return b.op("t", T.transpose(out, in_, ident), sem)

    def ldx(dst, src):
        return b.dma(SY, "sy", dst, src, LX)

    _ldq = ((SY, "sy", LX), (G, "g", LW), (S, "s", LS))

    def ldq(i, dst, src):
        q = _ldq[i % 3]
        return b.dma(q[0], q[1], dst, src, q[2])

    def ldw(dst, src):
        return b.dma(G, "g", dst, src, LW)

    def stx(dst, src):
        return b.dma(SY, "sy", dst, src, STP)

    def barrier():
        for en, sname in (("t", "TS"), ("v", "VS"), ("s", "SS"), ("g", "GS")):
            if b.last.get(en) is not None:
                b.fence(en, sname)
        vals = {sn: b.cnt[sn] for sn in ("TS", "VS", "SS", "GS")}
        for eng, en in ((T, "t"), (V, "v"), (S, "s"), (G, "g"), (SY, "sy")):
            for sn in ("TS", "VS", "SS", "GS"):
                b.wait(eng, en, sn, vals[sn])
            for pool in (LW, LX, STP):
                b.wait_pool(eng, en, pool)

    class Arena:
        _uid = [0]

        def __init__(self):
            self.stack = ExitStack()

        def _mk(self, n, dtype):
            Arena._uid[0] += 1
            t = self.stack.enter_context(
                nc.sbuf_tensor(f"ar{Arena._uid[0]}", [128, n], dtype))
            return t[:, :]

        def f32(self, n):
            return self._mk(n, F32)

        def bf16(self, n):
            return self._mk(n, BF16)

        def fp16(self, n):
            return self._mk(n, FP16)

        def w(self, n):
            return self._mk(n, BF16)

        def close(self):
            self.stack.close()

    def r3(ap2d, c, n):
        return ap2d.rearrange("p (c n) -> p c n", c=c, n=n)

    def emit_round(dst, u32, uh, th):
        if not ROUND_LADDER:
            vts(dst, u32, MAGIC - 0.5, MAGIC + 128.0, ALU.add, ALU.subtract)
            return
        vcopy(uh, u32)
        for c in (128.0, 64.0, 32.0, 16.0, 8.0, 4.0, 2.0, 1.0):
            vts(th, uh, c, -c, ALU.is_ge, ALU.mult)
            vtt(uh, uh, th, ALU.add)
        vstt(dst, u32, 128.0, uh, ALU.subtract, ALU.subtract)

    def round_sv_steps(dst, u32, uh, th):
        first = True
        for c in (128.0, 64.0, 32.0, 16.0, 8.0, 4.0, 2.0, 1.0):
            src = u32 if first else uh
            b.wait(S, "s", "VS")
            act(th, src, ACT.Sign, bias=CVS[:, 2:3], sem="SS")
            yield
            b.wait(V, "v", "SS")
            vstt(uh, th, -0.5 * c, src, ALU.mult, ALU.add, sem="VS")
            first = False
            yield
        vstt(dst, u32, 0.5, uh, ALU.subtract, ALU.subtract)
        yield

    def emit_round_sv(dst, u32, uh, th):
        for _ in round_sv_steps(dst, u32, uh, th):
            pass

    def emit_wq_prep():
        ar = Arena()
        NWB = 4
        wbuf = [ar.f32(4096) for _ in range(NWB)]
        sums = ar.f32(IKC)
        tot = ar.f32(2)
        sg = [ar.bf16(4096), ar.bf16(4096)]
        wab = [ar.bf16(4096), ar.bf16(4096)]
        wabl = [ar.bf16(4096), ar.bf16(4096)]
        wab32 = [ar.f32(4096), ar.f32(4096)]
        mg = [ar.bf16(4096), ar.bf16(4096)]
        cons = {}
        gl = [0]
        for (src, dst, nch, width, widx) in (
                (wgr, wqg, KC, INNER, 0), (wur, wqu, KC, INNER, 1),
                (wdr, wqd, IKC, D, 2)):
            src3 = src.rearrange("(c p) o -> p c o", p=128)
            dst3 = r3(dst, nch, width)
            mmtok = {}
            for k in range(nch):
                g_ = gl[0]
                d_ = k % 2
                q = _ldq[g_ % 3]
                b.wait_tok(q[0], q[1], cons.get(g_ - NWB))
                ltok = ldq(g_, wbuf[g_ % NWB][:, 0:width], src3[:, k, :])
                b.wait_tok(S, "s", ltok)
                if k >= 2:
                    b.wait(S, "s", "TS", mmtok.get(k - 2, 0))
                b.wait(S, "s", "VS")
                act(wab32[d_][:, 0:width], wbuf[g_ % NWB][:, 0:width], ACT.Abs,
                    sem="SS")
                cons[g_] = ("SS", b.cnt["SS"])
                gl[0] += 1
                b.wait(V, "v", "SS")
                if k >= 2:
                    b.wait(V, "v", "TS", mmtok.get(k - 2, 0))
                vcopy(wab[d_][:, 0:width], wab32[d_][:, 0:width])
                vtt(wabl[d_][:, 0:width], wab32[d_][:, 0:width], wab[d_][:, 0:width],
                    ALU.subtract, sem="VS")
                b.wait(T, "t", "SS")
                b.wait(T, "t", "VS")
                for cs in range(width // 512):
                    mm(PB[0][0:1, 0:512], ONB[:, 0:1],
                       wab[d_][:, cs * 512:(cs + 1) * 512],
                       (k == 0 and cs == 0), False)
                    mm(PB[0][0:1, 0:512], ONB[:, 0:1],
                       wabl[d_][:, cs * 512:(cs + 1) * 512],
                       False, (k == nch - 1 and cs == width // 512 - 1))
                b.fence("t", "TS")
                mmtok[k] = b.cnt["TS"]
            b.wait(V, "v", "TS")
            vred(tot[0:1, 0:1], PB[0][0:1, 0:512], ALU.add)
            vts(tot[0:1, 1:2], tot[0:1, 0:1], 1.0 / (nch * 128 * width), 1e-5,
                ALU.mult, ALU.max, sem="VS")
            b.wait(T, "t", "VS")
            mm(PB[1][:, 0:1], ONF[0:1, 0:128], tot[0:1, 1:2], True, True, sem="TS")
            b.wait(V, "v", "TS")
            vcopy(GLB[:, widx:widx + 1], PB[1][:, 0:1])
            vts(GLB[:, 3 + widx:4 + widx], PB[1][:, 0:1], 0.5, None, ALU.mult, sem="VS")
            vtok = {}
            for k in range(nch):
                g_ = gl[0]
                d_ = k % 2
                wb = wbuf[g_ % NWB][:, 0:width]
                q = _ldq[g_ % 3]
                b.wait_tok(q[0], q[1], cons.get(g_ - NWB))
                ltok = ldq(g_, wb, src3[:, k, :])
                b.wait_tok(S, "s", ltok)
                b.wait(S, "s", "VS", vtok.get(k - 2, 0) if k >= 2 else None)
                act(sg[d_][:, 0:width], wb, ACT.Sign)
                act(wab32[d_][:, 0:width], wb, ACT.Abs, sem="SS")
                cons[g_] = ("SS", b.cnt["SS"])
                gl[0] += 1
                b.wait_pool(V, "v", LX)
                for _sn in b.pools[STP]["names"]:
                    b.wait(V, "v", _sn)
                m_ = mg[k % 2][:, 0:width]
                b.wait(V, "v", "SS")
                vts(m_, wab32[d_][:, 0:width], GLB[:, 3 + widx:4 + widx], None, ALU.is_ge)
                vtt(m_, m_, sg[d_][:, 0:width], ALU.mult, sem="VS")
                vtok[k] = b.cnt["VS"]
                b.wait(SY, "sy", "VS")
                stx(dst3[:, k, :], m_)
        barrier()
        ar.close()

    def emit_w0_prep():
        ar = Arena()
        stag = [ar.f32(D) for _ in range(2)]
        hib = [ar.bf16(D) for _ in range(2)]
        lob = [ar.bf16(D) for _ in range(2)]
        for (src, dsth, dstl) in ((wzt, wzh, wzl), (wht, whh, whl), (wg0, wgh, wgl)):
            src3 = src.rearrange("(c p) o -> p c o", p=128)
            dh3, dl3 = r3(dsth, KC, D), r3(dstl, KC, D)
            for k in range(KC):
                cu = k % 2
                q = _ldq[k % 3]
                b.wait(q[0], q[1], "SS")
                b.wait(q[0], q[1], "VS")
                ltok = ldq(k, stag[cu][:, :], src3[:, k, :])
                b.wait_tok(S, "s", ltok)
                for _sn in b.pools[STP]["names"]:
                    b.wait(S, "s", _sn)
                scopy(hib[cu][:, :], stag[cu][:, :], sem="SS")
                b.wait(V, "v", "SS")
                vtt(lob[cu][:, :], stag[cu][:, :], hib[cu][:, :], ALU.subtract, sem="VS")
                b.wait(SY, "sy", "SS")
                b.wait(SY, "sy", "VS")
                stx(dh3[:, k, :], hib[cu][:, :])
                stx(dl3[:, k, :], lob[cu][:, :])
        barrier()
        ar.close()
        arM = Arena()
        M1H = r3(arM.bf16(KC * D), KC, D)
        M1L = r3(arM.bf16(KC * D), KC, D)
        ar1 = Arena()
        A1H = r3(ar1.bf16(KC * D), KC, D); A1L = r3(ar1.bf16(KC * D), KC, D)
        WTH = r3(ar1.bf16(KC * D), KC, D); WTL = r3(ar1.bf16(KC * D), KC, D)
        st1 = [ar1.f32(D) for _ in range(2)]
        for (src, dh, dl) in ((a10, A1H, A1L), (w0pt, WTH, WTL)):
            src3 = src.rearrange("(c p) o -> p c o", p=128)
            for k in range(KC):
                cu = k % 2
                q = _ldq[k % 3]
                b.wait(q[0], q[1], "SS")
                b.wait(q[0], q[1], "VS")
                ltok = ldq(k, st1[cu][:, :], src3[:, k, :])
                b.wait_tok(S, "s", ltok)
                scopy(dh[:, k, :], st1[cu][:, :], sem="SS")
                b.wait(V, "v", "SS")
                vtt(dl[:, k, :], st1[cu][:, :], dh[:, k, :], ALU.subtract, sem="VS")
        b.wait(T, "t", "SS")
        b.wait(T, "t", "VS")
        for cc in range(KC):
            for rn in range(2):
                pp = PB[0] if rn == 0 else PB[1]
                b.wait(T, "t", "SS")
                b.wait(T, "t", "VS")
                for j in range(KC):
                    la = A1H[:, j, cc * 128:(cc + 1) * 128]
                    ll = A1L[:, j, cc * 128:(cc + 1) * 128]
                    rh = WTH[:, j, rn * 512:(rn + 1) * 512]
                    rl = WTL[:, j, rn * 512:(rn + 1) * 512]
                    mm(pp[:, 0:512], la, rh, j == 0, False)
                    mm(pp[:, 0:512], ll, rh, False, False)
                    mm(pp[:, 0:512], la, rl, False, j == KC - 1)
                b.fence("t", "TS")
                b.wait(S, "s", "TS")
                scopy(M1H[:, cc, rn * 512:(rn + 1) * 512], pp[:, 0:512], sem="SS")
                b.wait(V, "v", "SS")
                b.wait(V, "v", "TS")
                vtt(M1L[:, cc, rn * 512:(rn + 1) * 512], pp[:, 0:512],
                    M1H[:, cc, rn * 512:(rn + 1) * 512], ALU.subtract, sem="VS")
        barrier()
        ar1.close()
        ar2 = Arena()
        A2H = r3(ar2.bf16(KC * D), KC, D); A2L = r3(ar2.bf16(KC * D), KC, D)
        st2 = [ar2.f32(D) for _ in range(2)]
        och = [ar2.bf16(D) for _ in range(2)]
        ocl = [ar2.bf16(D) for _ in range(2)]
        src3 = a20.rearrange("(c p) o -> p c o", p=128)
        for k in range(KC):
            cu = k % 2
            q = _ldq[k % 3]
            b.wait(q[0], q[1], "SS")
            b.wait(q[0], q[1], "VS")
            ltok = ldq(k, st2[cu][:, :], src3[:, k, :])
            b.wait_tok(S, "s", ltok)
            scopy(A2H[:, k, :], st2[cu][:, :], sem="SS")
            b.wait(V, "v", "SS")
            vtt(A2L[:, k, :], st2[cu][:, :], A2H[:, k, :], ALU.subtract, sem="VS")
        b.wait(T, "t", "SS")
        b.wait(T, "t", "VS")
        wch3, wcl3 = r3(wch, KC, D), r3(wcl, KC, D)
        ot_tok = {}
        for rc in range(KC):
            cu = rc % 2
            for on in range(2):
                pp = PB[0] if on == 0 else PB[1]
                b.wait(T, "t", "SS")
                b.wait(T, "t", "VS")
                for cc in range(KC):
                    la = M1H[:, cc, rc * 128:(rc + 1) * 128]
                    ll = M1L[:, cc, rc * 128:(rc + 1) * 128]
                    rh = A2H[:, cc, on * 512:(on + 1) * 512]
                    rl = A2L[:, cc, on * 512:(on + 1) * 512]
                    mm(pp[:, 0:512], la, rh, cc == 0, False)
                    mm(pp[:, 0:512], ll, rh, False, False)
                    mm(pp[:, 0:512], la, rl, False, cc == KC - 1)
                b.fence("t", "TS")
                b.wait(S, "s", "TS")
                b.wait_tok(S, "s", ot_tok.get((rc - 2, 0)))
                b.wait_tok(V, "v", ot_tok.get((rc - 2, 1)))
                scopy(och[cu][:, on * 512:(on + 1) * 512], pp[:, 0:512], sem="SS")
                b.wait(V, "v", "SS")
                b.wait(V, "v", "TS")
                vtt(ocl[cu][:, on * 512:(on + 1) * 512], pp[:, 0:512],
                    och[cu][:, on * 512:(on + 1) * 512], ALU.subtract, sem="VS")
            b.wait(SY, "sy", "SS")
            b.wait(SY, "sy", "VS")
            ot_tok[(rc, 0)] = stx(wch3[:, rc, :], och[cu][:, :])
            ot_tok[(rc, 1)] = stx(wcl3[:, rc, :], ocl[cu][:, :])
        barrier()
        ar2.close()
        arM.close()

    def emit_seg_a():
        ar = Arena()
        WZH = r3(ar.w(KC * D), KC, D); WZL = r3(ar.w(KC * D), KC, D)
        WHH = r3(ar.w(KC * D), KC, D); WHL = r3(ar.w(KC * D), KC, D)
        WGH = r3(ar.w(KC * D), KC, D); WGL = r3(ar.w(KC * D), KC, D)
        WCH = r3(ar.w(KC * D), KC, D); WCL = r3(ar.w(KC * D), KC, D)
        for wi, (dstw, srcw) in enumerate(((WZH, wzh), (WZL, wzl), (WHH, whh),
                                          (WHL, whl), (WGH, wgh), (WGL, wgl),
                                          (WCH, wch), (WCL, wcl))):
            ldq(wi + 1, dstw[:, :, :], r3(srcw, KC, D))

        NS = TPA // 128
        XT = [r3(ar.f32(NS * D), NS, D) for _ in range(2)]
        XNT = r3(ar.f32(NS * D), NS, D)
        XF = [r3(ar.f32(KC * TPA), KC, TPA) for _ in range(2)]
        XNFH = [r3(ar.bf16(KC * TPA), KC, TPA) for _ in range(2)]
        AA = r3(ar.f32(KC * TPA), KC, TPA)
        BB = r3(ar.f32(KC * TPA), KC, TPA)
        HF = BB
        HBH = r3(ar.bf16(KC * TPA), KC, TPA)
        ZS = ar.f32(TPA)
        GSc = ar.f32(TPA)
        FS = ar.f32(TPA)
        ATS = ar.f32(TPA)
        STT = ar.f32(8)

        def mm3(pp, wh_, wl_, xh_, o, n):
            for k in range(KC):
                mm(pp[:, 0:n], wh_[:, k, o * 128:(o + 1) * 128], xh_[:, k, :],
                   k == 0, False)
                mm(pp[:, 0:n], wl_[:, k, o * 128:(o + 1) * 128], xh_[:, k, :],
                   False, k == KC - 1)
            b.fence("t", "TS")

        x3 = x_in.rearrange("(s p) d -> p s d", p=128)
        b.dma(G, "g", SST[:, :], zeros_in[:, 0:2 * KC], LW)
        b.wait_pool(V, "v", LX)
        vts(BZN[:, :], BIAS[:, 0, :], -1.0, None, ALU.mult, sem="VS")
        b.wait_pool(G, "g", LW)
        b.wait_pool(T, "t", LW)
        b.wait_pool(T, "t", LS)
        b.wait_pool(T, "t", LX)
        b.wait(V, "v", "GS")

        tile_guard = {}
        st_toks = {}
        prep_done = {}
        scan_tok = {}
        hbs_tok = {}
        gate_tok = {}

        NTA2 = 2 * NTA

        def tile_prep(j):
            cu = j % 2
            p_ = j % 2
            sub0 = p_ * (SEQ // 128) + (j // 2) * NS
            if j >= 2:
                b.wait(SY, "sy", "TS", tile_guard[j - 2])
            ldx(XT[cu][:, :, :], x3[:, sub0:sub0 + NS, :])
            yield
            b.wait_pool(V, "v", LX)
            if j >= 1:
                b.wait(V, "v", "TS", tile_guard[j - 1])
            for sbi in range(NS):
                xs = XT[cu][:, sbi, :]
                sq = XNT[:, sbi, :]
                vred(STT[:, 0:1], xs, ALU.add)
                vstt(sq, xs, 0.0, xs, ALU.add, ALU.mult)
                vred(STT[:, 1:2], sq, ALU.add)
                vts(STT[:, 2:3], STT[:, 0:1], 1.0 / D, None, ALU.mult)
                vts(STT[:, 3:4], STT[:, 1:2], 1.0 / D, EPS_LN, ALU.mult, ALU.add)
                vtt(STT[:, 4:5], STT[:, 2:3], STT[:, 2:3], ALU.mult)
                vstt(STT[:, 3:4], STT[:, 4:5], -1.0, STT[:, 3:4], ALU.mult, ALU.add)
                vts(STT[:, 4:5], STT[:, 3:4], 0.0, 1.0, ALU.mult, ALU.add)
                for _ in range(5):
                    vtt(STT[:, 5:6], STT[:, 4:5], STT[:, 4:5], ALU.mult)
                    vtt(STT[:, 5:6], STT[:, 5:6], STT[:, 3:4], ALU.mult)
                    vts(STT[:, 5:6], STT[:, 5:6], -0.5, 1.5, ALU.mult, ALU.add)
                    vtt(STT[:, 4:5], STT[:, 4:5], STT[:, 5:6], ALU.mult)
                vts(XNT[:, sbi, :], xs, STT[:, 2:3], STT[:, 4:5],
                    ALU.subtract, ALU.mult)
                yield
            b.fence("v", "VS")
            ln_vs = b.cnt["VS"]
            b.wait(T, "t", "VS", ln_vs)
            pbtok = {}
            for sbi in range(NS):
                for k in range(KC):
                    bk = k % 2
                    pbx = PB[4 + bk]
                    if bk in pbtok:
                        b.wait(T, "t", "SS", pbtok[bk][0])
                    tr(pbx[0:128, 0:128], XNT[:, sbi, k * 128:(k + 1) * 128], IDF[:, :])
                    b.fence("t", "TS")
                    trts = b.cnt["TS"]
                    b.wait(S, "s", "TS", trts)
                    scopy(XNFH[cu][:, k, sbi * 128:(sbi + 1) * 128],
                          pbx[0:128, 0:128], sem="SS")
                    pbtok[bk] = (b.cnt["SS"], b.cnt["SS"])
                    if k % 4 == 3:
                        yield
            if j >= 2:
                b.wait_tok(V, "v", st_toks.get(j - 2))
            pbt2 = {}
            for sbi in range(NS):
                for k in range(KC):
                    bk = k % 2
                    pbx = PB[4 + bk]
                    if bk in pbt2:
                        b.wait(T, "t", "VS", pbt2[bk])
                    elif pbtok:
                        b.wait(T, "t", "SS", pbtok[bk][0])
                    tr(pbx[0:128, 0:128], XT[cu][:, sbi, k * 128:(k + 1) * 128],
                       IDF[:, :])
                    b.fence("t", "TS")
                    b.wait(V, "v", "TS", b.cnt["TS"])
                    vcopy(XF[cu][:, k, sbi * 128:(sbi + 1) * 128],
                          pbx[0:128, 0:128], sem="VS")
                    pbt2[bk] = b.cnt["VS"]
                    if k % 4 == 3:
                        yield
            tile_guard[j] = b.cnt["TS"]
            prep_done[j] = (b.cnt["SS"], b.cnt["VS"])
            yield

        prep = [None]

        def ppump(n):
            g = prep[0]
            if g is None:
                return
            try:
                for _ in range(n):
                    next(g)
            except StopIteration:
                prep[0] = None

        for _ in tile_prep(0):
            pass
        for i in range(NTA2):
            cur = i % 2
            p_ = i % 2
            ti = i // 2
            if i + 1 < NTA2:
                prep[0] = tile_prep(i + 1)
            zev = {}
            for o in range(KC):
                pz = PB[0] if o % 2 == 0 else PB[1]
                ph = PB[2] if o % 2 == 0 else PB[3]
                if o >= 2:
                    b.wait(T, "t", "SS", zev[o - 2][0])
                    b.wait(T, "t", "VS", zev[o - 2][1])
                elif i >= 1:
                    b.wait(T, "t", "SS", gate_tok[i - 1][0])
                    b.wait(T, "t", "VS", gate_tok[i - 1][1])
                mm3(pz, WZH, WZL, XNFH[cur], o, TPA)
                mm3(ph, WHH, WHL, XNFH[cur], o, TPA)
                b.wait(S, "s", "TS", b.cnt["TS"] - 1)
                if i >= 1:
                    b.wait(S, "s", "VS", scan_tok[i - 1])
                if o >= 1:
                    b.wait(S, "s", "VS", zev[o - 1][1])
                act(ZS[:, :], pz[:, 0:TPA], ACT.Sigmoid, bias=BIAS[:, 0, o:o + 1], sem="SS")
                act(AA[:, o, :], pz[:, 0:TPA], ACT.Sigmoid, bias=BZN[:, o:o + 1],
                    scale=-1.0, sem="SS")
                b.wait(V, "v", "TS", b.cnt["TS"])
                b.wait(V, "v", "SS", b.cnt["SS"])
                if i >= 1:
                    b.wait(V, "v", "SS", hbs_tok[i - 1][0])
                vstt(BB[:, o, :], ph[:, 0:TPA], BIAS[:, 1, o:o + 1], ZS[:, :],
                     ALU.add, ALU.mult, sem="VS")
                zev[o] = (b.cnt["SS"], b.cnt["VS"])
                ppump(1)
            if i >= 1:
                b.wait(V, "v", "SS", hbs_tok[i - 1][0])
            for k in range(KC):
                sk = p_ * KC + k
                b.pre("v", [AA[:, k, :], BB[:, k, :], SST[:, sk:sk + 1]], [HF[:, k, :]])
                vop(V.tensor_tensor_scan(HF[:, k, :], AA[:, k, :],
                                         BB[:, k, :], SST[:, sk:sk + 1],
                                         ALU.mult, ALU.add))
                vcopy(SST[:, sk:sk + 1], HF[:, k, TPA - 1:TPA])
            b.fence("v", "VS")
            vscan = b.cnt["VS"]
            scan_tok[i] = vscan
            ppump(2)
            b.wait(S, "s", "VS", vscan)
            if i >= 1:
                b.wait(S, "s", "TS", gate_tok[i - 1][2])
                b.wait(V, "v", "TS", gate_tok[i - 1][2])
            for k in range(KC):
                scopy(HBH[:, k, :], HF[:, k, :], sem="SS")
            hbs_tok[i] = (b.cnt["SS"], b.cnt["VS"])
            gev = {}
            for o in range(KC):
                pg = PB[0] if o % 2 == 0 else PB[1]
                pf = PB[2] if o % 2 == 0 else PB[3]
                if o >= 2:
                    b.wait(T, "t", "SS", gev[o - 2][0])
                else:
                    b.wait(T, "t", "SS", hbs_tok[i][0])
                    b.wait(T, "t", "VS", hbs_tok[i][1])
                mm3(pg, WGH, WGL, HBH, o, TPA)
                mm3(pf, WCH, WCL, XNFH[cur], o, TPA)
                b.wait(S, "s", "TS", b.cnt["TS"] - 1)
                if o >= 1:
                    b.wait(S, "s", "VS", gev[o - 1][1])
                act(GSc[:, :], pg[:, 0:TPA], ACT.Sigmoid, bias=BIAS[:, 2, o:o + 1], sem="SS")
                b.wait(S, "s", "TS", b.cnt["TS"])
                act(FS[:, :], pf[:, 0:TPA], ACT.Tanh, bias=BIAS[:, 3, o:o + 1], sem="SS")
                b.wait(V, "v", "SS", b.cnt["SS"] - 1)
                vtt(ATS[:, :], GSc[:, :], HF[:, o, :], ALU.mult)
                vtt(XF[cur][:, o, :], XF[cur][:, o, :], ATS[:, :], ALU.add)
                b.wait(V, "v", "SS", b.cnt["SS"])
                vtt(XF[cur][:, o, :], XF[cur][:, o, :], FS[:, :], ALU.add, sem="VS")
                gev[o] = (b.cnt["SS"], b.cnt["VS"])
                ppump(3)
            gate_tok[i] = (b.cnt["SS"], b.cnt["VS"], b.cnt["TS"])
            ppump(100)
            b.wait(SY, "sy", "VS")
            toff = p_ * SEQ + ti * TPA
            st_toks[i] = stx(x1a[:, :, toff:toff + TPA], XF[cur][:, :, :])
        barrier()
        ar.close()

    def emit_seg_b():
        ar = Arena()
        WQ = r3(ar.w(KC * D), KC, D); WK = r3(ar.w(KC * D), KC, D)
        WV = r3(ar.w(KC * D), KC, D); WO = r3(ar.w(KC * D), KC, D)
        for (dstw, srcw) in ((WQ, wqt), (WK, wkt), (WV, wvt), (WO, wot)):
            ldw(dstw[:, :, :], srcw.rearrange("(c p) o -> p c o", p=128))

        X1 = [r3(ar.f32(KC * TPB), KC, TPB) for _ in range(2)]
        X1B = r3(ar.bf16(KC * TPB), KC, TPB)
        X1BL = r3(ar.bf16(KC * TPB), KC, TPB)
        SQ = r3(ar.bf16(KC * TPB), KC, TPB)
        SQL = r3(ar.bf16(KC * TPB), KC, TPB)
        SQF = ar.f32(TPB)
        VARR = ar.f32(TPB)
        XN1 = r3(ar.bf16(KC * TPB), KC, TPB)
        PQ = r3(ar.bf16(KC * TPB), KC, TPB)
        PK = r3(ar.bf16(KC * TPB), KC, TPB)
        PKT = [r3(ar.bf16(D), KC, 128) for _ in range(2)]
        VEX = [r3(ar.bf16(NH * HD), NH, HD) for _ in range(2)]
        OBF = r3(ar.bf16(KC * TPB), KC, TPB)
        AT0 = ar.bf16(2 * TPB)
        AT1 = ar.bf16(256)
        QM = ar.f32(TPB)
        EX = ar.f32(TPB)
        RBS = ar.f32(TPB)
        DENS = ar.f32(2 * TPB)
        SEXT = r3(ar.f32(2 * NH * HD), 2 * NH, HD)
        SZP = r3(ar.f32(2 * NH), 2 * NH, 1)
        SEB = r3(ar.bf16(2 * NH * HD), 2 * NH, HD)
        SZB = r3(ar.bf16(2 * NH), 2 * NH, 1)
        MRW = ar.f32(TPB)
        RRW = ar.f32(TPB)
        STR = ar.f32(TPB)
        U32 = ar.f32(KC * TPB)
        UH = ar.f32(KC * TPB)
        TH = ar.f32(KC * TPB)
        XN1F = r3(ar.f32(KC * TPB), KC, TPB)
        XQ2 = ar.bf16(KC * TPB)
        XQ = r3(XQ2, KC, TPB)
        CM = ar.f32(TPB)
        ABB = ar.f32(2 * TPB)
        AMT = ar.f32(4)

        zs2 = zeros_in.rearrange("p (h e) -> p h e", h=2 * NH)
        b.dma(G, "g", SEXT[:, :, :], zs2[:, :, :], LW)
        b.dma(G, "g", SZP.rearrange("p h e -> p (h e)"), zeros_in[:, 0:2 * NH], LW)
        b.dma(G, "g", SEB[:, :, :], zs2[:, :, :], LW)
        b.dma(G, "g", SZB.rearrange("p h e -> p (h e)"), zeros_in[:, 0:2 * NH], LW)
        b.wait_pool(T, "t", LW)
        b.wait_pool(T, "t", LS)
        b.wait_pool(T, "t", LX)
        b.wait_pool(S, "s", LW)
        b.wait_pool(V, "v", LW)

        tguard = {}
        xq_tok = {}
        x1b_tok = {}
        xnfree = {}
        sgen = [None]

        def stats_gen(ci_, cur_, t0_):
            if ci_ >= 2:
                b.wait(SY, "sy", "VS", tguard[ci_ - 2])
                b.wait_tok(SY, "sy", x1b_tok.get(ci_ - 2))
            ldx(X1[cur_][:, :, :], x1a[:, :, t0_:t0_ + TPB])
            yield
            b.wait_pool(V, "v", LX)
            for k in range(KC):
                vcopy(X1B[:, k, :], X1[cur_][:, k, :])
                vtt(X1BL[:, k, :], X1[cur_][:, k, :], X1B[:, k, :], ALU.subtract)
                if k % 2 == 1:
                    yield
            b.fence("v", "VS")
            b.wait(T, "t", "VS", b.cnt["VS"])
            for k in range(KC):
                mm(PB[3][0:1, 256:256 + TPB], ONB[:, 0:1], X1B[:, k, :],
                   k == 0, False)
                mm(PB[3][0:1, 256:256 + TPB], ONB[:, 0:1], X1BL[:, k, :],
                   False, k == KC - 1)
            b.fence("t", "TS")
            yield
            for k in range(KC):
                b.wait(S, "s", "VS", b.cnt["VS"])
                act(SQF[:, :], X1[cur_][:, k, :], ACT.Square)
                scopy(SQ[:, k, :], SQF[:, :], sem="SS")
                b.wait(V, "v", "SS", b.cnt["SS"])
                vtt(SQL[:, k, :], SQF[:, :], SQ[:, k, :], ALU.subtract, sem="VS")
                yield
            b.wait(T, "t", "SS", b.cnt["SS"])
            b.wait(T, "t", "VS", b.cnt["VS"])
            for k in range(KC):
                mm(PB[4][0:1, 256:256 + TPB], ONB[:, 0:1], SQ[:, k, :],
                   k == 0, False)
                mm(PB[4][0:1, 256:256 + TPB], ONB[:, 0:1], SQL[:, k, :],
                   False, k == KC - 1)
            b.fence("t", "TS")
            yield
            b.wait(V, "v", "TS", b.cnt["TS"])
            vts(MRW[0:1, :], PB[3][0:1, 256:256 + TPB], 1.0 / D, None, ALU.mult)
            vts(RRW[0:1, :], PB[4][0:1, 256:256 + TPB], 1.0 / D, None, ALU.mult)
            vtt(STR[0:1, :], MRW[0:1, :], MRW[0:1, :], ALU.mult)
            vstt(VARR[0:1, :], STR[0:1, :], -1.0, RRW[0:1, :], ALU.mult, ALU.add)
            vts(VARR[0:1, :], VARR[0:1, :], EPS_LN, None, ALU.add, sem="VS")
            yield
            b.wait(S, "s", "VS", b.cnt["VS"])
            act(RRW[0:1, :], VARR[0:1, :], ACT.Ln)
            act(RRW[0:1, :], RRW[0:1, :], ACT.Exp, scale=-0.5, sem="SS")
            b.wait(V, "v", "SS", b.cnt["SS"])
            for _ in range(2):
                vtt(STR[0:1, :], RRW[0:1, :], RRW[0:1, :], ALU.mult)
                vtt(STR[0:1, :], STR[0:1, :], VARR[0:1, :], ALU.mult)
                vts(STR[0:1, :], STR[0:1, :], -0.5, 1.5, ALU.mult, ALU.add)
                vtt(RRW[0:1, :], RRW[0:1, :], STR[0:1, :], ALU.mult)
            vtt(MRW[0:1, :], MRW[0:1, :], RRW[0:1, :], ALU.mult, sem="VS")
            yield
            b.wait(T, "t", "VS", b.cnt["VS"])
            mm(PB[5][:, 256:256 + TPB], ONF[0:1, 0:128], RRW[0:1, :], True, True)
            b.fence("t", "TS")
            mm(PB[4][:, 256:256 + TPB], ONF[0:1, 0:128], MRW[0:1, :], True, True)
            b.fence("t", "TS")
            bc_ts = b.cnt["TS"]
            yield
            b.wait(V, "v", "TS", bc_ts)
            if ci_ >= 1:
                b.wait(V, "v", "SS", xnfree[ci_ - 1][0])
            for k in range(KC):
                vtt(QM[:, :], X1[cur_][:, k, :], PB[5][:, 256:256 + TPB], ALU.mult)
                vtt(XN1F[:, k, :], QM[:, :], PB[4][:, 256:256 + TPB], ALU.subtract)
                if k % 2 == 1:
                    yield
            b.fence("v", "VS")
            xn_vs = b.cnt["VS"]
            b.wait(S, "s", "VS", xn_vs)
            if ci_ >= 1:
                b.wait(S, "s", "TS", xnfree[ci_ - 1][1])
            for k in range(KC):
                scopy(XN1[:, k, :], XN1F[:, k, :])
                if k % 2 == 1:
                    yield
            b.fence("s", "SS")
            yield

        def spump(n):
            g = sgen[0]
            if g is None:
                return
            try:
                for _ in range(n):
                    next(g)
            except StopIteration:
                sgen[0] = None

        for _ in stats_gen(0, 0, 0):
            pass
        for ci in range(2 * NTB):
            cur = ci % 2
            p_ = ci % 2
            t0 = p_ * SEQ + (ci // 2) * TPB
            spump(100)
            b.wait(T, "t", "SS")
            tile_vs0 = b.cnt["VS"]
            tile_ss0 = b.cnt["SS"]
            qev = {}
            qn = 0
            for (wmat, dst) in ((WQ, PQ), (WK, PK)):
                for o in range(KC):
                    pp = PB[qn % 4]
                    if qn >= 4:
                        b.wait(T, "t", "VS", qev[qn - 4])
                    else:
                        b.wait(T, "t", "VS", tile_vs0)
                        b.wait(T, "t", "SS", tile_ss0)
                        if qn == 0:
                            b.wait(S, "s", "VS", b.cnt["VS"])
                    for k in range(KC):
                        mm(pp[:, 0:TPB], wmat[:, k, o * 128:(o + 1) * 128], XN1[:, k, :],
                           k == 0, k == KC - 1)
                    b.fence("t", "TS")
                    b.wait(S, "s", "TS", b.cnt["TS"])
                    if qn >= 1:
                        b.wait(S, "s", "VS", qev[qn - 1])
                    act(QM[:, :], pp[:, 0:TPB], ACT.Relu, scale=-1.0)
                    act(EX[:, :], QM[:, :], ACT.Exp, scale=-1.0, sem="SS")
                    b.wait(V, "v", "SS", b.cnt["SS"])
                    vstt(dst[:, o, :], pp[:, 0:TPB], 0.0, EX[:, :], ALU.max, ALU.add,
                         sem="VS")
                    qev[qn] = b.cnt["VS"]
                    qn += 1
            vev = {}
            vn = 0
            for sbi in range(TPB // 128):
                for nh2 in range(2):
                    pp = PB[vn % 4]
                    if vn == 0:
                        b.wait(T, "t", "VS", qev[qn - 4 + 0])
                        b.wait(T, "t", "VS", qev[qn - 4 + 1])
                        b.wait(T, "t", "VS", qev[qn - 4 + 2])
                        b.wait(T, "t", "VS", qev[qn - 4 + 3])
                    for k in range(KC):
                        mm(pp[:, 0:512], XN1[:, k, sbi * 128:(sbi + 1) * 128],
                           WV[:, k, nh2 * 512:(nh2 + 1) * 512], k == 0, k == KC - 1)
                    b.fence("t", "TS")
                    b.wait(S, "s", "TS", b.cnt["TS"])
                    scopy(VEX[sbi][:, nh2 * 8:(nh2 + 1) * 8, :],
                          pp[:, 0:512].rearrange("p (h e) -> p h e", h=8), sem="SS")
                    vev[vn] = b.cnt["SS"]
                    vn += 1
            for j in range(TPB // 128):
                for k in range(KC):
                    b.wait(T, "t", "VS")
                    tr(PT[k % 2][0:128, 0:128], PK[:, k, j * 128:(j + 1) * 128], IDB[:, :])
                    b.fence("t", "TS")
                    b.wait(S, "s", "TS")
                    scopy(PKT[j][:, k, :], PT[k % 2][0:128, 0:128], sem="SS")
                    b.wait(T, "t", "SS", b.cnt["SS"] - 1)
            b.wait(S, "s", "VS")
            act(CM[:, :], XN1F[:, 0, :], ACT.Abs)
            vg = {}
            for k in range(1, KC):
                if k >= 3:
                    b.wait(S, "s", "VS", vg[k - 2])
                act(ABB[:, (k % 2) * TPB:(k % 2 + 1) * TPB], XN1F[:, k, :], ACT.Abs, sem="SS")
                b.wait(V, "v", "SS")
                vtt(CM[:, :], CM[:, :], ABB[:, (k % 2) * TPB:(k % 2 + 1) * TPB], ALU.max, sem="VS")
                vg[k] = b.cnt["VS"]
            for sbi in range(TPB // 128):
                b.wait(T, "t", "VS")
                tr(PB[4][0:128, 0:128], CM[:, sbi * 128:(sbi + 1) * 128], IDF[:, :])
                b.fence("t", "TS")
                b.wait(V, "v", "TS")
                vred(AMT[:, sbi:sbi + 1], PB[4][0:128, 0:128], ALU.max)
                vcopy(AMX[:, t0 // 128 + sbi:t0 // 128 + sbi + 1], AMT[:, sbi:sbi + 1])
                b.pre("v", [AMT[:, sbi:sbi + 1]], [AMT[:, 2:3]])
                vop(V.reciprocal(AMT[:, 2:3], AMT[:, sbi:sbi + 1]))
                vts(AMT[:, 3:4], AMT[:, 2:3], 127.0, None, ALU.mult, sem="VS")
                b.wait(T, "t", "VS")
                mm(PB[5][0:1, 0:128], AMT[:, 3:4], IDF[:, 0:128], True, True, sem="TS")
                b.wait(V, "v", "TS")
                vcopy(RBS[0:1, sbi * 128:(sbi + 1) * 128], PB[5][0:1, 0:128], sem="VS")
            b.wait(T, "t", "VS")
            mm(PB[5][:, 0:TPB], ONF[0:1, 0:128], RBS[0:1, 0:TPB], True, True, sem="TS")
            b.wait(V, "v", "TS")
            b.wait_tok(V, "v", xq_tok.get(ci - 1))
            for k in range(KC):
                vtt(U32[:, k * TPB:(k + 1) * TPB], XN1F[:, k, :], PB[5][:, 0:TPB],
                    ALU.mult)
            vts(U32[:, :], U32[:, :], 128.5, None, ALU.add, sem="VS")

            def xq_gen(ci_, t0_):
                for _ in round_sv_steps(XQ2[:, :], U32[:, :], UH[:, :], TH[:, :]):
                    yield
                b.fence("v", "VS")
                b.wait(SY, "sy", "VS")
                xq_tok[ci_] = stx(xqs[:, :, t0_:t0_ + TPB], XQ[:, :, :])
                yield
            xpend = [xq_gen(ci, t0)]

            def xpump(n):
                g = xpend[0]
                if g is None:
                    return
                try:
                    for _ in range(n):
                        next(g)
                except StopIteration:
                    xpend[0] = None
            xnfree[ci] = (b.cnt["SS"], b.cnt["TS"])
            if ci + 1 < 2 * NTB:
                nci = ci + 1
                sgen[0] = stats_gen(nci, nci % 2, (nci % 2) * SEQ + (nci // 2) * TPB)
            b.wait(S, "s", "VS")
            scopy(SEB[:, :, :], SEXT[:, :, :])
            scopy(SZB[:, :, :], SZP[:, :, :], sem="SS")
            b.wait(T, "t", "SS")
            pair_tok = {}
            att_vs0 = b.cnt["VS"]
            att_ss0 = b.cnt["SS"]
            for pr in range(KC):
                hA, hB = 2 * pr, 2 * pr + 1
                sA, sB = p_ * NH + hA, p_ * NH + hB
                if pr >= 1:
                    b.wait(T, "t", "VS", pair_tok[pr - 1][0])
                    b.wait(T, "t", "SS", pair_tok[pr - 1][1])
                else:
                    b.wait(T, "t", "VS", att_vs0)
                    b.wait(T, "t", "SS", att_ss0)
                for bas, apb in ((0, PB[0]), (64, PB[1])):
                    mm(apb[0:128, 0:TPB], PK[bas:bas + 64, pr, 0:128],
                       PQ[bas:bas + 64, pr, 0:TPB], True, True)
                    b.fence("t", "TS")
                    mm(apb[0:128, TPB:TPB + 128], PK[bas:bas + 64, pr, 128:256],
                       PQ[bas:bas + 64, pr, 128:256], True, True)
                    b.fence("t", "TS")
                spb = PB[5]
                for j in range(2):
                    mm(spb[:, 0:128], PKT[j][:, pr, :],
                       VEX[j][:, hA:hB + 1, :].rearrange("p h e -> p (h e)"),
                       j == 0, j == 1)
                b.fence("t", "TS")
                for j in range(2):
                    mm(spb[:, 128:129], PKT[j][:, pr, :], ONB[:, 0:1], j == 0, j == 1)
                b.fence("t", "TS")
                b.wait(V, "v", "TS", b.cnt["TS"])
                for off, apb in ((0, PB[0]), (1, PB[1])):
                    vtt(AT0[0:128, off * TPB:off * TPB + 128], apb[0:128, 0:128],
                        MSK[:, :], ALU.mult)
                    vcopy(AT0[0:128, off * TPB + 128:off * TPB + TPB],
                          apb[0:128, 128:256])
                    vtt(AT1[0:128, off * 128:(off + 1) * 128],
                        apb[0:128, TPB:TPB + 128], MSK[:, :], ALU.mult)
                b.fence("v", "VS")
                vtt(SEXT[0:64, sA, :], SEXT[0:64, sA, :], spb[0:64, 0:64], ALU.add)
                vtt(SEXT[64:128, sB, :], SEXT[64:128, sB, :], spb[64:128, 64:128], ALU.add)
                vtt(SZP[0:64, sA, :], SZP[0:64, sA, :], spb[0:64, 128:129], ALU.add)
                vtt(SZP[64:128, sB, :], SZP[64:128, sB, :], spb[64:128, 128:129],
                    ALU.add, sem="VS")
                xpump(3)
                spump(4)
                mask_vs = b.cnt["VS"]
                for off, hh in ((0, sA), (1, sB)):
                    co = off * TPB
                    mm(PB[2][0:1, co:co + TPB], SZB[:, hh, :], PQ[:, pr, :],
                       True, False)
                    if off == 0:
                        b.wait(T, "t", "VS", mask_vs)
                    mm(PB[2][0:1, co:co + TPB], ONB[:, 0:1],
                       AT0[0:128, co:co + TPB], False, False)
                    mm(PB[2][0:1, co + 128:co + TPB], ONB[:, 0:1],
                       AT1[0:128, off * 128:(off + 1) * 128], False, True)
                    b.fence("t", "TS")
                b.wait(S, "s", "TS", b.cnt["TS"])
                act(DENS[0:1, 0:TPB], PB[2][0:1, 0:TPB], ACT.Identity,
                    bias=CVS[0:1, 1:2])
                act(DENS[0:1, TPB:2 * TPB], PB[2][0:1, TPB:2 * TPB], ACT.Identity,
                    bias=CVS[0:1, 1:2], sem="SS")
                b.wait(T, "t", "SS", b.cnt["SS"])
                opb = PB[4]
                for off, hh, sh, bas in ((0, hA, sA, 0), (1, hB, sB, 64)):
                    co = off * TPB
                    mm(PB[3][bas:bas + 64, 0:TPB], ONF[0:1, 0:64],
                       DENS[0:1, co:co + TPB], True, True, tile_position=(0, bas))
                    b.fence("t", "TS")
                    mm(opb[bas:bas + 64, 0:TPB], SEB[:, sh, :], PQ[:, pr, :],
                       True, False, tile_position=(0, bas))
                    mm(opb[bas:bas + 64, 0:TPB], VEX[0][:, hh, :],
                       AT0[0:128, co:co + TPB], False, False, tile_position=(0, bas))
                    mm(opb[bas:bas + 64, 128:TPB], VEX[1][:, hh, :],
                       AT1[0:128, off * 128:(off + 1) * 128], False, True,
                       tile_position=(0, bas))
                    b.fence("t", "TS")
                b.wait(V, "v", "TS", b.cnt["TS"])
                b.pre("v", [PB[3][:, 0:TPB]], [RBS[:, 0:TPB]])
                vop(V.reciprocal(RBS[:, 0:TPB], PB[3][:, 0:TPB]), "VS")
                vtt(OBF[:, pr, :], PB[4][:, 0:TPB], RBS[:, 0:TPB], ALU.mult, sem="VS")
                pair_tok[pr] = (b.cnt["VS"], b.cnt["SS"])
                xpump(1)
                spump(1)
            att_vs = b.cnt["VS"]
            att_ss2 = b.cnt["SS"]
            wev = {}
            for o in range(KC):
                pp = PB[o % 4]
                if o >= 4:
                    b.wait(T, "t", "VS", wev[o - 4])
                else:
                    b.wait(T, "t", "VS", att_vs)
                    b.wait(T, "t", "SS", att_ss2)
                for k in range(KC):
                    mm(pp[:, 0:TPB], WO[:, k, o * 128:(o + 1) * 128], OBF[:, k, :],
                       k == 0, k == KC - 1)
                b.fence("t", "TS")
                b.wait(V, "v", "TS", b.cnt["TS"])
                vstt(X1[cur][:, o, :], pp[:, 0:TPB], BIAS[:, 4, o:o + 1],
                     X1[cur][:, o, :], ALU.add, ALU.add, sem="VS")
                wev[o] = b.cnt["VS"]
                xpump(2)
                spump(3)
            xpump(50)
            spump(100)
            tguard[ci] = b.cnt["VS"]
            b.wait(SY, "sy", "VS")
            x1b_tok[ci] = stx(x1b[:, :, t0:t0 + TPB], X1[cur][:, :, :])
        barrier()
        ar.close()

    def emit_seg_c1():
        ar = Arena()
        WGA = r3(ar.w(KC * INNER), KC, INNER)
        WUP = r3(ar.w(KC * INNER), KC, INNER)
        wg4 = r3(wqg, KC, INNER)
        wu4 = r3(wqu, KC, INNER)
        ldq(1, WGA[:, 0:KC // 2, :], wg4[:, 0:KC // 2, :])
        ldq(2, WGA[:, KC // 2:KC, :], wg4[:, KC // 2:KC, :])
        ldq(1, WUP[:, 0:KC // 2, :], wu4[:, 0:KC // 2, :])
        ldq(2, WUP[:, KC // 2:KC, :], wu4[:, KC // 2:KC, :])

        NSUB = TPC // 128
        XQ = [r3(ar.bf16(KC * TPC), KC, TPC) for _ in range(2)]
        GSI = ar.bf16(1024)
        YB = [ar.f32(INNER), ar.f32(INNER)]
        b.dbg = getattr(b, "dbg", {})
        b.dbg["YB1"] = YB[0].tensor.name
        U32 = ar.f32(INNER // 2)
        UH = ar.f32(INNER // 2)
        TH = ar.bf16(INNER // 2)
        YQ = ar.bf16(INNER // 2)
        USG = ar.f32(12)
        yq3 = yqs.rearrange("(s p) i -> p s i", p=128)
        b.wait_pool(T, "t", LW)
        b.wait_pool(T, "t", LS)

        tguard = {}
        yq_tok = {}
        pend = [None]
        evac = {}
        ogg = [0]

        def quant_gen(qb, srow):
            c0 = 6 * qb
            vred(USG[:, c0 + 3:c0 + 4], YB[qb][:, :], ALU.max, absval=True)
            vcopy(AMY[:, srow:srow + 1], USG[:, c0 + 3:c0 + 4])
            b.pre("v", [USG[:, c0 + 3:c0 + 4]], [USG[:, c0 + 5:c0 + 6]])
            vop(V.reciprocal(USG[:, c0 + 5:c0 + 6], USG[:, c0 + 3:c0 + 4]))
            vts(USG[:, c0 + 4:c0 + 5], USG[:, c0 + 5:c0 + 6], 127.0, None,
                ALU.mult, sem="VS")
            yield
            for hf in range(2):
                sl = slice(hf * (INNER // 2), (hf + 1) * (INNER // 2))
                b.wait_tok(V, "v", yq_tok.get("last"))
                vts(U32[:, :], YB[qb][:, sl], USG[:, c0 + 4:c0 + 5], 128.5,
                    ALU.mult, ALU.add, sem="VS")
                yield
                for _ in round_sv_steps(YQ[:, :], U32[:, :], UH[:, :], TH[:, :]):
                    yield
                b.fence("v", "VS")
                b.wait(SY, "sy", "VS")
                yq_tok["last"] = stx(yq3[:, srow, sl], YQ[:, :])
                yield

        def pump(n):
            g = pend[0]
            if g is None:
                return
            try:
                for _ in range(n):
                    next(g)
            except StopIteration:
                pend[0] = None

        for ti in range(2 * NTC):
            cur = ti % 2
            t0 = (ti % 2) * SEQ + (ti // 2) * TPC
            if ti >= 2:
                b.wait(SY, "sy", "TS", tguard[ti - 2])
            ldx(XQ[cur][:, :, :], xqs[:, :, t0:t0 + TPC])
            b.wait_pool(T, "t", LX)
            for sbi in range(NSUB):
                srow = t0 // 128 + sbi
                qb = (ti * NSUB + sbi) % 2
                c0 = 6 * qb
                vts(USG[:, c0:c0 + 1], AMX[:, srow:srow + 1], 1.0 / 127.0,
                    None, ALU.mult)
                vtt(USG[:, c0 + 1:c0 + 2], USG[:, c0:c0 + 1], GLB[:, 0:1], ALU.mult)
                vtt(USG[:, c0 + 2:c0 + 3], USG[:, c0:c0 + 1], GLB[:, 1:2],
                    ALU.mult, sem="VS")
                scl_tok = b.cnt["VS"]
                for og in range(8):
                    obase = og * 512
                    g_ = ogg[0]
                    pg_ = PB[(g_ % 3) * 2]
                    pu_ = PB[(g_ % 3) * 2 + 1]
                    if g_ >= 3:
                        b.wait(T, "t", "SS", evac[g_ - 3][0])
                        b.wait(T, "t", "VS", evac[g_ - 3][1])
                    for k in range(KC):
                        mm(pg_[:, 0:512], XQ[cur][:, k, sbi * 128:(sbi + 1) * 128],
                           WGA[:, k, obase:obase + 512], k == 0, k == KC - 1)
                    b.fence("t", "TS")
                    for k in range(KC):
                        mm(pu_[:, 0:512], XQ[cur][:, k, sbi * 128:(sbi + 1) * 128],
                           WUP[:, k, obase:obase + 512], k == 0, k == KC - 1)
                    b.fence("t", "TS")
                    b.wait(S, "s", "TS", b.cnt["TS"] - 1)
                    b.wait(S, "s", "VS", scl_tok)
                    if g_ >= 2:
                        b.wait(S, "s", "VS", evac[g_ - 2][1])
                    act(GSI[:, (og % 2) * 512:(og % 2 + 1) * 512], pg_[:, 0:512],
                        ACT.Silu, scale=USG[:, c0 + 1:c0 + 2], sem="SS")
                    b.wait(V, "v", "SS", b.cnt["SS"])
                    b.wait(V, "v", "TS", b.cnt["TS"])
                    vstt(YB[qb][:, obase:obase + 512], pu_[:, 0:512],
                         USG[:, c0 + 2:c0 + 3],
                         GSI[:, (og % 2) * 512:(og % 2 + 1) * 512], ALU.mult, ALU.mult,
                         sem="VS")
                    evac[g_] = (b.cnt["SS"], b.cnt["VS"])
                    ogg[0] += 1
                    pump(5)
                pump(50)
                pend[0] = quant_gen(qb, srow)
                pump(2)
            tguard[ti] = b.cnt["TS"]
        pump(50)
        barrier()
        ar.close()

    def emit_seg_c2():
        ar = Arena()
        WD = r3(ar.w(IKC * D), IKC, D)
        wd4 = r3(wqd, IKC, D)
        ldq(1, WD[:, 0:IKC // 2, :], wd4[:, 0:IKC // 2, :])
        ldq(2, WD[:, IKC // 2:IKC, :], wd4[:, IKC // 2:IKC, :])
        YQF = [r3(ar.bf16(IKC * TPC), IKC, TPC) for _ in range(2)]
        X1T = [r3(ar.f32(KC * TPC), KC, TPC) for _ in range(2)]
        RB = ar.f32(TPC)
        RROW = ar.f32(TPC)
        OF = r3(ar.f32(KC * TPC), KC, TPC)
        b.dbg = getattr(b, "dbg", {})
        b.dbg["OF1"] = OF.tensor.name
        b.dbg["X1T1"] = X1T[1].tensor.name
        b.dbg["RB1"] = RB.tensor.name
        OT = [ar.f32(D) for _ in range(TPC // 128)]
        USD = ar.f32(2)
        out3 = out_hbm.rearrange("(s p) d -> p s d", p=128)
        b.wait_pool(T, "t", LW)
        b.wait_pool(T, "t", LS)

        tguard = {}
        ot_tok = {}
        for ti in range(2 * NTC):
            cur = ti % 2
            t0 = (ti % 2) * SEQ + (ti // 2) * TPC
            if ti >= 2:
                b.wait(SY, "sy", "TS", tguard[ti - 2][0])
                b.wait(SY, "sy", "VS", tguard[ti - 2][1])
            for k in range(IKC):
                b.dma(SY, "sy", YQF[cur][:, k, :],
                      yqs[t0:t0 + TPC, k * 128:(k + 1) * 128], LX, transpose=True)
            ldx(X1T[cur][:, :, :], x1b[:, :, t0:t0 + TPC])
            NSUB = TPC // 128
            for sbi in range(NSUB):
                srow = t0 // 128 + sbi
                b.wait(V, "v", "TS")
                b.wait(V, "v", "SS")
                vts(USD[:, 0:1], AMY[:, srow:srow + 1], 1.0 / 127.0, None, ALU.mult)
                vtt(USD[:, 1:2], USD[:, 0:1], GLB[:, 2:3], ALU.mult, sem="VS")
                b.wait(T, "t", "VS")
                mm(PB[4][0:1, 0:128], USD[:, 1:2], IDF[:, 0:128], True, True, sem="TS")
                b.wait(V, "v", "TS")
                vcopy(RROW[0:1, sbi * 128:(sbi + 1) * 128], PB[4][0:1, 0:128], sem="VS")
            b.wait(T, "t", "VS")
            mm(PB[4][:, 0:TPC], ONF[0:1, 0:128], RROW[0:1, 0:TPC], True, True, sem="TS")
            b.wait(S, "s", "TS")
            scopy(RB[:, :], PB[4][:, 0:TPC], sem="SS")
            b.wait_pool(T, "t", LX)
            b.wait_pool(T, "t", LW)
            b.wait_pool(T, "t", LS)
            c2vs0 = b.cnt["VS"]
            oev = {}
            for o in range(KC):
                pp = PB[0] if o % 2 == 0 else PB[1]
                if o >= 2:
                    b.wait(T, "t", "VS", oev[o - 2])
                else:
                    b.wait(T, "t", "VS", c2vs0)
                for k in range(IKC):
                    mm(pp[:, 0:TPC], WD[:, k, o * 128:(o + 1) * 128], YQF[cur][:, k, :],
                       k == 0, k == IKC - 1)
                b.fence("t", "TS")
                b.wait(V, "v", "TS", b.cnt["TS"])
                b.wait(V, "v", "SS", b.cnt["SS"])
                vtt(OF[:, o, :], pp[:, 0:TPC], RB[:, :], ALU.mult)
                vtt(OF[:, o, :], OF[:, o, :], X1T[cur][:, o, :], ALU.add, sem="VS")
                oev[o] = b.cnt["VS"]
            tg_ts = b.cnt["TS"]
            b.wait(T, "t", "SS")
            for sbi in range(TPC // 128):
                if ti >= 1:
                    b.wait_tok(V, "v", ot_tok.get((ti - 1, sbi)))
                for k in range(KC):
                    pbx = PB[4 + (k % 2)]
                    b.wait(T, "t", "VS")
                    tr(pbx[0:128, 0:128], OF[:, k, sbi * 128:(sbi + 1) * 128], IDF[:, :])
                    b.fence("t", "TS")
                    b.wait(V, "v", "TS")
                    vcopy(OT[sbi][:, k * 128:(k + 1) * 128], pbx[0:128, 0:128], sem="VS")
                b.wait(SY, "sy", "VS")
                ot_tok[(ti, sbi)] = stx(out3[:, t0 // 128 + sbi, :],
                                        OT[sbi][:, :])
            tguard[ti] = (tg_ts, b.cnt["VS"])
        barrier()
        ar.close()

    b.dma(G, "g", DUM[:, :], zeros_in[:, 0:2], LW)
    b.pre("g", (), [DUM[:, :]])
    b.op("g", G.engine_nop(), "GS")
    ldx(BIAS[:, :, :], bias_in)
    ldx(IDF[:, :], ident_f)
    ldx(IDB[:, :], ident_b)
    ldx(MSK[:, :], maskd)
    ldx(ONB[:, :], ones_b)
    ldx(ONF[:, :], ones_f)
    ldx(CVS[:, :], cvals)
    for eng, en in ((T, "t"), (V, "v"), (S, "s"), (G, "g")):
        b.wait_pool(eng, en, LX)

    emit_wq_prep()
    emit_w0_prep()
    emit_seg_a()
    emit_seg_b()
    emit_seg_c1()
    emit_seg_c2()

    b.wait_pool(SY, "sy", STP)
    b.stack.close()
    return b


_cached = {}


def _build():
    if "b" not in _cached:
        _cached["b"] = build_program()
    return _cached["b"]


def kernel(L0_attn_x, L0_attn_Wz, L0_attn_bz, L0_attn_Wh, L0_attn_bh, L0_attn_Wg,
           L0_attn_bg, L0_ffn_weight, L0_ffn_aux_weight1, L0_ffn_aux_weight2,
           L0_ffn_bias, L1_attn_Wq, L1_attn_Wk, L1_attn_Wv, L1_attn_Wo, L1_attn_bo,
           L1_ffn_w_gate, L1_ffn_w_up, L1_ffn_w_down):
    x = np.ascontiguousarray(np.asarray(L0_attn_x, np.float32))

    def t(a):
        return np.ascontiguousarray(np.asarray(a, np.float32).T)

    def c(a):
        return np.ascontiguousarray(np.asarray(a, np.float32))

    wzt_h = np.ascontiguousarray(
        np.transpose(np.asarray(L0_attn_Wz, np.float32), (1, 0, 2)).reshape(D, D))
    wht_h = np.ascontiguousarray(
        np.transpose(np.asarray(L0_attn_Wh, np.float32), (1, 0, 2)).reshape(D, D))
    biases = np.zeros((128, 5, KC), np.float32)
    biases[:, 0, :] = np.asarray(L0_attn_bz, np.float32).reshape(KC, 128).T
    biases[:, 1, :] = np.asarray(L0_attn_bh, np.float32).reshape(KC, 128).T
    biases[:, 2, :] = np.asarray(L0_attn_bg, np.float32).reshape(KC, 128).T
    biases[:, 3, :] = np.asarray(L0_ffn_bias, np.float32).reshape(KC, 128).T
    biases[:, 4, :] = np.asarray(L1_attn_bo, np.float32).reshape(KC, 128).T

    common = dict(
        wzt=wzt_h, wht=wht_h, wg0=c(L0_attn_Wg), w0pt=t(L0_ffn_weight),
        a10=c(L0_ffn_aux_weight1), a20=c(L0_ffn_aux_weight2),
        wqt=t(L1_attn_Wq), wkt=t(L1_attn_Wk), wvt=t(L1_attn_Wv), wot=t(L1_attn_Wo),
        wgr=t(L1_ffn_w_gate), wur=t(L1_ffn_w_up), wdr=t(L1_ffn_w_down),
        biases=biases,
        ident_f=np.eye(128, dtype=np.float32),
        ident_b=np.eye(128, dtype=bfdt),
        maskd=np.ascontiguousarray(np.tril(np.ones((128, 128), np.float32)).T).astype(bfdt),
        ones_b=np.ones((128, 128), dtype=bfdt),
        ones_f=np.ones((128, 128), dtype=np.float32),
        cvals=np.tile(np.array([EPS_LN, EPS_ATT, -128.0, 0, 0, 0, 0, 0], np.float32), (128, 1)),
        zeros=np.zeros((128, 2 * NH * HD), np.float32),
    )

    in_maps = []
    for ci in range(NCORES):
        m = dict(common)
        m["x"] = np.ascontiguousarray(x[2 * ci:2 * ci + 2].reshape(2 * SEQ, D))
        in_maps.append(m)

    bb = _build()
    res = run_bass_kernel_spmd(bb.nc, in_maps, core_ids=list(range(NCORES)))
    out = np.empty((16, SEQ, D), np.float32)
    for ci in range(NCORES):
        out[2 * ci:2 * ci + 2] = np.asarray(res.results[ci]["out"]).reshape(2, SEQ, D)
    return out


# version 29
# speedup vs baseline: 1.0154x; 1.0131x over previous
import numpy as np
import ml_dtypes
from contextlib import ExitStack

import concourse.bass as bass
import concourse.mybir as mybir
from concourse.bass_utils import run_bass_kernel_spmd

F32 = mybir.dt.float32
BF16 = mybir.dt.bfloat16
FP16 = mybir.dt.float16
ALU = mybir.AluOpType
ACT = mybir.ActivationFunctionType
AX = mybir.AxisListType.X

NCORES = 8
SEQ = 2048
D = 1024
KC = D // 128
NH = 16
HD = 64
INNER = 4096
IKC = INNER // 128

TPA = 256
TPB = 256
TPC = 512
NTA = SEQ // TPA
NTB = SEQ // TPB
NTC = SEQ // TPC

EPS_LN = 1e-5
EPS_ATT = 1e-5
MAGIC = float(np.float32(12582912.0))

ROUND_LADDER = True

bfdt = ml_dtypes.bfloat16


class B:
    def __init__(self):
        self.nc = bass.Bass("TRN2", target_bir_lowering=False)
        self.stack = ExitStack()
        self.sems = {}
        self.cnt = {}
        self.waited = {}
        self.last = {}
        self.lastinc = {}
        self.mkdummy = {}
        self.pools = {}
        self.dirty = {}
        self.engines = {}

    def sem(self, name):
        h = self.stack.enter_context(self.nc.semaphore(name))
        self.sems[name] = h
        self.cnt[name] = 0
        return h

    def pool(self, prefix, n):
        names = []
        for i in range(n):
            self.sem(f"{prefix}{i}")
            names.append(f"{prefix}{i}")
        self.pools[prefix] = {"names": names, "idx": 0}
        return prefix

    def dma(self, qeng, qname, dst, src, prefix, transpose=False):
        p = self.pools[prefix]
        sname = p["names"][p["idx"]]
        p["idx"] = (p["idx"] + 1) % len(p["names"])
        self.wait(qeng, qname, sname)
        if transpose:
            inst = qeng.dma_start_transpose(dst, src)
        else:
            inst = qeng.dma_start(dst, src)
        self.inc(inst, sname, 16)
        return (sname, self.cnt[sname])

    def wait_tok(self, eng, engname, tok):
        if tok is not None:
            self.wait(eng, engname, tok[0], tok[1])

    def wait_pool(self, eng, engname, prefix):
        for sname in self.pools[prefix]["names"]:
            self.wait(eng, engname, sname)

    def sbuf(self, name, shape, dtype):
        return self.stack.enter_context(self.nc.sbuf_tensor(name, shape, dtype))

    def psum(self, name, shape, dtype):
        return self.stack.enter_context(self.nc.psum_tensor(name, shape, dtype))

    def inc(self, inst, sem, k=1):
        inst.then_inc(self.sems[sem], k)
        self.cnt[sem] += k
        self.lastinc[inst] = sem

    def wait(self, eng, engname, sem, val=None):
        if val is None:
            val = self.cnt[sem]
        if val <= 0:
            return
        val = min(val, self.cnt[sem])
        key = (engname, sem)
        if self.waited.get(key, -1) >= val:
            return
        eng.wait_ge(self.sems[sem], val)
        self.waited[key] = val

    def op(self, engname, inst, sem=None, k=1):
        self.last[engname] = inst
        if sem is not None:
            self.inc(inst, sem, k)
        return inst

    @staticmethod
    def _span(ap):
        try:
            dims = ap.ap
            lo = ap.offset
            hi = lo + 1
            for (stride, size) in dims[1:]:
                hi += abs(stride) * (size - 1)
            pb = ap.base_partition
            return (ap.tensor.name, pb, pb + dims[0][1], lo, hi)
        except Exception:
            return (ap.tensor.name, 0, 128, 0, 1 << 40)

    @staticmethod
    def _overlap(a, b):
        return (a[0] == b[0] and a[1] < b[2] and b[1] < a[2]
                and a[3] < b[4] and b[3] < a[4])

    def pre(self, engname, reads=(), writes=()):
        st = self.dirty.get(engname)
        if not isinstance(st, tuple):
            self.dirty[engname] = st = ([], [])
        tw, trd = st
        rs = [self._span(ap) for ap in reads]
        ws = [self._span(ap) for ap in writes]
        hazard = (tw or trd) and (
            (not rs and not ws)
            or any(self._overlap(t, s) for t in tw for s in rs + ws)
            or any(self._overlap(t, s) for t in trd for s in ws))
        if hazard:
            self.engines[engname].drain()
            del tw[:]
            del trd[:]
        tw.extend(ws)
        trd.extend(rs)

    def fence(self, engname, sem):
        inst = self.last.get(engname)
        assert inst is not None, f"fence on idle engine {engname}"
        have = self.lastinc.get(inst)
        if have is None:
            self.inc(inst, sem, 1)
            return
        if have == sem:
            return
        inst2 = self.mkdummy[engname]()
        self.last[engname] = inst2
        self.inc(inst2, sem, 1)


def build_program():
    b = B()
    nc = b.nc
    T, V, S, G, SY = nc.tensor, nc.vector, nc.scalar, nc.gpsimd, nc.sync

    def din(name, shape, dtype=F32):
        return nc.dram_tensor(name, shape, dtype, kind="ExternalInput").ap()

    def dint(name, shape, dtype):
        return nc.dram_tensor(name, shape, dtype, kind="Internal").ap()

    x_in = din("x", [2 * SEQ, D])
    out_hbm = nc.dram_tensor("out", [2 * SEQ, D], F32, kind="ExternalOutput").ap()

    wzt = din("wzt", [D, D]); wht = din("wht", [D, D]); wg0 = din("wg0", [D, D])
    w0pt = din("w0pt", [D, D]); a10 = din("a10", [D, D]); a20 = din("a20", [D, D])
    wqt = din("wqt", [D, D]); wkt = din("wkt", [D, D])
    wvt = din("wvt", [D, D]); wot = din("wot", [D, D])
    wgr = din("wgr", [D, INNER]); wur = din("wur", [D, INNER]); wdr = din("wdr", [INNER, D])
    bias_in = din("biases", [128, 5, KC])
    ident_f = din("ident_f", [128, 128], F32)
    ident_b = din("ident_b", [128, 128], BF16)
    maskd = din("maskd", [128, 128], BF16)
    ones_b = din("ones_b", [128, 128], BF16)
    ones_f = din("ones_f", [128, 128], F32)
    cvals = din("cvals", [128, 8], F32)
    zeros_in = din("zeros", [128, 2 * NH * HD], F32)

    x1a = dint("x1a", [128, KC, 2 * SEQ], F32)
    x1b = dint("x1b", [128, KC, 2 * SEQ], F32)
    xqs = dint("xqs", [128, KC, 2 * SEQ], BF16)
    yqs = dint("yqs", [2 * SEQ, INNER], BF16)
    wqg = dint("wqg", [128, KC * INNER], BF16)
    wqu = dint("wqu", [128, KC * INNER], BF16)
    wqd = dint("wqd", [128, IKC * D], BF16)
    wzh = dint("wzh", [128, KC * D], BF16); wzl = dint("wzl", [128, KC * D], BF16)
    whh = dint("whh", [128, KC * D], BF16); whl = dint("whl", [128, KC * D], BF16)
    wgh = dint("wgh", [128, KC * D], BF16); wgl = dint("wgl", [128, KC * D], BF16)
    wch = dint("wch", [128, KC * D], BF16); wcl = dint("wcl", [128, KC * D], BF16)

    for sname in ["TS", "VS", "SS", "GS"]:
        b.sem(sname)
    LW = b.pool("LW", 4)
    LX = b.pool("LX", 6)
    STP = b.pool("ST", 6)
    LS = b.pool("LS", 3)
    LVQ = b.pool("LV", 3)

    GLB = b.sbuf("GLB", [128, 16], F32)
    BIAS = b.sbuf("BIAS", [128, 5, KC], F32)
    BZN = b.sbuf("BZN", [128, KC], F32)
    IDF = b.sbuf("IDF", [128, 128], F32)
    IDB = b.sbuf("IDB", [128, 128], BF16)
    MSK = b.sbuf("MSK", [128, 128], BF16)
    ONB = b.sbuf("ONB", [128, 128], BF16)
    ONF = b.sbuf("ONF", [128, 128], F32)
    CVS = b.sbuf("CVS", [128, 8], F32)
    AMX = b.sbuf("AMX", [128, 2 * SEQ // 128], F32)
    AMY = b.sbuf("AMY", [128, 2 * SEQ // 128], F32)
    SST = b.sbuf("SST", [128, 2 * KC], F32)

    PB = [b.psum(f"pb{i}", [128, 512], F32) for i in range(6)]
    PT = [b.psum(f"pt{i}", [128, 1024], BF16) for i in range(2)]
    DUM = b.sbuf("DUM", [128, 2], F32)
    b.engines = {"t": T, "v": V, "s": S, "g": G}

    def _dummy(en, mk):
        def f():
            b.engines[en].drain()
            d = b.dirty.get(en)
            if isinstance(d, tuple):
                del d[0][:]
                del d[1][:]
            else:
                b.dirty[en] = ([], [])
            return mk()
        return f
    b.mkdummy = {
        "t": _dummy("t", lambda: T.nop()),
        "v": _dummy("v", lambda: V.engine_nop()),
        "g": _dummy("g", lambda: G.engine_nop()),
        "s": _dummy("s", lambda: S.copy(DUM[0:1, 0:1], DUM[0:1, 1:2])),
    }

    def _aps(*xs):
        return [x for x in xs if hasattr(x, "tensor")]

    def act(out, in_, func, bias=0.0, scale=1.0, accum_out=None, sem=None):
        wr = _aps(out, accum_out)
        b.pre("s", _aps(in_, bias, scale), wr)
        i = b.op("s", S.activation(out, in_, func, bias=bias, scale=scale,
                                   accum_out=accum_out), sem)
        return i

    def vop(inst, sem=None):
        return b.op("v", inst, sem)

    def vts(out, in0, s1, s2, op0, op1=None, sem=None):
        b.pre("v", _aps(in0, s1, s2), [out])
        if op1 is None:
            return vop(V.tensor_scalar(out, in0, s1, None, op0), sem)
        return vop(V.tensor_scalar(out, in0, s1, s2, op0, op1), sem)

    def vtt(out, a_, b_, opx, sem=None):
        b.pre("v", _aps(a_, b_), [out])
        return vop(V.tensor_tensor(out, a_, b_, opx), sem)

    def vstt(out, in0, scal, in1, op0, op1, sem=None):
        b.pre("v", _aps(in0, scal, in1), [out])
        return vop(V.scalar_tensor_tensor(out, in0, scal, in1, op0, op1), sem)

    def vcopy(out, in_, sem=None):
        b.pre("v", _aps(in_), [out])
        return vop(V.tensor_copy(out, in_), sem)

    def vred(out, in_, opx, absval=False, sem=None):
        b.pre("v", _aps(in_), [out])
        return vop(V.tensor_reduce(out, in_, AX, opx,
                                   apply_absolute_value=absval), sem)

    def scopy(out, in_, sem=None):
        b.pre("s", _aps(in_), [out])
        i = b.op("s", S.copy(out, in_), sem)
        return i

    def mm(out, lhsT, rhs, start, stop, sem=None, tile_position=None):
        return b.op("t", T.matmul(out, lhsT, rhs, start=start, stop=stop,
                                  tile_position=tile_position), sem)

    def tr(out, in_, ident, sem=None):
        return b.op("t", T.transpose(out, in_, ident), sem)

    def ldx(dst, src):
        return b.dma(SY, "sy", dst, src, LX)

    _ldq = ((SY, "sy", LX), (G, "g", LW), (S, "s", LS))

    def ldq(i, dst, src):
        q = _ldq[i % 3]
        return b.dma(q[0], q[1], dst, src, q[2])

    def ldw(dst, src):
        return b.dma(G, "g", dst, src, LW)

    def stx(dst, src):
        return b.dma(SY, "sy", dst, src, STP)

    def barrier():
        for en, sname in (("t", "TS"), ("v", "VS"), ("s", "SS"), ("g", "GS")):
            if b.last.get(en) is not None:
                b.fence(en, sname)
        vals = {sn: b.cnt[sn] for sn in ("TS", "VS", "SS", "GS")}
        for eng, en in ((T, "t"), (V, "v"), (S, "s"), (G, "g"), (SY, "sy")):
            for sn in ("TS", "VS", "SS", "GS"):
                b.wait(eng, en, sn, vals[sn])
            for pool in (LW, LX, STP):
                b.wait_pool(eng, en, pool)

    class Arena:
        _uid = [0]

        def __init__(self):
            self.stack = ExitStack()

        def _mk(self, n, dtype):
            Arena._uid[0] += 1
            t = self.stack.enter_context(
                nc.sbuf_tensor(f"ar{Arena._uid[0]}", [128, n], dtype))
            return t[:, :]

        def f32(self, n):
            return self._mk(n, F32)

        def bf16(self, n):
            return self._mk(n, BF16)

        def fp16(self, n):
            return self._mk(n, FP16)

        def w(self, n):
            return self._mk(n, BF16)

        def close(self):
            self.stack.close()

    def r3(ap2d, c, n):
        return ap2d.rearrange("p (c n) -> p c n", c=c, n=n)

    def emit_round(dst, u32, uh, th):
        if not ROUND_LADDER:
            vts(dst, u32, MAGIC - 0.5, MAGIC + 128.0, ALU.add, ALU.subtract)
            return
        vcopy(uh, u32)
        for c in (128.0, 64.0, 32.0, 16.0, 8.0, 4.0, 2.0, 1.0):
            vts(th, uh, c, -c, ALU.is_ge, ALU.mult)
            vtt(uh, uh, th, ALU.add)
        vstt(dst, u32, 128.0, uh, ALU.subtract, ALU.subtract)

    def round_sv_steps(dst, u32, uh, th):
        first = True
        for c in (128.0, 64.0, 32.0, 16.0, 8.0, 4.0, 2.0, 1.0):
            src = u32 if first else uh
            b.wait(S, "s", "VS")
            act(th, src, ACT.Sign, bias=CVS[:, 2:3], sem="SS")
            yield
            b.wait(V, "v", "SS")
            vstt(uh, th, -0.5 * c, src, ALU.mult, ALU.add, sem="VS")
            first = False
            yield
        vstt(dst, u32, 0.5, uh, ALU.subtract, ALU.subtract)
        yield

    def emit_round_sv(dst, u32, uh, th):
        for _ in round_sv_steps(dst, u32, uh, th):
            pass

    def emit_wq_prep():
        ar = Arena()
        NWB = 4
        wbuf = [ar.f32(4096) for _ in range(NWB)]
        sums = ar.f32(IKC)
        tot = ar.f32(2)
        sg = [ar.bf16(4096), ar.bf16(4096)]
        wab = [ar.bf16(4096), ar.bf16(4096)]
        wabl = [ar.bf16(4096), ar.bf16(4096)]
        wab32 = [ar.f32(4096), ar.f32(4096)]
        mg = [ar.bf16(4096), ar.bf16(4096)]
        cons = {}
        gl = [0]
        for (src, dst, nch, width, widx) in (
                (wgr, wqg, KC, INNER, 0), (wur, wqu, KC, INNER, 1),
                (wdr, wqd, IKC, D, 2)):
            src3 = src.rearrange("(c p) o -> p c o", p=128)
            dst3 = r3(dst, nch, width)
            mmtok = {}
            for k in range(nch):
                g_ = gl[0]
                d_ = k % 2
                q = _ldq[g_ % 3]
                b.wait_tok(q[0], q[1], cons.get(g_ - NWB))
                ltok = ldq(g_, wbuf[g_ % NWB][:, 0:width], src3[:, k, :])
                b.wait_tok(S, "s", ltok)
                if k >= 2:
                    b.wait(S, "s", "TS", mmtok.get(k - 2, 0))
                b.wait(S, "s", "VS")
                act(wab32[d_][:, 0:width], wbuf[g_ % NWB][:, 0:width], ACT.Abs,
                    sem="SS")
                cons[g_] = ("SS", b.cnt["SS"])
                gl[0] += 1
                b.wait(V, "v", "SS")
                if k >= 2:
                    b.wait(V, "v", "TS", mmtok.get(k - 2, 0))
                vcopy(wab[d_][:, 0:width], wab32[d_][:, 0:width])
                vtt(wabl[d_][:, 0:width], wab32[d_][:, 0:width], wab[d_][:, 0:width],
                    ALU.subtract, sem="VS")
                b.wait(T, "t", "SS")
                b.wait(T, "t", "VS")
                for cs in range(width // 512):
                    mm(PB[0][0:1, 0:512], ONB[:, 0:1],
                       wab[d_][:, cs * 512:(cs + 1) * 512],
                       (k == 0 and cs == 0), False)
                    mm(PB[0][0:1, 0:512], ONB[:, 0:1],
                       wabl[d_][:, cs * 512:(cs + 1) * 512],
                       False, (k == nch - 1 and cs == width // 512 - 1))
                b.fence("t", "TS")
                mmtok[k] = b.cnt["TS"]
            b.wait(V, "v", "TS")
            vred(tot[0:1, 0:1], PB[0][0:1, 0:512], ALU.add)
            vts(tot[0:1, 1:2], tot[0:1, 0:1], 1.0 / (nch * 128 * width), 1e-5,
                ALU.mult, ALU.max, sem="VS")
            b.wait(T, "t", "VS")
            mm(PB[1][:, 0:1], ONF[0:1, 0:128], tot[0:1, 1:2], True, True, sem="TS")
            b.wait(V, "v", "TS")
            vcopy(GLB[:, widx:widx + 1], PB[1][:, 0:1])
            vts(GLB[:, 3 + widx:4 + widx], PB[1][:, 0:1], 0.5, None, ALU.mult, sem="VS")
            vtok = {}
            for k in range(nch):
                g_ = gl[0]
                d_ = k % 2
                wb = wbuf[g_ % NWB][:, 0:width]
                q = _ldq[g_ % 3]
                b.wait_tok(q[0], q[1], cons.get(g_ - NWB))
                ltok = ldq(g_, wb, src3[:, k, :])
                b.wait_tok(S, "s", ltok)
                b.wait(S, "s", "VS", vtok.get(k - 2, 0) if k >= 2 else None)
                act(sg[d_][:, 0:width], wb, ACT.Sign)
                act(wab32[d_][:, 0:width], wb, ACT.Abs, sem="SS")
                cons[g_] = ("SS", b.cnt["SS"])
                gl[0] += 1
                b.wait_pool(V, "v", LX)
                for _sn in b.pools[STP]["names"]:
                    b.wait(V, "v", _sn)
                m_ = mg[k % 2][:, 0:width]
                b.wait(V, "v", "SS")
                vts(m_, wab32[d_][:, 0:width], GLB[:, 3 + widx:4 + widx], None, ALU.is_ge)
                vtt(m_, m_, sg[d_][:, 0:width], ALU.mult, sem="VS")
                vtok[k] = b.cnt["VS"]
                b.wait(SY, "sy", "VS")
                stx(dst3[:, k, :], m_)
        barrier()
        ar.close()

    def emit_w0_prep():
        arM = Arena()
        M1H = r3(arM.bf16(KC * D), KC, D)
        M1L = r3(arM.bf16(KC * D), KC, D)
        ar1 = Arena()
        A1H = r3(ar1.bf16(KC * D), KC, D); A1L = r3(ar1.bf16(KC * D), KC, D)
        WTH = r3(ar1.bf16(KC * D), KC, D); WTL = r3(ar1.bf16(KC * D), KC, D)
        st1 = [ar1.f32(D) for _ in range(2)]
        for (src, dh, dl) in ((a10, A1H, A1L), (w0pt, WTH, WTL)):
            src3 = src.rearrange("(c p) o -> p c o", p=128)
            for k in range(KC):
                cu = k % 2
                q = _ldq[k % 3]
                b.wait(q[0], q[1], "SS")
                b.wait(q[0], q[1], "VS")
                ltok = ldq(k, st1[cu][:, :], src3[:, k, :])
                b.wait_tok(S, "s", ltok)
                scopy(dh[:, k, :], st1[cu][:, :], sem="SS")
                b.wait(V, "v", "SS")
                vtt(dl[:, k, :], st1[cu][:, :], dh[:, k, :], ALU.subtract, sem="VS")
        b.wait(T, "t", "SS")
        b.wait(T, "t", "VS")
        for cc in range(KC):
            for rn in range(2):
                pp = PB[0] if rn == 0 else PB[1]
                b.wait(T, "t", "SS")
                b.wait(T, "t", "VS")
                for j in range(KC):
                    la = A1H[:, j, cc * 128:(cc + 1) * 128]
                    ll = A1L[:, j, cc * 128:(cc + 1) * 128]
                    rh = WTH[:, j, rn * 512:(rn + 1) * 512]
                    rl = WTL[:, j, rn * 512:(rn + 1) * 512]
                    mm(pp[:, 0:512], la, rh, j == 0, False)
                    mm(pp[:, 0:512], ll, rh, False, False)
                    mm(pp[:, 0:512], la, rl, False, j == KC - 1)
                b.fence("t", "TS")
                b.wait(S, "s", "TS")
                scopy(M1H[:, cc, rn * 512:(rn + 1) * 512], pp[:, 0:512], sem="SS")
                b.wait(V, "v", "SS")
                b.wait(V, "v", "TS")
                vtt(M1L[:, cc, rn * 512:(rn + 1) * 512], pp[:, 0:512],
                    M1H[:, cc, rn * 512:(rn + 1) * 512], ALU.subtract, sem="VS")
        barrier()
        ar1.close()
        ar2 = Arena()
        A2H = r3(ar2.bf16(KC * D), KC, D); A2L = r3(ar2.bf16(KC * D), KC, D)
        st2 = [ar2.f32(D) for _ in range(2)]
        och = [ar2.bf16(D) for _ in range(2)]
        ocl = [ar2.bf16(D) for _ in range(2)]
        src3 = a20.rearrange("(c p) o -> p c o", p=128)
        for k in range(KC):
            cu = k % 2
            q = _ldq[k % 3]
            b.wait(q[0], q[1], "SS")
            b.wait(q[0], q[1], "VS")
            ltok = ldq(k, st2[cu][:, :], src3[:, k, :])
            b.wait_tok(S, "s", ltok)
            scopy(A2H[:, k, :], st2[cu][:, :], sem="SS")
            b.wait(V, "v", "SS")
            vtt(A2L[:, k, :], st2[cu][:, :], A2H[:, k, :], ALU.subtract, sem="VS")
        b.wait(T, "t", "SS")
        b.wait(T, "t", "VS")
        wch3, wcl3 = r3(wch, KC, D), r3(wcl, KC, D)
        ot_tok = {}
        for rc in range(KC):
            cu = rc % 2
            for on in range(2):
                pp = PB[0] if on == 0 else PB[1]
                b.wait(T, "t", "SS")
                b.wait(T, "t", "VS")
                for cc in range(KC):
                    la = M1H[:, cc, rc * 128:(rc + 1) * 128]
                    ll = M1L[:, cc, rc * 128:(rc + 1) * 128]
                    rh = A2H[:, cc, on * 512:(on + 1) * 512]
                    rl = A2L[:, cc, on * 512:(on + 1) * 512]
                    mm(pp[:, 0:512], la, rh, cc == 0, False)
                    mm(pp[:, 0:512], ll, rh, False, False)
                    mm(pp[:, 0:512], la, rl, False, cc == KC - 1)
                b.fence("t", "TS")
                b.wait(S, "s", "TS")
                b.wait_tok(S, "s", ot_tok.get((rc - 2, 0)))
                b.wait_tok(V, "v", ot_tok.get((rc - 2, 1)))
                scopy(och[cu][:, on * 512:(on + 1) * 512], pp[:, 0:512], sem="SS")
                b.wait(V, "v", "SS")
                b.wait(V, "v", "TS")
                vtt(ocl[cu][:, on * 512:(on + 1) * 512], pp[:, 0:512],
                    och[cu][:, on * 512:(on + 1) * 512], ALU.subtract, sem="VS")
            b.wait(SY, "sy", "SS")
            b.wait(SY, "sy", "VS")
            ot_tok[(rc, 0)] = stx(wch3[:, rc, :], och[cu][:, :])
            ot_tok[(rc, 1)] = stx(wcl3[:, rc, :], ocl[cu][:, :])
        barrier()
        ar2.close()
        arM.close()

    def emit_seg_a():
        ar = Arena()
        WZH = r3(ar.w(KC * D), KC, D); WZL = r3(ar.w(KC * D), KC, D)
        WHH = r3(ar.w(KC * D), KC, D); WHL = r3(ar.w(KC * D), KC, D)
        WGH = r3(ar.w(KC * D), KC, D); WGL = r3(ar.w(KC * D), KC, D)
        WCH = r3(ar.w(KC * D), KC, D); WCL = r3(ar.w(KC * D), KC, D)
        for wi, (dstw, srcw) in enumerate(((WCH, wch), (WCL, wcl))):
            ldq(wi + 1, dstw[:, :, :], r3(srcw, KC, D))
        WST = [ar.f32(D)]
        wcons = {}
        gi = 0
        for (dh, dl, srcw) in ((WZH, WZL, wzt), (WHH, WHL, wht), (WGH, WGL, wg0)):
            s3 = srcw.rearrange("(c p) o -> p c o", p=128)
            for k in range(KC):
                cu = 0
                q = _ldq[gi % 3]
                b.wait_tok(q[0], q[1], wcons.get(gi - 1))
                ltok = ldq(gi, WST[cu][:, :], s3[:, k, :])
                b.wait_tok(S, "s", ltok)
                scopy(dh[:, k, :], WST[cu][:, :], sem="SS")
                b.wait(V, "v", "SS", b.cnt["SS"])
                vtt(dl[:, k, :], WST[cu][:, :], dh[:, k, :], ALU.subtract, sem="VS")
                wcons[gi] = ("VS", b.cnt["VS"])
                gi += 1
        wsplit_tok = (b.cnt["SS"], b.cnt["VS"])

        NS = TPA // 128
        XT = [r3(ar.f32(NS * D), NS, D) for _ in range(2)]
        XNT = r3(ar.f32(NS * D), NS, D)
        XF = [r3(ar.f32(KC * TPA), KC, TPA) for _ in range(2)]
        XNFH = [r3(ar.bf16(KC * TPA), KC, TPA) for _ in range(2)]
        AA = r3(ar.f32(KC * TPA), KC, TPA)
        BB = r3(ar.f32(KC * TPA), KC, TPA)
        HF = BB
        HBH = r3(ar.bf16(KC * TPA), KC, TPA)
        ZS = ar.f32(TPA)
        GSc = ar.f32(TPA)
        FS = ar.f32(TPA)
        ATS = ZS
        STT = ar.f32(8)

        def mm3(pp, wh_, wl_, xh_, o, n):
            for k in range(KC):
                mm(pp[:, 0:n], wh_[:, k, o * 128:(o + 1) * 128], xh_[:, k, :],
                   k == 0, False)
                mm(pp[:, 0:n], wl_[:, k, o * 128:(o + 1) * 128], xh_[:, k, :],
                   False, k == KC - 1)
            b.fence("t", "TS")

        x3 = x_in.rearrange("(s p) d -> p s d", p=128)
        b.dma(G, "g", SST[:, :], zeros_in[:, 0:2 * KC], LW)
        b.wait_pool(V, "v", LX)
        vts(BZN[:, :], BIAS[:, 0, :], -1.0, None, ALU.mult, sem="VS")
        b.wait_pool(G, "g", LW)
        b.wait_pool(T, "t", LW)
        b.wait_pool(T, "t", LS)
        b.wait_pool(T, "t", LX)
        b.wait(V, "v", "GS")

        tile_guard = {}
        st_toks = {}
        prep_done = {}
        scan_tok = {}
        hbs_tok = {}
        gate_tok = {}

        NTA2 = 2 * NTA

        def tile_prep(j):
            cu = j % 2
            p_ = j % 2
            sub0 = p_ * (SEQ // 128) + (j // 2) * NS
            if j >= 2:
                b.wait(SY, "sy", "TS", tile_guard[j - 2])
            ldx(XT[cu][:, :, :], x3[:, sub0:sub0 + NS, :])
            yield
            b.wait_pool(V, "v", LX)
            if j >= 1:
                b.wait(V, "v", "TS", tile_guard[j - 1])
            for sbi in range(NS):
                xs = XT[cu][:, sbi, :]
                sq = XNT[:, sbi, :]
                vred(STT[:, 0:1], xs, ALU.add)
                vstt(sq, xs, 0.0, xs, ALU.add, ALU.mult)
                vred(STT[:, 1:2], sq, ALU.add)
                vts(STT[:, 2:3], STT[:, 0:1], 1.0 / D, None, ALU.mult)
                vts(STT[:, 3:4], STT[:, 1:2], 1.0 / D, EPS_LN, ALU.mult, ALU.add)
                vtt(STT[:, 4:5], STT[:, 2:3], STT[:, 2:3], ALU.mult)
                vstt(STT[:, 3:4], STT[:, 4:5], -1.0, STT[:, 3:4], ALU.mult, ALU.add)
                vts(STT[:, 4:5], STT[:, 3:4], 0.0, 1.0, ALU.mult, ALU.add)
                for _ in range(5):
                    vtt(STT[:, 5:6], STT[:, 4:5], STT[:, 4:5], ALU.mult)
                    vtt(STT[:, 5:6], STT[:, 5:6], STT[:, 3:4], ALU.mult)
                    vts(STT[:, 5:6], STT[:, 5:6], -0.5, 1.5, ALU.mult, ALU.add)
                    vtt(STT[:, 4:5], STT[:, 4:5], STT[:, 5:6], ALU.mult)
                vts(XNT[:, sbi, :], xs, STT[:, 2:3], STT[:, 4:5],
                    ALU.subtract, ALU.mult)
                yield
            b.fence("v", "VS")
            ln_vs = b.cnt["VS"]
            b.wait(T, "t", "VS", ln_vs)
            pbtok = {}
            for sbi in range(NS):
                for k in range(KC):
                    bk = k % 2
                    pbx = PB[4 + bk]
                    if bk in pbtok:
                        b.wait(T, "t", "SS", pbtok[bk][0])
                    tr(pbx[0:128, 0:128], XNT[:, sbi, k * 128:(k + 1) * 128], IDF[:, :])
                    b.fence("t", "TS")
                    trts = b.cnt["TS"]
                    b.wait(S, "s", "TS", trts)
                    scopy(XNFH[cu][:, k, sbi * 128:(sbi + 1) * 128],
                          pbx[0:128, 0:128], sem="SS")
                    pbtok[bk] = (b.cnt["SS"], b.cnt["SS"])
                    if k % 4 == 3:
                        yield
            if j >= 2:
                b.wait_tok(V, "v", st_toks.get(j - 2))
            pbt2 = {}
            for sbi in range(NS):
                for k in range(KC):
                    bk = k % 2
                    pbx = PB[4 + bk]
                    if bk in pbt2:
                        b.wait(T, "t", "VS", pbt2[bk])
                    elif pbtok:
                        b.wait(T, "t", "SS", pbtok[bk][0])
                    tr(pbx[0:128, 0:128], XT[cu][:, sbi, k * 128:(k + 1) * 128],
                       IDF[:, :])
                    b.fence("t", "TS")
                    b.wait(V, "v", "TS", b.cnt["TS"])
                    vcopy(XF[cu][:, k, sbi * 128:(sbi + 1) * 128],
                          pbx[0:128, 0:128], sem="VS")
                    pbt2[bk] = b.cnt["VS"]
                    if k % 4 == 3:
                        yield
            tile_guard[j] = b.cnt["TS"]
            prep_done[j] = (b.cnt["SS"], b.cnt["VS"])
            yield

        prep = [None]

        def ppump(n):
            g = prep[0]
            if g is None:
                return
            try:
                for _ in range(n):
                    next(g)
            except StopIteration:
                prep[0] = None

        for _ in tile_prep(0):
            pass
        b.wait(T, "t", "SS", wsplit_tok[0])
        b.wait(T, "t", "VS", wsplit_tok[1])
        for i in range(NTA2):
            cur = i % 2
            p_ = i % 2
            ti = i // 2
            if i + 1 < NTA2:
                prep[0] = tile_prep(i + 1)
            zev = {}
            for o in range(KC):
                pz = PB[0] if o % 2 == 0 else PB[1]
                ph = PB[2] if o % 2 == 0 else PB[3]
                if o >= 2:
                    b.wait(T, "t", "SS", zev[o - 2][0])
                    b.wait(T, "t", "VS", zev[o - 2][1])
                elif i >= 1:
                    b.wait(T, "t", "SS", gate_tok[i - 1][0])
                    b.wait(T, "t", "VS", gate_tok[i - 1][1])
                mm3(pz, WZH, WZL, XNFH[cur], o, TPA)
                mm3(ph, WHH, WHL, XNFH[cur], o, TPA)
                b.wait(S, "s", "TS", b.cnt["TS"] - 1)
                if i >= 1:
                    b.wait(S, "s", "VS", scan_tok[i - 1])
                if o >= 1:
                    b.wait(S, "s", "VS", zev[o - 1][1])
                act(ZS[:, :], pz[:, 0:TPA], ACT.Sigmoid, bias=BIAS[:, 0, o:o + 1], sem="SS")
                act(AA[:, o, :], pz[:, 0:TPA], ACT.Sigmoid, bias=BZN[:, o:o + 1],
                    scale=-1.0, sem="SS")
                b.wait(V, "v", "TS", b.cnt["TS"])
                b.wait(V, "v", "SS", b.cnt["SS"])
                if i >= 1:
                    b.wait(V, "v", "SS", hbs_tok[i - 1][0])
                vstt(BB[:, o, :], ph[:, 0:TPA], BIAS[:, 1, o:o + 1], ZS[:, :],
                     ALU.add, ALU.mult, sem="VS")
                zev[o] = (b.cnt["SS"], b.cnt["VS"])
                ppump(1)
            if i >= 1:
                b.wait(V, "v", "SS", hbs_tok[i - 1][0])
            for k in range(KC):
                sk = p_ * KC + k
                b.pre("v", [AA[:, k, :], BB[:, k, :], SST[:, sk:sk + 1]], [HF[:, k, :]])
                vop(V.tensor_tensor_scan(HF[:, k, :], AA[:, k, :],
                                         BB[:, k, :], SST[:, sk:sk + 1],
                                         ALU.mult, ALU.add))
                vcopy(SST[:, sk:sk + 1], HF[:, k, TPA - 1:TPA])
            b.fence("v", "VS")
            vscan = b.cnt["VS"]
            scan_tok[i] = vscan
            ppump(2)
            b.wait(S, "s", "VS", vscan)
            if i >= 1:
                b.wait(S, "s", "TS", gate_tok[i - 1][2])
                b.wait(V, "v", "TS", gate_tok[i - 1][2])
            for k in range(KC):
                scopy(HBH[:, k, :], HF[:, k, :], sem="SS")
            hbs_tok[i] = (b.cnt["SS"], b.cnt["VS"])
            gev = {}
            for o in range(KC):
                pg = PB[0] if o % 2 == 0 else PB[1]
                pf = PB[2] if o % 2 == 0 else PB[3]
                if o >= 2:
                    b.wait(T, "t", "SS", gev[o - 2][0])
                else:
                    b.wait(T, "t", "SS", hbs_tok[i][0])
                    b.wait(T, "t", "VS", hbs_tok[i][1])
                mm3(pg, WGH, WGL, HBH, o, TPA)
                mm3(pf, WCH, WCL, XNFH[cur], o, TPA)
                b.wait(S, "s", "TS", b.cnt["TS"] - 1)
                if o >= 1:
                    b.wait(S, "s", "VS", gev[o - 1][1])
                act(GSc[:, :], pg[:, 0:TPA], ACT.Sigmoid, bias=BIAS[:, 2, o:o + 1], sem="SS")
                b.wait(S, "s", "TS", b.cnt["TS"])
                act(FS[:, :], pf[:, 0:TPA], ACT.Tanh, bias=BIAS[:, 3, o:o + 1], sem="SS")
                b.wait(V, "v", "SS", b.cnt["SS"] - 1)
                vtt(ATS[:, :], GSc[:, :], HF[:, o, :], ALU.mult)
                vtt(XF[cur][:, o, :], XF[cur][:, o, :], ATS[:, :], ALU.add)
                b.wait(V, "v", "SS", b.cnt["SS"])
                vtt(XF[cur][:, o, :], XF[cur][:, o, :], FS[:, :], ALU.add, sem="VS")
                gev[o] = (b.cnt["SS"], b.cnt["VS"])
                ppump(3)
            gate_tok[i] = (b.cnt["SS"], b.cnt["VS"], b.cnt["TS"])
            ppump(100)
            b.wait(SY, "sy", "VS")
            toff = p_ * SEQ + ti * TPA
            st_toks[i] = stx(x1a[:, :, toff:toff + TPA], XF[cur][:, :, :])
        barrier()
        ar.close()

    def emit_seg_b():
        ar = Arena()
        WQ = r3(ar.w(KC * D), KC, D); WK = r3(ar.w(KC * D), KC, D)
        WV = r3(ar.w(KC * D), KC, D); WO = r3(ar.w(KC * D), KC, D)
        for (dstw, srcw) in ((WQ, wqt), (WK, wkt), (WV, wvt), (WO, wot)):
            ldw(dstw[:, :, :], srcw.rearrange("(c p) o -> p c o", p=128))

        X1 = [r3(ar.f32(KC * TPB), KC, TPB) for _ in range(2)]
        X1B = r3(ar.bf16(KC * TPB), KC, TPB)
        X1BL = r3(ar.bf16(KC * TPB), KC, TPB)
        SQ = r3(ar.bf16(KC * TPB), KC, TPB)
        SQL = r3(ar.bf16(KC * TPB), KC, TPB)
        SQF = ar.f32(TPB)
        VARR = ar.f32(TPB)
        XN1 = r3(ar.bf16(KC * TPB), KC, TPB)
        PQ = r3(ar.bf16(KC * TPB), KC, TPB)
        PK = r3(ar.bf16(KC * TPB), KC, TPB)
        PKT = [r3(ar.bf16(D), KC, 128) for _ in range(2)]
        VEX = [r3(ar.bf16(NH * HD), NH, HD) for _ in range(2)]
        OBF = r3(ar.bf16(KC * TPB), KC, TPB)
        AT0 = ar.bf16(2 * TPB)
        AT1 = ar.bf16(256)
        QM = ar.f32(TPB)
        EX = ar.f32(TPB)
        RBS = ar.f32(TPB)
        DENS = ar.f32(2 * TPB)
        SEXT = r3(ar.f32(2 * NH * HD), 2 * NH, HD)
        SZP = r3(ar.f32(2 * NH), 2 * NH, 1)
        SEB = r3(ar.bf16(2 * NH * HD), 2 * NH, HD)
        SZB = r3(ar.bf16(2 * NH), 2 * NH, 1)
        MRW = ar.f32(TPB)
        RRW = ar.f32(TPB)
        STR = ar.f32(TPB)
        U32 = ar.f32(KC * TPB)
        UH = ar.f32(KC * TPB)
        TH = ar.f32(KC * TPB)
        XN1F = r3(ar.f32(KC * TPB), KC, TPB)
        XQ2 = ar.bf16(KC * TPB)
        XQ = r3(XQ2, KC, TPB)
        CM = ar.f32(TPB)
        ABB = ar.f32(2 * TPB)
        AMT = ar.f32(4)

        zs2 = zeros_in.rearrange("p (h e) -> p h e", h=2 * NH)
        b.dma(G, "g", SEXT[:, :, :], zs2[:, :, :], LW)
        b.dma(G, "g", SZP.rearrange("p h e -> p (h e)"), zeros_in[:, 0:2 * NH], LW)
        b.dma(G, "g", SEB[:, :, :], zs2[:, :, :], LW)
        b.dma(G, "g", SZB.rearrange("p h e -> p (h e)"), zeros_in[:, 0:2 * NH], LW)
        b.wait_pool(T, "t", LW)
        b.wait_pool(T, "t", LS)
        b.wait_pool(T, "t", LX)
        b.wait_pool(S, "s", LW)
        b.wait_pool(V, "v", LW)

        tguard = {}
        xq_tok = {}
        x1b_tok = {}
        xnfree = {}
        sgen = [None]

        def stats_gen(ci_, cur_, t0_):
            if ci_ >= 2:
                b.wait(SY, "sy", "VS", tguard[ci_ - 2])
                b.wait_tok(SY, "sy", x1b_tok.get(ci_ - 2))
            ldx(X1[cur_][:, :, :], x1a[:, :, t0_:t0_ + TPB])
            yield
            b.wait_pool(V, "v", LX)
            for k in range(KC):
                vcopy(X1B[:, k, :], X1[cur_][:, k, :])
                vtt(X1BL[:, k, :], X1[cur_][:, k, :], X1B[:, k, :], ALU.subtract)
                if k % 2 == 1:
                    yield
            b.fence("v", "VS")
            b.wait(T, "t", "VS", b.cnt["VS"])
            for k in range(KC):
                mm(PB[3][0:1, 256:256 + TPB], ONB[:, 0:1], X1B[:, k, :],
                   k == 0, False)
                mm(PB[3][0:1, 256:256 + TPB], ONB[:, 0:1], X1BL[:, k, :],
                   False, k == KC - 1)
            b.fence("t", "TS")
            yield
            for k in range(KC):
                b.wait(S, "s", "VS", b.cnt["VS"])
                act(SQF[:, :], X1[cur_][:, k, :], ACT.Square)
                scopy(SQ[:, k, :], SQF[:, :], sem="SS")
                b.wait(V, "v", "SS", b.cnt["SS"])
                vtt(SQL[:, k, :], SQF[:, :], SQ[:, k, :], ALU.subtract, sem="VS")
                yield
            b.wait(T, "t", "SS", b.cnt["SS"])
            b.wait(T, "t", "VS", b.cnt["VS"])
            for k in range(KC):
                mm(PB[4][0:1, 256:256 + TPB], ONB[:, 0:1], SQ[:, k, :],
                   k == 0, False)
                mm(PB[4][0:1, 256:256 + TPB], ONB[:, 0:1], SQL[:, k, :],
                   False, k == KC - 1)
            b.fence("t", "TS")
            yield
            b.wait(V, "v", "TS", b.cnt["TS"])
            vts(MRW[0:1, :], PB[3][0:1, 256:256 + TPB], 1.0 / D, None, ALU.mult)
            vts(RRW[0:1, :], PB[4][0:1, 256:256 + TPB], 1.0 / D, None, ALU.mult)
            vtt(STR[0:1, :], MRW[0:1, :], MRW[0:1, :], ALU.mult)
            vstt(VARR[0:1, :], STR[0:1, :], -1.0, RRW[0:1, :], ALU.mult, ALU.add)
            vts(VARR[0:1, :], VARR[0:1, :], EPS_LN, None, ALU.add, sem="VS")
            yield
            b.wait(S, "s", "VS", b.cnt["VS"])
            act(RRW[0:1, :], VARR[0:1, :], ACT.Ln)
            act(RRW[0:1, :], RRW[0:1, :], ACT.Exp, scale=-0.5, sem="SS")
            b.wait(V, "v", "SS", b.cnt["SS"])
            for _ in range(2):
                vtt(STR[0:1, :], RRW[0:1, :], RRW[0:1, :], ALU.mult)
                vtt(STR[0:1, :], STR[0:1, :], VARR[0:1, :], ALU.mult)
                vts(STR[0:1, :], STR[0:1, :], -0.5, 1.5, ALU.mult, ALU.add)
                vtt(RRW[0:1, :], RRW[0:1, :], STR[0:1, :], ALU.mult)
            vtt(MRW[0:1, :], MRW[0:1, :], RRW[0:1, :], ALU.mult, sem="VS")
            yield
            b.wait(T, "t", "VS", b.cnt["VS"])
            mm(PB[5][:, 256:256 + TPB], ONF[0:1, 0:128], RRW[0:1, :], True, True)
            b.fence("t", "TS")
            mm(PB[4][:, 256:256 + TPB], ONF[0:1, 0:128], MRW[0:1, :], True, True)
            b.fence("t", "TS")
            bc_ts = b.cnt["TS"]
            yield
            b.wait(V, "v", "TS", bc_ts)
            if ci_ >= 1:
                b.wait(V, "v", "SS", xnfree[ci_ - 1][0])
            for k in range(KC):
                vtt(QM[:, :], X1[cur_][:, k, :], PB[5][:, 256:256 + TPB], ALU.mult)
                vtt(XN1F[:, k, :], QM[:, :], PB[4][:, 256:256 + TPB], ALU.subtract)
                if k % 2 == 1:
                    yield
            b.fence("v", "VS")
            xn_vs = b.cnt["VS"]
            b.wait(S, "s", "VS", xn_vs)
            if ci_ >= 1:
                b.wait(S, "s", "TS", xnfree[ci_ - 1][1])
            for k in range(KC):
                scopy(XN1[:, k, :], XN1F[:, k, :])
                if k % 2 == 1:
                    yield
            b.fence("s", "SS")
            yield

        def spump(n):
            g = sgen[0]
            if g is None:
                return
            try:
                for _ in range(n):
                    next(g)
            except StopIteration:
                sgen[0] = None

        for _ in stats_gen(0, 0, 0):
            pass
        for ci in range(2 * NTB):
            cur = ci % 2
            p_ = ci % 2
            t0 = p_ * SEQ + (ci // 2) * TPB
            spump(100)
            b.wait(T, "t", "SS")
            tile_vs0 = b.cnt["VS"]
            tile_ss0 = b.cnt["SS"]
            qev = {}
            qn = 0
            for (wmat, dst) in ((WQ, PQ), (WK, PK)):
                for o in range(KC):
                    pp = PB[qn % 4]
                    if qn >= 4:
                        b.wait(T, "t", "VS", qev[qn - 4])
                    else:
                        b.wait(T, "t", "VS", tile_vs0)
                        b.wait(T, "t", "SS", tile_ss0)
                        if qn == 0:
                            b.wait(S, "s", "VS", b.cnt["VS"])
                    for k in range(KC):
                        mm(pp[:, 0:TPB], wmat[:, k, o * 128:(o + 1) * 128], XN1[:, k, :],
                           k == 0, k == KC - 1)
                    b.fence("t", "TS")
                    b.wait(S, "s", "TS", b.cnt["TS"])
                    if qn >= 1:
                        b.wait(S, "s", "VS", qev[qn - 1])
                    act(QM[:, :], pp[:, 0:TPB], ACT.Relu, scale=-1.0)
                    act(EX[:, :], QM[:, :], ACT.Exp, scale=-1.0, sem="SS")
                    b.wait(V, "v", "SS", b.cnt["SS"])
                    vstt(dst[:, o, :], pp[:, 0:TPB], 0.0, EX[:, :], ALU.max, ALU.add,
                         sem="VS")
                    qev[qn] = b.cnt["VS"]
                    qn += 1
            vev = {}
            vn = 0
            for sbi in range(TPB // 128):
                for nh2 in range(2):
                    pp = PB[vn % 4]
                    if vn == 0:
                        b.wait(T, "t", "VS", qev[qn - 4 + 0])
                        b.wait(T, "t", "VS", qev[qn - 4 + 1])
                        b.wait(T, "t", "VS", qev[qn - 4 + 2])
                        b.wait(T, "t", "VS", qev[qn - 4 + 3])
                    for k in range(KC):
                        mm(pp[:, 0:512], XN1[:, k, sbi * 128:(sbi + 1) * 128],
                           WV[:, k, nh2 * 512:(nh2 + 1) * 512], k == 0, k == KC - 1)
                    b.fence("t", "TS")
                    b.wait(S, "s", "TS", b.cnt["TS"])
                    scopy(VEX[sbi][:, nh2 * 8:(nh2 + 1) * 8, :],
                          pp[:, 0:512].rearrange("p (h e) -> p h e", h=8), sem="SS")
                    vev[vn] = b.cnt["SS"]
                    vn += 1
            for j in range(TPB // 128):
                for k in range(KC):
                    b.wait(T, "t", "VS")
                    tr(PT[k % 2][0:128, 0:128], PK[:, k, j * 128:(j + 1) * 128], IDB[:, :])
                    b.fence("t", "TS")
                    b.wait(S, "s", "TS")
                    scopy(PKT[j][:, k, :], PT[k % 2][0:128, 0:128], sem="SS")
                    b.wait(T, "t", "SS", b.cnt["SS"] - 1)
            b.wait(S, "s", "VS")
            act(CM[:, :], XN1F[:, 0, :], ACT.Abs)
            vg = {}
            for k in range(1, KC):
                if k >= 3:
                    b.wait(S, "s", "VS", vg[k - 2])
                act(ABB[:, (k % 2) * TPB:(k % 2 + 1) * TPB], XN1F[:, k, :], ACT.Abs, sem="SS")
                b.wait(V, "v", "SS")
                vtt(CM[:, :], CM[:, :], ABB[:, (k % 2) * TPB:(k % 2 + 1) * TPB], ALU.max, sem="VS")
                vg[k] = b.cnt["VS"]
            for sbi in range(TPB // 128):
                b.wait(T, "t", "VS")
                tr(PB[4][0:128, 0:128], CM[:, sbi * 128:(sbi + 1) * 128], IDF[:, :])
                b.fence("t", "TS")
                b.wait(V, "v", "TS")
                vred(AMT[:, sbi:sbi + 1], PB[4][0:128, 0:128], ALU.max)
                vcopy(AMX[:, t0 // 128 + sbi:t0 // 128 + sbi + 1], AMT[:, sbi:sbi + 1])
                b.pre("v", [AMT[:, sbi:sbi + 1]], [AMT[:, 2:3]])
                vop(V.reciprocal(AMT[:, 2:3], AMT[:, sbi:sbi + 1]))
                vts(AMT[:, 3:4], AMT[:, 2:3], 127.0, None, ALU.mult, sem="VS")
                b.wait(T, "t", "VS")
                mm(PB[5][0:1, 0:128], AMT[:, 3:4], IDF[:, 0:128], True, True, sem="TS")
                b.wait(V, "v", "TS")
                vcopy(RBS[0:1, sbi * 128:(sbi + 1) * 128], PB[5][0:1, 0:128], sem="VS")
            b.wait(T, "t", "VS")
            mm(PB[5][:, 0:TPB], ONF[0:1, 0:128], RBS[0:1, 0:TPB], True, True, sem="TS")
            b.wait(V, "v", "TS")
            b.wait_tok(V, "v", xq_tok.get(ci - 1))
            for k in range(KC):
                vtt(U32[:, k * TPB:(k + 1) * TPB], XN1F[:, k, :], PB[5][:, 0:TPB],
                    ALU.mult)
            vts(U32[:, :], U32[:, :], 128.5, None, ALU.add, sem="VS")

            def xq_gen(ci_, t0_):
                for _ in round_sv_steps(XQ2[:, :], U32[:, :], UH[:, :], TH[:, :]):
                    yield
                b.fence("v", "VS")
                b.wait(SY, "sy", "VS")
                xq_tok[ci_] = stx(xqs[:, :, t0_:t0_ + TPB], XQ[:, :, :])
                yield
            xpend = [xq_gen(ci, t0)]

            def xpump(n):
                g = xpend[0]
                if g is None:
                    return
                try:
                    for _ in range(n):
                        next(g)
                except StopIteration:
                    xpend[0] = None
            xnfree[ci] = (b.cnt["SS"], b.cnt["TS"])
            if ci + 1 < 2 * NTB:
                nci = ci + 1
                sgen[0] = stats_gen(nci, nci % 2, (nci % 2) * SEQ + (nci // 2) * TPB)
            b.wait(S, "s", "VS")
            scopy(SEB[:, :, :], SEXT[:, :, :])
            scopy(SZB[:, :, :], SZP[:, :, :], sem="SS")
            b.wait(T, "t", "SS")
            pair_tok = {}
            att_vs0 = b.cnt["VS"]
            att_ss0 = b.cnt["SS"]
            for pr in range(KC):
                hA, hB = 2 * pr, 2 * pr + 1
                sA, sB = p_ * NH + hA, p_ * NH + hB
                if pr >= 1:
                    b.wait(T, "t", "VS", pair_tok[pr - 1][0])
                    b.wait(T, "t", "SS", pair_tok[pr - 1][1])
                else:
                    b.wait(T, "t", "VS", att_vs0)
                    b.wait(T, "t", "SS", att_ss0)
                for bas, apb in ((0, PB[0]), (64, PB[1])):
                    mm(apb[0:128, 0:TPB], PK[bas:bas + 64, pr, 0:128],
                       PQ[bas:bas + 64, pr, 0:TPB], True, True)
                    b.fence("t", "TS")
                    mm(apb[0:128, TPB:TPB + 128], PK[bas:bas + 64, pr, 128:256],
                       PQ[bas:bas + 64, pr, 128:256], True, True)
                    b.fence("t", "TS")
                spb = PB[5]
                for j in range(2):
                    mm(spb[:, 0:128], PKT[j][:, pr, :],
                       VEX[j][:, hA:hB + 1, :].rearrange("p h e -> p (h e)"),
                       j == 0, j == 1)
                b.fence("t", "TS")
                for j in range(2):
                    mm(spb[:, 128:129], PKT[j][:, pr, :], ONB[:, 0:1], j == 0, j == 1)
                b.fence("t", "TS")
                b.wait(V, "v", "TS", b.cnt["TS"])
                for off, apb in ((0, PB[0]), (1, PB[1])):
                    vtt(AT0[0:128, off * TPB:off * TPB + 128], apb[0:128, 0:128],
                        MSK[:, :], ALU.mult)
                    vcopy(AT0[0:128, off * TPB + 128:off * TPB + TPB],
                          apb[0:128, 128:256])
                    vtt(AT1[0:128, off * 128:(off + 1) * 128],
                        apb[0:128, TPB:TPB + 128], MSK[:, :], ALU.mult)
                b.fence("v", "VS")
                vtt(SEXT[0:64, sA, :], SEXT[0:64, sA, :], spb[0:64, 0:64], ALU.add)
                vtt(SEXT[64:128, sB, :], SEXT[64:128, sB, :], spb[64:128, 64:128], ALU.add)
                vtt(SZP[0:64, sA, :], SZP[0:64, sA, :], spb[0:64, 128:129], ALU.add)
                vtt(SZP[64:128, sB, :], SZP[64:128, sB, :], spb[64:128, 128:129],
                    ALU.add, sem="VS")
                xpump(2)
                spump(3)
                mask_vs = b.cnt["VS"]
                b.wait(T, "t", "VS", mask_vs)
                for off, hh in ((0, sA), (1, sB)):
                    co = off * TPB
                    mm(PB[2][0:1, co:co + TPB], ONB[:, 0:1],
                       AT0[0:128, co:co + TPB], True, False)
                    mm(PB[2][0:1, co + 128:co + TPB], ONB[:, 0:1],
                       AT1[0:128, off * 128:(off + 1) * 128], False, False)
                    mm(PB[2][0:1, co:co + TPB], SZB[:, hh, :], PQ[:, pr, :],
                       False, True)
                    b.fence("t", "TS")
                b.wait(S, "s", "TS", b.cnt["TS"])
                act(DENS[0:1, 0:TPB], PB[2][0:1, 0:TPB], ACT.Identity,
                    bias=CVS[0:1, 1:2])
                act(DENS[0:1, TPB:2 * TPB], PB[2][0:1, TPB:2 * TPB], ACT.Identity,
                    bias=CVS[0:1, 1:2], sem="SS")
                b.wait(T, "t", "SS", b.cnt["SS"])
                opb = PB[4]
                for off, hh, sh, bas in ((0, hA, sA, 0), (1, hB, sB, 64)):
                    co = off * TPB
                    mm(PB[3][bas:bas + 64, 0:TPB], ONF[0:1, 0:64],
                       DENS[0:1, co:co + TPB], True, True, tile_position=(0, bas))
                    b.fence("t", "TS")
                    mm(opb[bas:bas + 64, 0:TPB], SEB[:, sh, :], PQ[:, pr, :],
                       True, False, tile_position=(0, bas))
                    mm(opb[bas:bas + 64, 0:TPB], VEX[0][:, hh, :],
                       AT0[0:128, co:co + TPB], False, False, tile_position=(0, bas))
                    mm(opb[bas:bas + 64, 128:TPB], VEX[1][:, hh, :],
                       AT1[0:128, off * 128:(off + 1) * 128], False, True,
                       tile_position=(0, bas))
                    b.fence("t", "TS")
                b.wait(V, "v", "TS", b.cnt["TS"])
                b.pre("v", [PB[3][:, 0:TPB]], [RBS[:, 0:TPB]])
                vop(V.reciprocal(RBS[:, 0:TPB], PB[3][:, 0:TPB]), "VS")
                vtt(OBF[:, pr, :], PB[4][:, 0:TPB], RBS[:, 0:TPB], ALU.mult, sem="VS")
                pair_tok[pr] = (b.cnt["VS"], b.cnt["SS"])
                xpump(1)
                spump(1)
            att_vs = b.cnt["VS"]
            att_ss2 = b.cnt["SS"]
            wev = {}
            for o in range(KC):
                pp = PB[o % 4]
                if o >= 4:
                    b.wait(T, "t", "VS", wev[o - 4])
                else:
                    b.wait(T, "t", "VS", att_vs)
                    b.wait(T, "t", "SS", att_ss2)
                for k in range(KC):
                    mm(pp[:, 0:TPB], WO[:, k, o * 128:(o + 1) * 128], OBF[:, k, :],
                       k == 0, k == KC - 1)
                b.fence("t", "TS")
                b.wait(V, "v", "TS", b.cnt["TS"])
                vstt(X1[cur][:, o, :], pp[:, 0:TPB], BIAS[:, 4, o:o + 1],
                     X1[cur][:, o, :], ALU.add, ALU.add, sem="VS")
                wev[o] = b.cnt["VS"]
                xpump(2)
                spump(3)
            xpump(50)
            spump(100)
            tguard[ci] = b.cnt["VS"]
            b.wait(SY, "sy", "VS")
            x1b_tok[ci] = stx(x1b[:, :, t0:t0 + TPB], X1[cur][:, :, :])
        barrier()
        ar.close()

    def emit_seg_c1():
        ar = Arena()
        WGA = r3(ar.w(KC * INNER), KC, INNER)
        WUP = r3(ar.w(KC * INNER), KC, INNER)
        wg4 = r3(wqg, KC, INNER)
        wu4 = r3(wqu, KC, INNER)
        ldq(1, WGA[:, 0:KC // 2, :], wg4[:, 0:KC // 2, :])
        ldq(2, WGA[:, KC // 2:KC, :], wg4[:, KC // 2:KC, :])
        ldq(1, WUP[:, 0:KC // 2, :], wu4[:, 0:KC // 2, :])
        ldq(2, WUP[:, KC // 2:KC, :], wu4[:, KC // 2:KC, :])

        NSUB = TPC // 128
        XQ = [r3(ar.bf16(KC * TPC), KC, TPC) for _ in range(2)]
        GSI = ar.bf16(1024)
        YB = [ar.f32(INNER), ar.f32(INNER)]
        b.dbg = getattr(b, "dbg", {})
        b.dbg["YB1"] = YB[0].tensor.name
        U32 = ar.f32(INNER // 2)
        UH = ar.f32(INNER // 2)
        TH = ar.bf16(INNER // 2)
        YQ = ar.bf16(INNER // 2)
        USG = ar.f32(12)
        yq3 = yqs.rearrange("(s p) i -> p s i", p=128)
        b.wait_pool(T, "t", LW)
        b.wait_pool(T, "t", LS)

        tguard = {}
        yq_tok = {}
        pend = [None]
        evac = {}
        ogg = [0]

        def quant_gen(qb, srow):
            c0 = 6 * qb
            vred(USG[:, c0 + 3:c0 + 4], YB[qb][:, :], ALU.max, absval=True)
            vcopy(AMY[:, srow:srow + 1], USG[:, c0 + 3:c0 + 4])
            b.pre("v", [USG[:, c0 + 3:c0 + 4]], [USG[:, c0 + 5:c0 + 6]])
            vop(V.reciprocal(USG[:, c0 + 5:c0 + 6], USG[:, c0 + 3:c0 + 4]))
            vts(USG[:, c0 + 4:c0 + 5], USG[:, c0 + 5:c0 + 6], 127.0, None,
                ALU.mult, sem="VS")
            yield
            for hf in range(2):
                sl = slice(hf * (INNER // 2), (hf + 1) * (INNER // 2))
                b.wait_tok(V, "v", yq_tok.get("last"))
                vts(U32[:, :], YB[qb][:, sl], USG[:, c0 + 4:c0 + 5], 128.5,
                    ALU.mult, ALU.add, sem="VS")
                yield
                for _ in round_sv_steps(YQ[:, :], U32[:, :], UH[:, :], TH[:, :]):
                    yield
                b.fence("v", "VS")
                b.wait(SY, "sy", "VS")
                yq_tok["last"] = stx(yq3[:, srow, sl], YQ[:, :])
                yield

        def pump(n):
            g = pend[0]
            if g is None:
                return
            try:
                for _ in range(n):
                    next(g)
            except StopIteration:
                pend[0] = None

        for ti in range(2 * NTC):
            cur = ti % 2
            t0 = (ti % 2) * SEQ + (ti // 2) * TPC
            if ti >= 2:
                b.wait(SY, "sy", "TS", tguard[ti - 2])
            ldx(XQ[cur][:, :, :], xqs[:, :, t0:t0 + TPC])
            b.wait_pool(T, "t", LX)
            for sbi in range(NSUB):
                srow = t0 // 128 + sbi
                qb = (ti * NSUB + sbi) % 2
                c0 = 6 * qb
                vts(USG[:, c0:c0 + 1], AMX[:, srow:srow + 1], 1.0 / 127.0,
                    None, ALU.mult)
                vtt(USG[:, c0 + 1:c0 + 2], USG[:, c0:c0 + 1], GLB[:, 0:1], ALU.mult)
                vtt(USG[:, c0 + 2:c0 + 3], USG[:, c0:c0 + 1], GLB[:, 1:2],
                    ALU.mult, sem="VS")
                scl_tok = b.cnt["VS"]
                for og in range(8):
                    obase = og * 512
                    g_ = ogg[0]
                    pg_ = PB[(g_ % 3) * 2]
                    pu_ = PB[(g_ % 3) * 2 + 1]
                    if g_ >= 3:
                        b.wait(T, "t", "SS", evac[g_ - 3][0])
                        b.wait(T, "t", "VS", evac[g_ - 3][1])
                    for k in range(KC):
                        mm(pg_[:, 0:512], XQ[cur][:, k, sbi * 128:(sbi + 1) * 128],
                           WGA[:, k, obase:obase + 512], k == 0, k == KC - 1)
                    b.fence("t", "TS")
                    for k in range(KC):
                        mm(pu_[:, 0:512], XQ[cur][:, k, sbi * 128:(sbi + 1) * 128],
                           WUP[:, k, obase:obase + 512], k == 0, k == KC - 1)
                    b.fence("t", "TS")
                    b.wait(S, "s", "TS", b.cnt["TS"] - 1)
                    b.wait(S, "s", "VS", scl_tok)
                    if g_ >= 2:
                        b.wait(S, "s", "VS", evac[g_ - 2][1])
                    act(GSI[:, (og % 2) * 512:(og % 2 + 1) * 512], pg_[:, 0:512],
                        ACT.Silu, scale=USG[:, c0 + 1:c0 + 2], sem="SS")
                    b.wait(V, "v", "SS", b.cnt["SS"])
                    b.wait(V, "v", "TS", b.cnt["TS"])
                    vstt(YB[qb][:, obase:obase + 512], pu_[:, 0:512],
                         USG[:, c0 + 2:c0 + 3],
                         GSI[:, (og % 2) * 512:(og % 2 + 1) * 512], ALU.mult, ALU.mult,
                         sem="VS")
                    evac[g_] = (b.cnt["SS"], b.cnt["VS"])
                    ogg[0] += 1
                    pump(5)
                pump(50)
                pend[0] = quant_gen(qb, srow)
                pump(2)
            tguard[ti] = b.cnt["TS"]
        pump(50)
        barrier()
        ar.close()

    def emit_seg_c2():
        ar = Arena()
        WD = r3(ar.w(IKC * D), IKC, D)
        wd4 = r3(wqd, IKC, D)
        ldq(1, WD[:, 0:IKC // 2, :], wd4[:, 0:IKC // 2, :])
        ldq(2, WD[:, IKC // 2:IKC, :], wd4[:, IKC // 2:IKC, :])
        YQF = [r3(ar.bf16(IKC * TPC), IKC, TPC) for _ in range(2)]
        X1T = [r3(ar.f32(KC * TPC), KC, TPC) for _ in range(2)]
        RB = ar.f32(TPC)
        RROW = ar.f32(TPC)
        OF = r3(ar.f32(KC * TPC), KC, TPC)
        b.dbg = getattr(b, "dbg", {})
        b.dbg["OF1"] = OF.tensor.name
        b.dbg["X1T1"] = X1T[1].tensor.name
        b.dbg["RB1"] = RB.tensor.name
        OT = [ar.f32(D) for _ in range(TPC // 128)]
        USD = ar.f32(2)
        out3 = out_hbm.rearrange("(s p) d -> p s d", p=128)
        b.wait_pool(T, "t", LW)
        b.wait_pool(T, "t", LS)

        tguard = {}
        ot_tok = {}
        for ti in range(2 * NTC):
            cur = ti % 2
            t0 = (ti % 2) * SEQ + (ti // 2) * TPC
            if ti >= 2:
                b.wait(SY, "sy", "TS", tguard[ti - 2][0])
                b.wait(SY, "sy", "VS", tguard[ti - 2][1])
            for k in range(IKC):
                b.dma(SY, "sy", YQF[cur][:, k, :],
                      yqs[t0:t0 + TPC, k * 128:(k + 1) * 128], LX, transpose=True)
            ldx(X1T[cur][:, :, :], x1b[:, :, t0:t0 + TPC])
            NSUB = TPC // 128
            for sbi in range(NSUB):
                srow = t0 // 128 + sbi
                b.wait(V, "v", "TS")
                b.wait(V, "v", "SS")
                vts(USD[:, 0:1], AMY[:, srow:srow + 1], 1.0 / 127.0, None, ALU.mult)
                vtt(USD[:, 1:2], USD[:, 0:1], GLB[:, 2:3], ALU.mult, sem="VS")
                b.wait(T, "t", "VS")
                mm(PB[4][0:1, 0:128], USD[:, 1:2], IDF[:, 0:128], True, True, sem="TS")
                b.wait(V, "v", "TS")
                vcopy(RROW[0:1, sbi * 128:(sbi + 1) * 128], PB[4][0:1, 0:128], sem="VS")
            b.wait(T, "t", "VS")
            mm(PB[4][:, 0:TPC], ONF[0:1, 0:128], RROW[0:1, 0:TPC], True, True, sem="TS")
            b.wait(S, "s", "TS")
            scopy(RB[:, :], PB[4][:, 0:TPC], sem="SS")
            b.wait_pool(T, "t", LX)
            b.wait_pool(T, "t", LW)
            b.wait_pool(T, "t", LS)
            c2vs0 = b.cnt["VS"]
            oev = {}
            for o in range(KC):
                pp = PB[0] if o % 2 == 0 else PB[1]
                if o >= 2:
                    b.wait(T, "t", "VS", oev[o - 2])
                else:
                    b.wait(T, "t", "VS", c2vs0)
                for k in range(IKC):
                    mm(pp[:, 0:TPC], WD[:, k, o * 128:(o + 1) * 128], YQF[cur][:, k, :],
                       k == 0, k == IKC - 1)
                b.fence("t", "TS")
                b.wait(V, "v", "TS", b.cnt["TS"])
                b.wait(V, "v", "SS", b.cnt["SS"])
                vtt(OF[:, o, :], pp[:, 0:TPC], RB[:, :], ALU.mult)
                vtt(OF[:, o, :], OF[:, o, :], X1T[cur][:, o, :], ALU.add, sem="VS")
                oev[o] = b.cnt["VS"]
            tg_ts = b.cnt["TS"]
            b.wait(T, "t", "SS")
            for sbi in range(TPC // 128):
                if ti >= 1:
                    b.wait_tok(V, "v", ot_tok.get((ti - 1, sbi)))
                for k in range(KC):
                    pbx = PB[4 + (k % 2)]
                    b.wait(T, "t", "VS")
                    tr(pbx[0:128, 0:128], OF[:, k, sbi * 128:(sbi + 1) * 128], IDF[:, :])
                    b.fence("t", "TS")
                    b.wait(V, "v", "TS")
                    vcopy(OT[sbi][:, k * 128:(k + 1) * 128], pbx[0:128, 0:128], sem="VS")
                b.wait(SY, "sy", "VS")
                ot_tok[(ti, sbi)] = stx(out3[:, t0 // 128 + sbi, :],
                                        OT[sbi][:, :])
            tguard[ti] = (tg_ts, b.cnt["VS"])
        barrier()
        ar.close()

    b.dma(G, "g", DUM[:, :], zeros_in[:, 0:2], LW)
    b.pre("g", (), [DUM[:, :]])
    b.op("g", G.engine_nop(), "GS")
    ldx(BIAS[:, :, :], bias_in)
    ldx(IDF[:, :], ident_f)
    ldx(IDB[:, :], ident_b)
    ldx(MSK[:, :], maskd)
    ldx(ONB[:, :], ones_b)
    ldx(ONF[:, :], ones_f)
    ldx(CVS[:, :], cvals)
    for eng, en in ((T, "t"), (V, "v"), (S, "s"), (G, "g")):
        b.wait_pool(eng, en, LX)

    emit_wq_prep()
    emit_w0_prep()
    emit_seg_a()
    emit_seg_b()
    emit_seg_c1()
    emit_seg_c2()

    b.wait_pool(SY, "sy", STP)
    b.stack.close()
    return b


_cached = {}


def _build():
    if "b" not in _cached:
        _cached["b"] = build_program()
    return _cached["b"]


def kernel(L0_attn_x, L0_attn_Wz, L0_attn_bz, L0_attn_Wh, L0_attn_bh, L0_attn_Wg,
           L0_attn_bg, L0_ffn_weight, L0_ffn_aux_weight1, L0_ffn_aux_weight2,
           L0_ffn_bias, L1_attn_Wq, L1_attn_Wk, L1_attn_Wv, L1_attn_Wo, L1_attn_bo,
           L1_ffn_w_gate, L1_ffn_w_up, L1_ffn_w_down):
    x = np.ascontiguousarray(np.asarray(L0_attn_x, np.float32))

    def t(a):
        return np.ascontiguousarray(np.asarray(a, np.float32).T)

    def c(a):
        return np.ascontiguousarray(np.asarray(a, np.float32))

    wzt_h = np.ascontiguousarray(
        np.transpose(np.asarray(L0_attn_Wz, np.float32), (1, 0, 2)).reshape(D, D))
    wht_h = np.ascontiguousarray(
        np.transpose(np.asarray(L0_attn_Wh, np.float32), (1, 0, 2)).reshape(D, D))
    biases = np.zeros((128, 5, KC), np.float32)
    biases[:, 0, :] = np.asarray(L0_attn_bz, np.float32).reshape(KC, 128).T
    biases[:, 1, :] = np.asarray(L0_attn_bh, np.float32).reshape(KC, 128).T
    biases[:, 2, :] = np.asarray(L0_attn_bg, np.float32).reshape(KC, 128).T
    biases[:, 3, :] = np.asarray(L0_ffn_bias, np.float32).reshape(KC, 128).T
    biases[:, 4, :] = np.asarray(L1_attn_bo, np.float32).reshape(KC, 128).T

    common = dict(
        wzt=wzt_h, wht=wht_h, wg0=c(L0_attn_Wg), w0pt=t(L0_ffn_weight),
        a10=c(L0_ffn_aux_weight1), a20=c(L0_ffn_aux_weight2),
        wqt=t(L1_attn_Wq), wkt=t(L1_attn_Wk), wvt=t(L1_attn_Wv), wot=t(L1_attn_Wo),
        wgr=t(L1_ffn_w_gate), wur=t(L1_ffn_w_up), wdr=t(L1_ffn_w_down),
        biases=biases,
        ident_f=np.eye(128, dtype=np.float32),
        ident_b=np.eye(128, dtype=bfdt),
        maskd=np.ascontiguousarray(np.tril(np.ones((128, 128), np.float32)).T).astype(bfdt),
        ones_b=np.ones((128, 128), dtype=bfdt),
        ones_f=np.ones((128, 128), dtype=np.float32),
        cvals=np.tile(np.array([EPS_LN, EPS_ATT, -128.0, 0, 0, 0, 0, 0], np.float32), (128, 1)),
        zeros=np.zeros((128, 2 * NH * HD), np.float32),
    )

    in_maps = []
    for ci in range(NCORES):
        m = dict(common)
        m["x"] = np.ascontiguousarray(x[2 * ci:2 * ci + 2].reshape(2 * SEQ, D))
        in_maps.append(m)

    bb = _build()
    res = run_bass_kernel_spmd(bb.nc, in_maps, core_ids=list(range(NCORES)))
    out = np.empty((16, SEQ, D), np.float32)
    for ci in range(NCORES):
        out[2 * ci:2 * ci + 2] = np.asarray(res.results[ci]["out"]).reshape(2, SEQ, D)
    return out


# version 30 (final)
# speedup vs baseline: 1.0258x; 1.0102x over previous
import numpy as np
import ml_dtypes
from contextlib import ExitStack

import concourse.bass as bass
import concourse.mybir as mybir
from concourse.bass_utils import run_bass_kernel_spmd

F32 = mybir.dt.float32
BF16 = mybir.dt.bfloat16
FP16 = mybir.dt.float16
ALU = mybir.AluOpType
ACT = mybir.ActivationFunctionType
AX = mybir.AxisListType.X

NCORES = 8
SEQ = 2048
D = 1024
KC = D // 128
NH = 16
HD = 64
INNER = 4096
IKC = INNER // 128

TPA = 256
TPB = 256
TPC = 512
NTA = SEQ // TPA
NTB = SEQ // TPB
NTC = SEQ // TPC

EPS_LN = 1e-5
EPS_ATT = 1e-5
MAGIC = float(np.float32(12582912.0))

ROUND_LADDER = True

bfdt = ml_dtypes.bfloat16


class B:
    def __init__(self):
        self.nc = bass.Bass("TRN2", target_bir_lowering=False)
        self.stack = ExitStack()
        self.sems = {}
        self.cnt = {}
        self.waited = {}
        self.last = {}
        self.lastinc = {}
        self.mkdummy = {}
        self.pools = {}
        self.dirty = {}
        self.engines = {}

    def sem(self, name):
        h = self.stack.enter_context(self.nc.semaphore(name))
        self.sems[name] = h
        self.cnt[name] = 0
        return h

    def pool(self, prefix, n):
        names = []
        for i in range(n):
            self.sem(f"{prefix}{i}")
            names.append(f"{prefix}{i}")
        self.pools[prefix] = {"names": names, "idx": 0}
        return prefix

    def dma(self, qeng, qname, dst, src, prefix, transpose=False):
        p = self.pools[prefix]
        sname = p["names"][p["idx"]]
        p["idx"] = (p["idx"] + 1) % len(p["names"])
        self.wait(qeng, qname, sname)
        if transpose:
            inst = qeng.dma_start_transpose(dst, src)
        else:
            inst = qeng.dma_start(dst, src)
        self.inc(inst, sname, 16)
        return (sname, self.cnt[sname])

    def wait_tok(self, eng, engname, tok):
        if tok is not None:
            self.wait(eng, engname, tok[0], tok[1])

    def wait_pool(self, eng, engname, prefix):
        for sname in self.pools[prefix]["names"]:
            self.wait(eng, engname, sname)

    def sbuf(self, name, shape, dtype):
        return self.stack.enter_context(self.nc.sbuf_tensor(name, shape, dtype))

    def psum(self, name, shape, dtype):
        return self.stack.enter_context(self.nc.psum_tensor(name, shape, dtype))

    def inc(self, inst, sem, k=1):
        inst.then_inc(self.sems[sem], k)
        self.cnt[sem] += k
        self.lastinc[inst] = sem

    def wait(self, eng, engname, sem, val=None):
        if val is None:
            val = self.cnt[sem]
        if val <= 0:
            return
        val = min(val, self.cnt[sem])
        key = (engname, sem)
        if self.waited.get(key, -1) >= val:
            return
        eng.wait_ge(self.sems[sem], val)
        self.waited[key] = val

    def op(self, engname, inst, sem=None, k=1):
        self.last[engname] = inst
        if sem is not None:
            self.inc(inst, sem, k)
        return inst

    @staticmethod
    def _span(ap):
        try:
            dims = ap.ap
            lo = ap.offset
            hi = lo + 1
            for (stride, size) in dims[1:]:
                hi += abs(stride) * (size - 1)
            pb = ap.base_partition
            return (ap.tensor.name, pb, pb + dims[0][1], lo, hi)
        except Exception:
            return (ap.tensor.name, 0, 128, 0, 1 << 40)

    @staticmethod
    def _overlap(a, b):
        return (a[0] == b[0] and a[1] < b[2] and b[1] < a[2]
                and a[3] < b[4] and b[3] < a[4])

    def pre(self, engname, reads=(), writes=()):
        st = self.dirty.get(engname)
        if not isinstance(st, tuple):
            self.dirty[engname] = st = ([], [])
        tw, trd = st
        rs = [self._span(ap) for ap in reads]
        ws = [self._span(ap) for ap in writes]
        hazard = (tw or trd) and (
            (not rs and not ws)
            or any(self._overlap(t, s) for t in tw for s in rs + ws)
            or any(self._overlap(t, s) for t in trd for s in ws))
        if hazard:
            self.engines[engname].drain()
            del tw[:]
            del trd[:]
        tw.extend(ws)
        trd.extend(rs)

    def fence(self, engname, sem):
        inst = self.last.get(engname)
        assert inst is not None, f"fence on idle engine {engname}"
        have = self.lastinc.get(inst)
        if have is None:
            self.inc(inst, sem, 1)
            return
        if have == sem:
            return
        inst2 = self.mkdummy[engname]()
        self.last[engname] = inst2
        self.inc(inst2, sem, 1)


def build_program():
    b = B()
    nc = b.nc
    T, V, S, G, SY = nc.tensor, nc.vector, nc.scalar, nc.gpsimd, nc.sync

    def din(name, shape, dtype=F32):
        return nc.dram_tensor(name, shape, dtype, kind="ExternalInput").ap()

    def dint(name, shape, dtype):
        return nc.dram_tensor(name, shape, dtype, kind="Internal").ap()

    x_in = din("x", [2 * SEQ, D])
    out_hbm = nc.dram_tensor("out", [2 * SEQ, D], F32, kind="ExternalOutput").ap()

    wzt = din("wzt", [D, D]); wht = din("wht", [D, D]); wg0 = din("wg0", [D, D])
    w0pt = din("w0pt", [D, D]); a10 = din("a10", [D, D]); a20 = din("a20", [D, D])
    wqt = din("wqt", [D, D]); wkt = din("wkt", [D, D])
    wvt = din("wvt", [D, D]); wot = din("wot", [D, D])
    wgr = din("wgr", [D, INNER]); wur = din("wur", [D, INNER]); wdr = din("wdr", [INNER, D])
    bias_in = din("biases", [128, 5, KC])
    ident_f = din("ident_f", [128, 128], F32)
    ident_b = din("ident_b", [128, 128], BF16)
    maskd = din("maskd", [128, 128], BF16)
    ones_b = din("ones_b", [128, 128], BF16)
    ones_f = din("ones_f", [128, 128], F32)
    cvals = din("cvals", [128, 8], F32)
    zeros_in = din("zeros", [128, 2 * NH * HD], F32)

    x1a = dint("x1a", [128, KC, 2 * SEQ], F32)
    x1b = dint("x1b", [128, KC, 2 * SEQ], F32)
    xqs = dint("xqs", [128, KC, 2 * SEQ], BF16)
    yqs = dint("yqs", [2 * SEQ, INNER], BF16)
    wqg = dint("wqg", [128, KC * INNER], BF16)
    wqu = dint("wqu", [128, KC * INNER], BF16)
    wqd = dint("wqd", [128, IKC * D], BF16)
    wzh = dint("wzh", [128, KC * D], BF16); wzl = dint("wzl", [128, KC * D], BF16)
    whh = dint("whh", [128, KC * D], BF16); whl = dint("whl", [128, KC * D], BF16)
    wgh = dint("wgh", [128, KC * D], BF16); wgl = dint("wgl", [128, KC * D], BF16)
    wch = dint("wch", [128, KC * D], BF16); wcl = dint("wcl", [128, KC * D], BF16)

    for sname in ["TS", "VS", "SS", "GS"]:
        b.sem(sname)
    LW = b.pool("LW", 4)
    LX = b.pool("LX", 6)
    STP = b.pool("ST", 6)
    LS = b.pool("LS", 3)
    LVQ = b.pool("LV", 3)

    GLB = b.sbuf("GLB", [128, 16], F32)
    BIAS = b.sbuf("BIAS", [128, 5, KC], F32)
    BZN = b.sbuf("BZN", [128, KC], F32)
    IDF = b.sbuf("IDF", [128, 128], F32)
    IDB = b.sbuf("IDB", [128, 128], BF16)
    MSK = b.sbuf("MSK", [128, 128], BF16)
    ONB = b.sbuf("ONB", [128, 128], BF16)
    ONF = b.sbuf("ONF", [128, 128], F32)
    CVS = b.sbuf("CVS", [128, 8], F32)
    AMX = b.sbuf("AMX", [128, 2 * SEQ // 128], F32)
    AMY = b.sbuf("AMY", [128, 2 * SEQ // 128], F32)
    SST = b.sbuf("SST", [128, 2 * KC], F32)

    PB = [b.psum(f"pb{i}", [128, 512], F32) for i in range(6)]
    PT = [b.psum(f"pt{i}", [128, 1024], BF16) for i in range(2)]
    DUM = b.sbuf("DUM", [128, 2], F32)
    b.engines = {"t": T, "v": V, "s": S, "g": G}

    def _dummy(en, mk):
        def f():
            b.engines[en].drain()
            d = b.dirty.get(en)
            if isinstance(d, tuple):
                del d[0][:]
                del d[1][:]
            else:
                b.dirty[en] = ([], [])
            return mk()
        return f
    b.mkdummy = {
        "t": _dummy("t", lambda: T.nop()),
        "v": _dummy("v", lambda: V.engine_nop()),
        "g": _dummy("g", lambda: G.engine_nop()),
        "s": _dummy("s", lambda: S.copy(DUM[0:1, 0:1], DUM[0:1, 1:2])),
    }

    def _aps(*xs):
        return [x for x in xs if hasattr(x, "tensor")]

    def act(out, in_, func, bias=0.0, scale=1.0, accum_out=None, sem=None):
        wr = _aps(out, accum_out)
        b.pre("s", _aps(in_, bias, scale), wr)
        i = b.op("s", S.activation(out, in_, func, bias=bias, scale=scale,
                                   accum_out=accum_out), sem)
        return i

    def vop(inst, sem=None):
        return b.op("v", inst, sem)

    def vts(out, in0, s1, s2, op0, op1=None, sem=None):
        b.pre("v", _aps(in0, s1, s2), [out])
        if op1 is None:
            return vop(V.tensor_scalar(out, in0, s1, None, op0), sem)
        return vop(V.tensor_scalar(out, in0, s1, s2, op0, op1), sem)

    def vtt(out, a_, b_, opx, sem=None):
        b.pre("v", _aps(a_, b_), [out])
        return vop(V.tensor_tensor(out, a_, b_, opx), sem)

    def vstt(out, in0, scal, in1, op0, op1, sem=None):
        b.pre("v", _aps(in0, scal, in1), [out])
        return vop(V.scalar_tensor_tensor(out, in0, scal, in1, op0, op1), sem)

    def vcopy(out, in_, sem=None):
        b.pre("v", _aps(in_), [out])
        return vop(V.tensor_copy(out, in_), sem)

    def vred(out, in_, opx, absval=False, sem=None):
        b.pre("v", _aps(in_), [out])
        return vop(V.tensor_reduce(out, in_, AX, opx,
                                   apply_absolute_value=absval), sem)

    def scopy(out, in_, sem=None):
        b.pre("s", _aps(in_), [out])
        i = b.op("s", S.copy(out, in_), sem)
        return i

    def mm(out, lhsT, rhs, start, stop, sem=None, tile_position=None):
        return b.op("t", T.matmul(out, lhsT, rhs, start=start, stop=stop,
                                  tile_position=tile_position), sem)

    def tr(out, in_, ident, sem=None):
        return b.op("t", T.transpose(out, in_, ident), sem)

    def ldx(dst, src):
        return b.dma(SY, "sy", dst, src, LX)

    _ldq = ((SY, "sy", LX), (G, "g", LW), (S, "s", LS))

    def ldq(i, dst, src):
        q = _ldq[i % 3]
        return b.dma(q[0], q[1], dst, src, q[2])

    def ldw(dst, src):
        return b.dma(G, "g", dst, src, LW)

    def stx(dst, src):
        return b.dma(SY, "sy", dst, src, STP)

    def barrier():
        for en, sname in (("t", "TS"), ("v", "VS"), ("s", "SS"), ("g", "GS")):
            if b.last.get(en) is not None:
                b.fence(en, sname)
        vals = {sn: b.cnt[sn] for sn in ("TS", "VS", "SS", "GS")}
        for eng, en in ((T, "t"), (V, "v"), (S, "s"), (G, "g"), (SY, "sy")):
            for sn in ("TS", "VS", "SS", "GS"):
                b.wait(eng, en, sn, vals[sn])
            for pool in (LW, LX, STP):
                b.wait_pool(eng, en, pool)

    class Arena:
        _uid = [0]

        def __init__(self):
            self.stack = ExitStack()

        def _mk(self, n, dtype):
            Arena._uid[0] += 1
            t = self.stack.enter_context(
                nc.sbuf_tensor(f"ar{Arena._uid[0]}", [128, n], dtype))
            return t[:, :]

        def f32(self, n):
            return self._mk(n, F32)

        def bf16(self, n):
            return self._mk(n, BF16)

        def fp16(self, n):
            return self._mk(n, FP16)

        def w(self, n):
            return self._mk(n, BF16)

        def close(self):
            self.stack.close()

    def r3(ap2d, c, n):
        return ap2d.rearrange("p (c n) -> p c n", c=c, n=n)

    def emit_round(dst, u32, uh, th):
        if not ROUND_LADDER:
            vts(dst, u32, MAGIC - 0.5, MAGIC + 128.0, ALU.add, ALU.subtract)
            return
        vcopy(uh, u32)
        for c in (128.0, 64.0, 32.0, 16.0, 8.0, 4.0, 2.0, 1.0):
            vts(th, uh, c, -c, ALU.is_ge, ALU.mult)
            vtt(uh, uh, th, ALU.add)
        vstt(dst, u32, 128.0, uh, ALU.subtract, ALU.subtract)

    def round_sv_steps(dst, u32, uh, th):
        first = True
        for c in (128.0, 64.0, 32.0, 16.0, 8.0, 4.0, 2.0, 1.0):
            src = u32 if first else uh
            b.wait(S, "s", "VS")
            act(th, src, ACT.Sign, bias=CVS[:, 2:3], sem="SS")
            yield
            b.wait(V, "v", "SS")
            vstt(uh, th, -0.5 * c, src, ALU.mult, ALU.add, sem="VS")
            first = False
            yield
        vstt(dst, u32, 0.5, uh, ALU.subtract, ALU.subtract)
        yield

    def emit_round_sv(dst, u32, uh, th):
        for _ in round_sv_steps(dst, u32, uh, th):
            pass

    def emit_wq_prep():
        ar = Arena()
        NWB = 4
        wbuf = [ar.f32(4096) for _ in range(NWB)]
        sums = ar.f32(IKC)
        tot = ar.f32(2)
        sg = [ar.bf16(4096), ar.bf16(4096)]
        wab = [ar.bf16(4096), ar.bf16(4096)]
        wabl = [ar.bf16(4096), ar.bf16(4096)]
        wab32 = [ar.f32(4096), ar.f32(4096)]
        mg = [ar.bf16(4096), ar.bf16(4096)]
        cons = {}
        gl = [0]
        for (src, dst, nch, width, widx) in (
                (wgr, wqg, KC, INNER, 0), (wur, wqu, KC, INNER, 1),
                (wdr, wqd, IKC, D, 2)):
            src3 = src.rearrange("(c p) o -> p c o", p=128)
            dst3 = r3(dst, nch, width)
            mmtok = {}
            for k in range(nch):
                g_ = gl[0]
                d_ = k % 2
                q = _ldq[g_ % 3]
                b.wait_tok(q[0], q[1], cons.get(g_ - NWB))
                ltok = ldq(g_, wbuf[g_ % NWB][:, 0:width], src3[:, k, :])
                b.wait_tok(S, "s", ltok)
                if k >= 2:
                    b.wait(S, "s", "TS", mmtok.get(k - 2, 0))
                b.wait(S, "s", "VS")
                act(wab32[d_][:, 0:width], wbuf[g_ % NWB][:, 0:width], ACT.Abs,
                    sem="SS")
                cons[g_] = ("SS", b.cnt["SS"])
                gl[0] += 1
                b.wait(V, "v", "SS")
                if k >= 2:
                    b.wait(V, "v", "TS", mmtok.get(k - 2, 0))
                vcopy(wab[d_][:, 0:width], wab32[d_][:, 0:width])
                vtt(wabl[d_][:, 0:width], wab32[d_][:, 0:width], wab[d_][:, 0:width],
                    ALU.subtract, sem="VS")
                b.wait(T, "t", "SS")
                b.wait(T, "t", "VS")
                for cs in range(width // 512):
                    mm(PB[0][0:1, 0:512], ONB[:, 0:1],
                       wab[d_][:, cs * 512:(cs + 1) * 512],
                       (k == 0 and cs == 0), False)
                    mm(PB[0][0:1, 0:512], ONB[:, 0:1],
                       wabl[d_][:, cs * 512:(cs + 1) * 512],
                       False, (k == nch - 1 and cs == width // 512 - 1))
                b.fence("t", "TS")
                mmtok[k] = b.cnt["TS"]
            b.wait(V, "v", "TS")
            vred(tot[0:1, 0:1], PB[0][0:1, 0:512], ALU.add)
            vts(tot[0:1, 1:2], tot[0:1, 0:1], 1.0 / (nch * 128 * width), 1e-5,
                ALU.mult, ALU.max, sem="VS")
            b.wait(T, "t", "VS")
            mm(PB[1][:, 0:1], ONF[0:1, 0:128], tot[0:1, 1:2], True, True, sem="TS")
            b.wait(V, "v", "TS")
            vcopy(GLB[:, widx:widx + 1], PB[1][:, 0:1])
            vts(GLB[:, 3 + widx:4 + widx], PB[1][:, 0:1], 0.5, None, ALU.mult, sem="VS")
            vtok = {}
            for k in range(nch):
                g_ = gl[0]
                d_ = k % 2
                wb = wbuf[g_ % NWB][:, 0:width]
                q = _ldq[g_ % 3]
                b.wait_tok(q[0], q[1], cons.get(g_ - NWB))
                ltok = ldq(g_, wb, src3[:, k, :])
                b.wait_tok(S, "s", ltok)
                b.wait(S, "s", "VS", vtok.get(k - 2, 0) if k >= 2 else None)
                act(sg[d_][:, 0:width], wb, ACT.Sign)
                act(wab32[d_][:, 0:width], wb, ACT.Abs, sem="SS")
                cons[g_] = ("SS", b.cnt["SS"])
                gl[0] += 1
                b.wait_pool(V, "v", LX)
                for _sn in b.pools[STP]["names"]:
                    b.wait(V, "v", _sn)
                m_ = mg[k % 2][:, 0:width]
                b.wait(V, "v", "SS")
                vts(m_, wab32[d_][:, 0:width], GLB[:, 3 + widx:4 + widx], None, ALU.is_ge)
                vtt(m_, m_, sg[d_][:, 0:width], ALU.mult, sem="VS")
                vtok[k] = b.cnt["VS"]
                b.wait(SY, "sy", "VS")
                stx(dst3[:, k, :], m_)
        barrier()
        ar.close()

    def emit_w0_prep():
        arM = Arena()
        M1H = r3(arM.bf16(KC * D), KC, D)
        M1L = r3(arM.bf16(KC * D), KC, D)
        ar1 = Arena()
        A1H = r3(ar1.bf16(KC * D), KC, D); A1L = r3(ar1.bf16(KC * D), KC, D)
        WTH = r3(ar1.bf16(KC * D), KC, D); WTL = r3(ar1.bf16(KC * D), KC, D)
        st1 = [ar1.f32(D) for _ in range(2)]
        for (src, dh, dl) in ((a10, A1H, A1L), (w0pt, WTH, WTL)):
            src3 = src.rearrange("(c p) o -> p c o", p=128)
            for k in range(KC):
                cu = k % 2
                q = _ldq[k % 3]
                b.wait(q[0], q[1], "SS")
                b.wait(q[0], q[1], "VS")
                ltok = ldq(k, st1[cu][:, :], src3[:, k, :])
                b.wait_tok(S, "s", ltok)
                scopy(dh[:, k, :], st1[cu][:, :], sem="SS")
                b.wait(V, "v", "SS")
                vtt(dl[:, k, :], st1[cu][:, :], dh[:, k, :], ALU.subtract, sem="VS")
        b.wait(T, "t", "SS")
        b.wait(T, "t", "VS")
        for cc in range(KC):
            for rn in range(2):
                pp = PB[0] if rn == 0 else PB[1]
                b.wait(T, "t", "SS")
                b.wait(T, "t", "VS")
                for j in range(KC):
                    la = A1H[:, j, cc * 128:(cc + 1) * 128]
                    ll = A1L[:, j, cc * 128:(cc + 1) * 128]
                    rh = WTH[:, j, rn * 512:(rn + 1) * 512]
                    rl = WTL[:, j, rn * 512:(rn + 1) * 512]
                    mm(pp[:, 0:512], la, rh, j == 0, False)
                    mm(pp[:, 0:512], ll, rh, False, False)
                    mm(pp[:, 0:512], la, rl, False, j == KC - 1)
                b.fence("t", "TS")
                b.wait(S, "s", "TS")
                scopy(M1H[:, cc, rn * 512:(rn + 1) * 512], pp[:, 0:512], sem="SS")
                b.wait(V, "v", "SS")
                b.wait(V, "v", "TS")
                vtt(M1L[:, cc, rn * 512:(rn + 1) * 512], pp[:, 0:512],
                    M1H[:, cc, rn * 512:(rn + 1) * 512], ALU.subtract, sem="VS")
        barrier()
        ar1.close()
        ar2 = Arena()
        A2H = r3(ar2.bf16(KC * D), KC, D); A2L = r3(ar2.bf16(KC * D), KC, D)
        st2 = [ar2.f32(D) for _ in range(2)]
        och = [ar2.bf16(D) for _ in range(2)]
        ocl = [ar2.bf16(D) for _ in range(2)]
        src3 = a20.rearrange("(c p) o -> p c o", p=128)
        for k in range(KC):
            cu = k % 2
            q = _ldq[k % 3]
            b.wait(q[0], q[1], "SS")
            b.wait(q[0], q[1], "VS")
            ltok = ldq(k, st2[cu][:, :], src3[:, k, :])
            b.wait_tok(S, "s", ltok)
            scopy(A2H[:, k, :], st2[cu][:, :], sem="SS")
            b.wait(V, "v", "SS")
            vtt(A2L[:, k, :], st2[cu][:, :], A2H[:, k, :], ALU.subtract, sem="VS")
        b.wait(T, "t", "SS")
        b.wait(T, "t", "VS")
        wch3, wcl3 = r3(wch, KC, D), r3(wcl, KC, D)
        ot_tok = {}
        for rc in range(KC):
            cu = rc % 2
            for on in range(2):
                pp = PB[0] if on == 0 else PB[1]
                b.wait(T, "t", "SS")
                b.wait(T, "t", "VS")
                for cc in range(KC):
                    la = M1H[:, cc, rc * 128:(rc + 1) * 128]
                    ll = M1L[:, cc, rc * 128:(rc + 1) * 128]
                    rh = A2H[:, cc, on * 512:(on + 1) * 512]
                    rl = A2L[:, cc, on * 512:(on + 1) * 512]
                    mm(pp[:, 0:512], la, rh, cc == 0, False)
                    mm(pp[:, 0:512], ll, rh, False, False)
                    mm(pp[:, 0:512], la, rl, False, cc == KC - 1)
                b.fence("t", "TS")
                b.wait(S, "s", "TS")
                b.wait_tok(S, "s", ot_tok.get((rc - 2, 0)))
                b.wait_tok(V, "v", ot_tok.get((rc - 2, 1)))
                scopy(och[cu][:, on * 512:(on + 1) * 512], pp[:, 0:512], sem="SS")
                b.wait(V, "v", "SS")
                b.wait(V, "v", "TS")
                vtt(ocl[cu][:, on * 512:(on + 1) * 512], pp[:, 0:512],
                    och[cu][:, on * 512:(on + 1) * 512], ALU.subtract, sem="VS")
            b.wait(SY, "sy", "SS")
            b.wait(SY, "sy", "VS")
            ot_tok[(rc, 0)] = stx(wch3[:, rc, :], och[cu][:, :])
            ot_tok[(rc, 1)] = stx(wcl3[:, rc, :], ocl[cu][:, :])
        barrier()
        ar2.close()
        arM.close()

    def emit_seg_a():
        ar = Arena()
        WZH = r3(ar.w(KC * D), KC, D); WZL = r3(ar.w(KC * D), KC, D)
        WHH = r3(ar.w(KC * D), KC, D); WHL = r3(ar.w(KC * D), KC, D)
        WGH = r3(ar.w(KC * D), KC, D); WGL = r3(ar.w(KC * D), KC, D)
        WCH = r3(ar.w(KC * D), KC, D); WCL = r3(ar.w(KC * D), KC, D)
        for wi, (dstw, srcw) in enumerate(((WCH, wch), (WCL, wcl))):
            ldq(wi + 1, dstw[:, :, :], r3(srcw, KC, D))
        WST = [ar.f32(D)]
        wcons = {}
        gi = 0
        for (dh, dl, srcw) in ((WZH, WZL, wzt), (WHH, WHL, wht), (WGH, WGL, wg0)):
            s3 = srcw.rearrange("(c p) o -> p c o", p=128)
            for k in range(KC):
                cu = 0
                q = _ldq[gi % 3]
                b.wait_tok(q[0], q[1], wcons.get(gi - 1))
                ltok = ldq(gi, WST[cu][:, :], s3[:, k, :])
                b.wait_tok(S, "s", ltok)
                scopy(dh[:, k, :], WST[cu][:, :], sem="SS")
                b.wait(V, "v", "SS", b.cnt["SS"])
                vtt(dl[:, k, :], WST[cu][:, :], dh[:, k, :], ALU.subtract, sem="VS")
                wcons[gi] = ("VS", b.cnt["VS"])
                gi += 1
        wsplit_tok = (b.cnt["SS"], b.cnt["VS"])

        NS = TPA // 128
        XT = [r3(ar.f32(NS * D), NS, D) for _ in range(2)]
        XNT = r3(ar.f32(NS * D), NS, D)
        XF = [r3(ar.f32(KC * TPA), KC, TPA) for _ in range(2)]
        XNFH = [r3(ar.bf16(KC * TPA), KC, TPA) for _ in range(2)]
        AA = r3(ar.f32(KC * TPA), KC, TPA)
        BB = r3(ar.f32(KC * TPA), KC, TPA)
        HF = BB
        HBH = r3(ar.bf16(KC * TPA), KC, TPA)
        ZS = ar.f32(TPA)
        GSc = ar.f32(TPA)
        FS = ar.f32(TPA)
        ATS = ZS
        STT = ar.f32(8)

        def mm3(pp, wh_, wl_, xh_, o, n):
            for k in range(KC):
                mm(pp[:, 0:n], wh_[:, k, o * 128:(o + 1) * 128], xh_[:, k, :],
                   k == 0, False)
                mm(pp[:, 0:n], wl_[:, k, o * 128:(o + 1) * 128], xh_[:, k, :],
                   False, k == KC - 1)
            b.fence("t", "TS")

        x3 = x_in.rearrange("(s p) d -> p s d", p=128)
        b.dma(G, "g", SST[:, :], zeros_in[:, 0:2 * KC], LW)
        b.wait_pool(V, "v", LX)
        vts(BZN[:, :], BIAS[:, 0, :], -1.0, None, ALU.mult, sem="VS")
        b.wait_pool(G, "g", LW)
        b.wait_pool(T, "t", LW)
        b.wait_pool(T, "t", LS)
        b.wait_pool(T, "t", LX)
        b.wait(V, "v", "GS")

        tile_guard = {}
        st_toks = {}
        prep_done = {}
        scan_tok = {}
        hbs_tok = {}
        gate_tok = {}

        NTA2 = 2 * NTA

        def tile_prep(j):
            cu = j % 2
            p_ = j % 2
            sub0 = p_ * (SEQ // 128) + (j // 2) * NS
            if j >= 2:
                b.wait(SY, "sy", "TS", tile_guard[j - 2])
            ldx(XT[cu][:, :, :], x3[:, sub0:sub0 + NS, :])
            yield
            b.wait_pool(V, "v", LX)
            if j >= 1:
                b.wait(V, "v", "TS", tile_guard[j - 1])
            for sbi in range(NS):
                xs = XT[cu][:, sbi, :]
                sq = XNT[:, sbi, :]
                vred(STT[:, 0:1], xs, ALU.add)
                vstt(sq, xs, 0.0, xs, ALU.add, ALU.mult)
                vred(STT[:, 1:2], sq, ALU.add)
                vts(STT[:, 2:3], STT[:, 0:1], 1.0 / D, None, ALU.mult)
                vts(STT[:, 3:4], STT[:, 1:2], 1.0 / D, EPS_LN, ALU.mult, ALU.add)
                vtt(STT[:, 4:5], STT[:, 2:3], STT[:, 2:3], ALU.mult)
                vstt(STT[:, 3:4], STT[:, 4:5], -1.0, STT[:, 3:4], ALU.mult, ALU.add)
                vts(STT[:, 4:5], STT[:, 3:4], 0.0, 1.0, ALU.mult, ALU.add)
                for _ in range(4):
                    vtt(STT[:, 5:6], STT[:, 4:5], STT[:, 4:5], ALU.mult)
                    vtt(STT[:, 5:6], STT[:, 5:6], STT[:, 3:4], ALU.mult)
                    vts(STT[:, 5:6], STT[:, 5:6], -0.5, 1.5, ALU.mult, ALU.add)
                    vtt(STT[:, 4:5], STT[:, 4:5], STT[:, 5:6], ALU.mult)
                vts(XNT[:, sbi, :], xs, STT[:, 2:3], STT[:, 4:5],
                    ALU.subtract, ALU.mult)
                yield
            b.fence("v", "VS")
            ln_vs = b.cnt["VS"]
            b.wait(T, "t", "VS", ln_vs)
            pbtok = {}
            for sbi in range(NS):
                for k in range(KC):
                    bk = k % 2
                    pbx = PB[4 + bk]
                    if bk in pbtok:
                        b.wait(T, "t", "SS", pbtok[bk][0])
                    tr(pbx[0:128, 0:128], XNT[:, sbi, k * 128:(k + 1) * 128], IDF[:, :])
                    b.fence("t", "TS")
                    trts = b.cnt["TS"]
                    b.wait(S, "s", "TS", trts)
                    scopy(XNFH[cu][:, k, sbi * 128:(sbi + 1) * 128],
                          pbx[0:128, 0:128], sem="SS")
                    pbtok[bk] = (b.cnt["SS"], b.cnt["SS"])
                    if k % 4 == 3:
                        yield
            if j >= 2:
                b.wait_tok(V, "v", st_toks.get(j - 2))
            pbt2 = {}
            for sbi in range(NS):
                for k in range(KC):
                    bk = k % 2
                    pbx = PB[4 + bk]
                    if bk in pbt2:
                        b.wait(T, "t", "VS", pbt2[bk])
                    elif pbtok:
                        b.wait(T, "t", "SS", pbtok[bk][0])
                    tr(pbx[0:128, 0:128], XT[cu][:, sbi, k * 128:(k + 1) * 128],
                       IDF[:, :])
                    b.fence("t", "TS")
                    b.wait(V, "v", "TS", b.cnt["TS"])
                    vcopy(XF[cu][:, k, sbi * 128:(sbi + 1) * 128],
                          pbx[0:128, 0:128], sem="VS")
                    pbt2[bk] = b.cnt["VS"]
                    if k % 4 == 3:
                        yield
            tile_guard[j] = b.cnt["TS"]
            prep_done[j] = (b.cnt["SS"], b.cnt["VS"])
            yield

        prep = [None]

        def ppump(n):
            g = prep[0]
            if g is None:
                return
            try:
                for _ in range(n):
                    next(g)
            except StopIteration:
                prep[0] = None

        for _ in tile_prep(0):
            pass
        b.wait(T, "t", "SS", wsplit_tok[0])
        b.wait(T, "t", "VS", wsplit_tok[1])
        for i in range(NTA2):
            cur = i % 2
            p_ = i % 2
            ti = i // 2
            if i + 1 < NTA2:
                prep[0] = tile_prep(i + 1)
            zev = {}
            for o in range(KC):
                pz = PB[0] if o % 2 == 0 else PB[1]
                ph = PB[2] if o % 2 == 0 else PB[3]
                if o >= 2:
                    b.wait(T, "t", "SS", zev[o - 2][0])
                    b.wait(T, "t", "VS", zev[o - 2][1])
                elif i >= 1:
                    b.wait(T, "t", "SS", gate_tok[i - 1][0])
                    b.wait(T, "t", "VS", gate_tok[i - 1][1])
                mm3(pz, WZH, WZL, XNFH[cur], o, TPA)
                mm3(ph, WHH, WHL, XNFH[cur], o, TPA)
                b.wait(S, "s", "TS", b.cnt["TS"] - 1)
                if i >= 1:
                    b.wait(S, "s", "VS", scan_tok[i - 1])
                if o >= 1:
                    b.wait(S, "s", "VS", zev[o - 1][1])
                act(ZS[:, :], pz[:, 0:TPA], ACT.Sigmoid, bias=BIAS[:, 0, o:o + 1], sem="SS")
                act(AA[:, o, :], pz[:, 0:TPA], ACT.Sigmoid, bias=BZN[:, o:o + 1],
                    scale=-1.0, sem="SS")
                b.wait(V, "v", "TS", b.cnt["TS"])
                b.wait(V, "v", "SS", b.cnt["SS"])
                if i >= 1:
                    b.wait(V, "v", "SS", hbs_tok[i - 1][0])
                vstt(BB[:, o, :], ph[:, 0:TPA], BIAS[:, 1, o:o + 1], ZS[:, :],
                     ALU.add, ALU.mult, sem="VS")
                zev[o] = (b.cnt["SS"], b.cnt["VS"])
                ppump(1)
            if i >= 1:
                b.wait(V, "v", "SS", hbs_tok[i - 1][0])
            for k in range(KC):
                sk = p_ * KC + k
                b.pre("v", [AA[:, k, :], BB[:, k, :], SST[:, sk:sk + 1]], [HF[:, k, :]])
                vop(V.tensor_tensor_scan(HF[:, k, :], AA[:, k, :],
                                         BB[:, k, :], SST[:, sk:sk + 1],
                                         ALU.mult, ALU.add))
                vcopy(SST[:, sk:sk + 1], HF[:, k, TPA - 1:TPA])
            b.fence("v", "VS")
            vscan = b.cnt["VS"]
            scan_tok[i] = vscan
            ppump(2)
            b.wait(S, "s", "VS", vscan)
            if i >= 1:
                b.wait(S, "s", "TS", gate_tok[i - 1][2])
                b.wait(V, "v", "TS", gate_tok[i - 1][2])
            for k in range(KC):
                scopy(HBH[:, k, :], HF[:, k, :], sem="SS")
            hbs_tok[i] = (b.cnt["SS"], b.cnt["VS"])
            gev = {}
            for o in range(KC):
                pg = PB[0] if o % 2 == 0 else PB[1]
                pf = PB[2] if o % 2 == 0 else PB[3]
                if o >= 2:
                    b.wait(T, "t", "SS", gev[o - 2][0])
                else:
                    b.wait(T, "t", "SS", hbs_tok[i][0])
                    b.wait(T, "t", "VS", hbs_tok[i][1])
                mm3(pg, WGH, WGL, HBH, o, TPA)
                mm3(pf, WCH, WCL, XNFH[cur], o, TPA)
                b.wait(S, "s", "TS", b.cnt["TS"] - 1)
                if o >= 1:
                    b.wait(S, "s", "VS", gev[o - 1][1])
                act(GSc[:, :], pg[:, 0:TPA], ACT.Sigmoid, bias=BIAS[:, 2, o:o + 1], sem="SS")
                b.wait(S, "s", "TS", b.cnt["TS"])
                act(FS[:, :], pf[:, 0:TPA], ACT.Tanh, bias=BIAS[:, 3, o:o + 1], sem="SS")
                b.wait(V, "v", "SS", b.cnt["SS"] - 1)
                vtt(ATS[:, :], GSc[:, :], HF[:, o, :], ALU.mult)
                vtt(XF[cur][:, o, :], XF[cur][:, o, :], ATS[:, :], ALU.add)
                b.wait(V, "v", "SS", b.cnt["SS"])
                vtt(XF[cur][:, o, :], XF[cur][:, o, :], FS[:, :], ALU.add, sem="VS")
                gev[o] = (b.cnt["SS"], b.cnt["VS"])
                ppump(3)
            gate_tok[i] = (b.cnt["SS"], b.cnt["VS"], b.cnt["TS"])
            ppump(100)
            b.wait(SY, "sy", "VS")
            toff = p_ * SEQ + ti * TPA
            st_toks[i] = stx(x1a[:, :, toff:toff + TPA], XF[cur][:, :, :])
        barrier()
        ar.close()

    def emit_seg_b():
        ar = Arena()
        WQ = r3(ar.w(KC * D), KC, D); WK = r3(ar.w(KC * D), KC, D)
        WV = r3(ar.w(KC * D), KC, D); WO = r3(ar.w(KC * D), KC, D)
        for (dstw, srcw) in ((WQ, wqt), (WK, wkt), (WV, wvt), (WO, wot)):
            ldw(dstw[:, :, :], srcw.rearrange("(c p) o -> p c o", p=128))

        X1 = [r3(ar.f32(KC * TPB), KC, TPB) for _ in range(2)]
        X1B = r3(ar.bf16(KC * TPB), KC, TPB)
        X1BL = r3(ar.bf16(KC * TPB), KC, TPB)
        SQ = r3(ar.bf16(KC * TPB), KC, TPB)
        SQL = r3(ar.bf16(KC * TPB), KC, TPB)
        SQF = ar.f32(TPB)
        VARR = ar.f32(TPB)
        XN1 = r3(ar.bf16(KC * TPB), KC, TPB)
        PQ = r3(ar.bf16(KC * TPB), KC, TPB)
        PK = r3(ar.bf16(KC * TPB), KC, TPB)
        PKT = [r3(ar.bf16(D), KC, 128) for _ in range(2)]
        VEX = [r3(ar.bf16(NH * HD), NH, HD) for _ in range(2)]
        OBF = r3(ar.bf16(KC * TPB), KC, TPB)
        AT0 = ar.bf16(2 * TPB)
        AT1 = ar.bf16(256)
        QM = ar.f32(TPB)
        EX = ar.f32(TPB)
        RBS = ar.f32(TPB)
        DENS = ar.f32(2 * TPB)
        SEXT = r3(ar.f32(2 * NH * HD), 2 * NH, HD)
        SZP = r3(ar.f32(2 * NH), 2 * NH, 1)
        SEB = r3(ar.bf16(2 * NH * HD), 2 * NH, HD)
        SZB = r3(ar.bf16(2 * NH), 2 * NH, 1)
        MRW = ar.f32(TPB)
        RRW = ar.f32(TPB)
        STR = ar.f32(TPB)
        U32 = ar.f32(KC * TPB)
        UH = ar.f32(KC * TPB)
        TH = ar.f32(KC * TPB)
        XN1F = r3(ar.f32(KC * TPB), KC, TPB)
        XQ2 = ar.bf16(KC * TPB)
        XQ = r3(XQ2, KC, TPB)
        CM = ar.f32(TPB)
        ABB = ar.f32(2 * TPB)
        AMT = ar.f32(4)

        zs2 = zeros_in.rearrange("p (h e) -> p h e", h=2 * NH)
        b.dma(G, "g", SEXT[:, :, :], zs2[:, :, :], LW)
        b.dma(G, "g", SZP.rearrange("p h e -> p (h e)"), zeros_in[:, 0:2 * NH], LW)
        b.dma(G, "g", SEB[:, :, :], zs2[:, :, :], LW)
        b.dma(G, "g", SZB.rearrange("p h e -> p (h e)"), zeros_in[:, 0:2 * NH], LW)
        b.wait_pool(T, "t", LW)
        b.wait_pool(T, "t", LS)
        b.wait_pool(T, "t", LX)
        b.wait_pool(S, "s", LW)
        b.wait_pool(V, "v", LW)

        tguard = {}
        xq_tok = {}
        x1b_tok = {}
        xnfree = {}
        sgen = [None]

        def stats_gen(ci_, cur_, t0_):
            if ci_ >= 2:
                b.wait(SY, "sy", "VS", tguard[ci_ - 2])
                b.wait_tok(SY, "sy", x1b_tok.get(ci_ - 2))
            ldx(X1[cur_][:, :, :], x1a[:, :, t0_:t0_ + TPB])
            yield
            b.wait_pool(V, "v", LX)
            for k in range(KC):
                vcopy(X1B[:, k, :], X1[cur_][:, k, :])
                vtt(X1BL[:, k, :], X1[cur_][:, k, :], X1B[:, k, :], ALU.subtract)
                if k % 2 == 1:
                    yield
            b.fence("v", "VS")
            b.wait(T, "t", "VS", b.cnt["VS"])
            for k in range(KC):
                mm(PB[3][0:1, 256:256 + TPB], ONB[:, 0:1], X1B[:, k, :],
                   k == 0, False)
                mm(PB[3][0:1, 256:256 + TPB], ONB[:, 0:1], X1BL[:, k, :],
                   False, k == KC - 1)
            b.fence("t", "TS")
            yield
            for k in range(KC):
                b.wait(S, "s", "VS", b.cnt["VS"])
                act(SQF[:, :], X1[cur_][:, k, :], ACT.Square)
                scopy(SQ[:, k, :], SQF[:, :], sem="SS")
                b.wait(V, "v", "SS", b.cnt["SS"])
                vtt(SQL[:, k, :], SQF[:, :], SQ[:, k, :], ALU.subtract, sem="VS")
                yield
            b.wait(T, "t", "SS", b.cnt["SS"])
            b.wait(T, "t", "VS", b.cnt["VS"])
            for k in range(KC):
                mm(PB[4][0:1, 256:256 + TPB], ONB[:, 0:1], SQ[:, k, :],
                   k == 0, False)
                mm(PB[4][0:1, 256:256 + TPB], ONB[:, 0:1], SQL[:, k, :],
                   False, k == KC - 1)
            b.fence("t", "TS")
            yield
            b.wait(V, "v", "TS", b.cnt["TS"])
            vts(MRW[0:1, :], PB[3][0:1, 256:256 + TPB], 1.0 / D, None, ALU.mult)
            vts(RRW[0:1, :], PB[4][0:1, 256:256 + TPB], 1.0 / D, None, ALU.mult)
            vtt(STR[0:1, :], MRW[0:1, :], MRW[0:1, :], ALU.mult)
            vstt(VARR[0:1, :], STR[0:1, :], -1.0, RRW[0:1, :], ALU.mult, ALU.add)
            vts(VARR[0:1, :], VARR[0:1, :], EPS_LN, None, ALU.add, sem="VS")
            yield
            b.wait(S, "s", "VS", b.cnt["VS"])
            act(RRW[0:1, :], VARR[0:1, :], ACT.Ln)
            act(RRW[0:1, :], RRW[0:1, :], ACT.Exp, scale=-0.5, sem="SS")
            b.wait(V, "v", "SS", b.cnt["SS"])
            for _ in range(1):
                vtt(STR[0:1, :], RRW[0:1, :], RRW[0:1, :], ALU.mult)
                vtt(STR[0:1, :], STR[0:1, :], VARR[0:1, :], ALU.mult)
                vts(STR[0:1, :], STR[0:1, :], -0.5, 1.5, ALU.mult, ALU.add)
                vtt(RRW[0:1, :], RRW[0:1, :], STR[0:1, :], ALU.mult)
            vtt(MRW[0:1, :], MRW[0:1, :], RRW[0:1, :], ALU.mult, sem="VS")
            yield
            b.wait(T, "t", "VS", b.cnt["VS"])
            mm(PB[5][:, 256:256 + TPB], ONF[0:1, 0:128], RRW[0:1, :], True, True)
            b.fence("t", "TS")
            mm(PB[4][:, 256:256 + TPB], ONF[0:1, 0:128], MRW[0:1, :], True, True)
            b.fence("t", "TS")
            bc_ts = b.cnt["TS"]
            yield
            b.wait(V, "v", "TS", bc_ts)
            if ci_ >= 1:
                b.wait(V, "v", "SS", xnfree[ci_ - 1][0])
            for k in range(KC):
                vtt(QM[:, :], X1[cur_][:, k, :], PB[5][:, 256:256 + TPB], ALU.mult)
                vtt(XN1F[:, k, :], QM[:, :], PB[4][:, 256:256 + TPB], ALU.subtract)
                if k % 2 == 1:
                    yield
            b.fence("v", "VS")
            xn_vs = b.cnt["VS"]
            b.wait(S, "s", "VS", xn_vs)
            if ci_ >= 1:
                b.wait(S, "s", "TS", xnfree[ci_ - 1][1])
            for k in range(KC):
                scopy(XN1[:, k, :], XN1F[:, k, :])
                if k % 2 == 1:
                    yield
            b.fence("s", "SS")
            yield

        def spump(n):
            g = sgen[0]
            if g is None:
                return
            try:
                for _ in range(n):
                    next(g)
            except StopIteration:
                sgen[0] = None

        for _ in stats_gen(0, 0, 0):
            pass
        for ci in range(2 * NTB):
            cur = ci % 2
            p_ = ci % 2
            t0 = p_ * SEQ + (ci // 2) * TPB
            spump(100)
            b.wait(T, "t", "SS")
            tile_vs0 = b.cnt["VS"]
            tile_ss0 = b.cnt["SS"]
            qev = {}
            qn = 0
            for (wmat, dst) in ((WQ, PQ), (WK, PK)):
                for o in range(KC):
                    pp = PB[qn % 4]
                    if qn >= 4:
                        b.wait(T, "t", "VS", qev[qn - 4])
                    else:
                        b.wait(T, "t", "VS", tile_vs0)
                        b.wait(T, "t", "SS", tile_ss0)
                        if qn == 0:
                            b.wait(S, "s", "VS", b.cnt["VS"])
                    for k in range(KC):
                        mm(pp[:, 0:TPB], wmat[:, k, o * 128:(o + 1) * 128], XN1[:, k, :],
                           k == 0, k == KC - 1)
                    b.fence("t", "TS")
                    b.wait(S, "s", "TS", b.cnt["TS"])
                    if qn >= 1:
                        b.wait(S, "s", "VS", qev[qn - 1])
                    act(QM[:, :], pp[:, 0:TPB], ACT.Relu, scale=-1.0)
                    act(EX[:, :], QM[:, :], ACT.Exp, scale=-1.0, sem="SS")
                    b.wait(V, "v", "SS", b.cnt["SS"])
                    vstt(dst[:, o, :], pp[:, 0:TPB], 0.0, EX[:, :], ALU.max, ALU.add,
                         sem="VS")
                    qev[qn] = b.cnt["VS"]
                    qn += 1
            vev = {}
            vn = 0
            for sbi in range(TPB // 128):
                for nh2 in range(2):
                    pp = PB[vn % 4]
                    if vn == 0:
                        b.wait(T, "t", "VS", qev[qn - 4 + 0])
                        b.wait(T, "t", "VS", qev[qn - 4 + 1])
                        b.wait(T, "t", "VS", qev[qn - 4 + 2])
                        b.wait(T, "t", "VS", qev[qn - 4 + 3])
                    for k in range(KC):
                        mm(pp[:, 0:512], XN1[:, k, sbi * 128:(sbi + 1) * 128],
                           WV[:, k, nh2 * 512:(nh2 + 1) * 512], k == 0, k == KC - 1)
                    b.fence("t", "TS")
                    b.wait(S, "s", "TS", b.cnt["TS"])
                    scopy(VEX[sbi][:, nh2 * 8:(nh2 + 1) * 8, :],
                          pp[:, 0:512].rearrange("p (h e) -> p h e", h=8), sem="SS")
                    vev[vn] = b.cnt["SS"]
                    vn += 1
            for j in range(TPB // 128):
                for k in range(KC):
                    b.wait(T, "t", "VS")
                    tr(PT[k % 2][0:128, 0:128], PK[:, k, j * 128:(j + 1) * 128], IDB[:, :])
                    b.fence("t", "TS")
                    b.wait(S, "s", "TS")
                    scopy(PKT[j][:, k, :], PT[k % 2][0:128, 0:128], sem="SS")
                    b.wait(T, "t", "SS", b.cnt["SS"] - 1)
            b.wait(S, "s", "VS")
            act(CM[:, :], XN1F[:, 0, :], ACT.Abs)
            vg = {}
            for k in range(1, KC):
                if k >= 3:
                    b.wait(S, "s", "VS", vg[k - 2])
                act(ABB[:, (k % 2) * TPB:(k % 2 + 1) * TPB], XN1F[:, k, :], ACT.Abs, sem="SS")
                b.wait(V, "v", "SS")
                vtt(CM[:, :], CM[:, :], ABB[:, (k % 2) * TPB:(k % 2 + 1) * TPB], ALU.max, sem="VS")
                vg[k] = b.cnt["VS"]
            for sbi in range(TPB // 128):
                b.wait(T, "t", "VS")
                tr(PB[4][0:128, 0:128], CM[:, sbi * 128:(sbi + 1) * 128], IDF[:, :])
                b.fence("t", "TS")
                b.wait(V, "v", "TS")
                vred(AMT[:, sbi:sbi + 1], PB[4][0:128, 0:128], ALU.max)
                vcopy(AMX[:, t0 // 128 + sbi:t0 // 128 + sbi + 1], AMT[:, sbi:sbi + 1])
                b.pre("v", [AMT[:, sbi:sbi + 1]], [AMT[:, 2:3]])
                vop(V.reciprocal(AMT[:, 2:3], AMT[:, sbi:sbi + 1]))
                vts(AMT[:, 3:4], AMT[:, 2:3], 127.0, None, ALU.mult, sem="VS")
                b.wait(T, "t", "VS")
                mm(PB[5][0:1, 0:128], AMT[:, 3:4], IDF[:, 0:128], True, True, sem="TS")
                b.wait(V, "v", "TS")
                vcopy(RBS[0:1, sbi * 128:(sbi + 1) * 128], PB[5][0:1, 0:128], sem="VS")
            b.wait(T, "t", "VS")
            mm(PB[5][:, 0:TPB], ONF[0:1, 0:128], RBS[0:1, 0:TPB], True, True, sem="TS")
            b.wait(V, "v", "TS")
            b.wait_tok(V, "v", xq_tok.get(ci - 1))
            for k in range(KC):
                vtt(U32[:, k * TPB:(k + 1) * TPB], XN1F[:, k, :], PB[5][:, 0:TPB],
                    ALU.mult)
            vts(U32[:, :], U32[:, :], 128.5, None, ALU.add, sem="VS")

            def xq_gen(ci_, t0_):
                for _ in round_sv_steps(XQ2[:, :], U32[:, :], UH[:, :], TH[:, :]):
                    yield
                b.fence("v", "VS")
                b.wait(SY, "sy", "VS")
                xq_tok[ci_] = stx(xqs[:, :, t0_:t0_ + TPB], XQ[:, :, :])
                yield
            xpend = [xq_gen(ci, t0)]

            def xpump(n):
                g = xpend[0]
                if g is None:
                    return
                try:
                    for _ in range(n):
                        next(g)
                except StopIteration:
                    xpend[0] = None
            xnfree[ci] = (b.cnt["SS"], b.cnt["TS"])
            if ci + 1 < 2 * NTB:
                nci = ci + 1
                sgen[0] = stats_gen(nci, nci % 2, (nci % 2) * SEQ + (nci // 2) * TPB)
            b.wait(S, "s", "VS")
            scopy(SEB[:, :, :], SEXT[:, :, :])
            scopy(SZB[:, :, :], SZP[:, :, :], sem="SS")
            b.wait(T, "t", "SS")
            pair_tok = {}
            att_vs0 = b.cnt["VS"]
            att_ss0 = b.cnt["SS"]
            for pr in range(KC):
                hA, hB = 2 * pr, 2 * pr + 1
                sA, sB = p_ * NH + hA, p_ * NH + hB
                if pr >= 1:
                    b.wait(T, "t", "VS", pair_tok[pr - 1][0])
                    b.wait(T, "t", "SS", pair_tok[pr - 1][1])
                else:
                    b.wait(T, "t", "VS", att_vs0)
                    b.wait(T, "t", "SS", att_ss0)
                for bas, apb in ((0, PB[0]), (64, PB[1])):
                    mm(apb[0:128, 0:TPB], PK[bas:bas + 64, pr, 0:128],
                       PQ[bas:bas + 64, pr, 0:TPB], True, True)
                    b.fence("t", "TS")
                    mm(apb[0:128, TPB:TPB + 128], PK[bas:bas + 64, pr, 128:256],
                       PQ[bas:bas + 64, pr, 128:256], True, True)
                    b.fence("t", "TS")
                spb = PB[5]
                for j in range(2):
                    mm(spb[:, 0:128], PKT[j][:, pr, :],
                       VEX[j][:, hA:hB + 1, :].rearrange("p h e -> p (h e)"),
                       j == 0, j == 1)
                b.fence("t", "TS")
                for j in range(2):
                    mm(spb[:, 128:129], PKT[j][:, pr, :], ONB[:, 0:1], j == 0, j == 1)
                b.fence("t", "TS")
                b.wait(V, "v", "TS", b.cnt["TS"])
                for off, apb in ((0, PB[0]), (1, PB[1])):
                    vtt(AT0[0:128, off * TPB:off * TPB + 128], apb[0:128, 0:128],
                        MSK[:, :], ALU.mult)
                    vcopy(AT0[0:128, off * TPB + 128:off * TPB + TPB],
                          apb[0:128, 128:256])
                    vtt(AT1[0:128, off * 128:(off + 1) * 128],
                        apb[0:128, TPB:TPB + 128], MSK[:, :], ALU.mult)
                b.fence("v", "VS")
                vtt(SEXT[0:64, sA, :], SEXT[0:64, sA, :], spb[0:64, 0:64], ALU.add)
                vtt(SEXT[64:128, sB, :], SEXT[64:128, sB, :], spb[64:128, 64:128], ALU.add)
                vtt(SZP[0:64, sA, :], SZP[0:64, sA, :], spb[0:64, 128:129], ALU.add)
                vtt(SZP[64:128, sB, :], SZP[64:128, sB, :], spb[64:128, 128:129],
                    ALU.add, sem="VS")
                xpump(2)
                spump(3)
                mask_vs = b.cnt["VS"]
                b.wait(T, "t", "VS", mask_vs)
                for off, hh in ((0, sA), (1, sB)):
                    co = off * TPB
                    mm(PB[2][0:1, co:co + TPB], ONB[:, 0:1],
                       AT0[0:128, co:co + TPB], True, False)
                    mm(PB[2][0:1, co + 128:co + TPB], ONB[:, 0:1],
                       AT1[0:128, off * 128:(off + 1) * 128], False, False)
                    mm(PB[2][0:1, co:co + TPB], SZB[:, hh, :], PQ[:, pr, :],
                       False, True)
                    b.fence("t", "TS")
                b.wait(S, "s", "TS", b.cnt["TS"])
                act(DENS[0:1, 0:TPB], PB[2][0:1, 0:TPB], ACT.Identity,
                    bias=CVS[0:1, 1:2])
                act(DENS[0:1, TPB:2 * TPB], PB[2][0:1, TPB:2 * TPB], ACT.Identity,
                    bias=CVS[0:1, 1:2], sem="SS")
                b.wait(T, "t", "SS", b.cnt["SS"])
                opb = PB[4]
                for off, hh, sh, bas in ((0, hA, sA, 0), (1, hB, sB, 64)):
                    co = off * TPB
                    mm(PB[3][bas:bas + 64, 0:TPB], ONF[0:1, 0:64],
                       DENS[0:1, co:co + TPB], True, True, tile_position=(0, bas))
                    b.fence("t", "TS")
                    mm(opb[bas:bas + 64, 0:TPB], SEB[:, sh, :], PQ[:, pr, :],
                       True, False, tile_position=(0, bas))
                    mm(opb[bas:bas + 64, 0:TPB], VEX[0][:, hh, :],
                       AT0[0:128, co:co + TPB], False, False, tile_position=(0, bas))
                    mm(opb[bas:bas + 64, 128:TPB], VEX[1][:, hh, :],
                       AT1[0:128, off * 128:(off + 1) * 128], False, True,
                       tile_position=(0, bas))
                    b.fence("t", "TS")
                b.wait(V, "v", "TS", b.cnt["TS"])
                b.pre("v", [PB[3][:, 0:TPB]], [RBS[:, 0:TPB]])
                vop(V.reciprocal(RBS[:, 0:TPB], PB[3][:, 0:TPB]), "VS")
                vtt(OBF[:, pr, :], PB[4][:, 0:TPB], RBS[:, 0:TPB], ALU.mult, sem="VS")
                pair_tok[pr] = (b.cnt["VS"], b.cnt["SS"])
                xpump(1)
                spump(1)
            att_vs = b.cnt["VS"]
            att_ss2 = b.cnt["SS"]
            wev = {}
            for o in range(KC):
                pp = PB[o % 4]
                if o >= 4:
                    b.wait(T, "t", "VS", wev[o - 4])
                else:
                    b.wait(T, "t", "VS", att_vs)
                    b.wait(T, "t", "SS", att_ss2)
                for k in range(KC):
                    mm(pp[:, 0:TPB], WO[:, k, o * 128:(o + 1) * 128], OBF[:, k, :],
                       k == 0, k == KC - 1)
                b.fence("t", "TS")
                b.wait(V, "v", "TS", b.cnt["TS"])
                vstt(X1[cur][:, o, :], pp[:, 0:TPB], BIAS[:, 4, o:o + 1],
                     X1[cur][:, o, :], ALU.add, ALU.add, sem="VS")
                wev[o] = b.cnt["VS"]
                xpump(2)
                spump(3)
            xpump(50)
            spump(100)
            tguard[ci] = b.cnt["VS"]
            b.wait(SY, "sy", "VS")
            x1b_tok[ci] = stx(x1b[:, :, t0:t0 + TPB], X1[cur][:, :, :])
        barrier()
        ar.close()

    def emit_seg_c1():
        ar = Arena()
        WGA = r3(ar.w(KC * INNER), KC, INNER)
        WUP = r3(ar.w(KC * INNER), KC, INNER)
        wg4 = r3(wqg, KC, INNER)
        wu4 = r3(wqu, KC, INNER)
        ldq(1, WGA[:, 0:KC // 2, :], wg4[:, 0:KC // 2, :])
        ldq(2, WGA[:, KC // 2:KC, :], wg4[:, KC // 2:KC, :])
        ldq(1, WUP[:, 0:KC // 2, :], wu4[:, 0:KC // 2, :])
        ldq(2, WUP[:, KC // 2:KC, :], wu4[:, KC // 2:KC, :])

        NSUB = TPC // 128
        XQ = [r3(ar.bf16(KC * TPC), KC, TPC) for _ in range(2)]
        GSI = ar.bf16(1024)
        YB = [ar.f32(INNER), ar.f32(INNER)]
        b.dbg = getattr(b, "dbg", {})
        b.dbg["YB1"] = YB[0].tensor.name
        U32 = ar.f32(INNER // 2)
        UH = ar.f32(INNER // 2)
        TH = ar.bf16(INNER // 2)
        YQ = ar.bf16(INNER // 2)
        USG = ar.f32(12)
        yq3 = yqs.rearrange("(s p) i -> p s i", p=128)
        b.wait_pool(T, "t", LW)
        b.wait_pool(T, "t", LS)

        tguard = {}
        yq_tok = {}
        pend = [None]
        evac = {}
        ogg = [0]

        def quant_gen(qb, srow):
            c0 = 6 * qb
            vred(USG[:, c0 + 3:c0 + 4], YB[qb][:, :], ALU.max, absval=True)
            vcopy(AMY[:, srow:srow + 1], USG[:, c0 + 3:c0 + 4])
            b.pre("v", [USG[:, c0 + 3:c0 + 4]], [USG[:, c0 + 5:c0 + 6]])
            vop(V.reciprocal(USG[:, c0 + 5:c0 + 6], USG[:, c0 + 3:c0 + 4]))
            vts(USG[:, c0 + 4:c0 + 5], USG[:, c0 + 5:c0 + 6], 127.0, None,
                ALU.mult, sem="VS")
            yield
            for hf in range(2):
                sl = slice(hf * (INNER // 2), (hf + 1) * (INNER // 2))
                b.wait_tok(V, "v", yq_tok.get("last"))
                vts(U32[:, :], YB[qb][:, sl], USG[:, c0 + 4:c0 + 5], 128.5,
                    ALU.mult, ALU.add, sem="VS")
                yield
                for _ in round_sv_steps(YQ[:, :], U32[:, :], UH[:, :], TH[:, :]):
                    yield
                b.fence("v", "VS")
                b.wait(SY, "sy", "VS")
                yq_tok["last"] = stx(yq3[:, srow, sl], YQ[:, :])
                yield

        def pump(n):
            g = pend[0]
            if g is None:
                return
            try:
                for _ in range(n):
                    next(g)
            except StopIteration:
                pend[0] = None

        for ti in range(2 * NTC):
            cur = ti % 2
            t0 = (ti % 2) * SEQ + (ti // 2) * TPC
            if ti >= 2:
                b.wait(SY, "sy", "TS", tguard[ti - 2])
            ldx(XQ[cur][:, :, :], xqs[:, :, t0:t0 + TPC])
            b.wait_pool(T, "t", LX)
            for sbi in range(NSUB):
                srow = t0 // 128 + sbi
                qb = (ti * NSUB + sbi) % 2
                c0 = 6 * qb
                vts(USG[:, c0:c0 + 1], AMX[:, srow:srow + 1], 1.0 / 127.0,
                    None, ALU.mult)
                vtt(USG[:, c0 + 1:c0 + 2], USG[:, c0:c0 + 1], GLB[:, 0:1], ALU.mult)
                vtt(USG[:, c0 + 2:c0 + 3], USG[:, c0:c0 + 1], GLB[:, 1:2],
                    ALU.mult, sem="VS")
                scl_tok = b.cnt["VS"]
                for og in range(8):
                    obase = og * 512
                    g_ = ogg[0]
                    pg_ = PB[(g_ % 3) * 2]
                    pu_ = PB[(g_ % 3) * 2 + 1]
                    if g_ >= 3:
                        b.wait(T, "t", "SS", evac[g_ - 3][0])
                        b.wait(T, "t", "VS", evac[g_ - 3][1])
                    for k in range(KC):
                        mm(pg_[:, 0:512], XQ[cur][:, k, sbi * 128:(sbi + 1) * 128],
                           WGA[:, k, obase:obase + 512], k == 0, k == KC - 1)
                    b.fence("t", "TS")
                    for k in range(KC):
                        mm(pu_[:, 0:512], XQ[cur][:, k, sbi * 128:(sbi + 1) * 128],
                           WUP[:, k, obase:obase + 512], k == 0, k == KC - 1)
                    b.fence("t", "TS")
                    b.wait(S, "s", "TS", b.cnt["TS"] - 1)
                    b.wait(S, "s", "VS", scl_tok)
                    if g_ >= 2:
                        b.wait(S, "s", "VS", evac[g_ - 2][1])
                    act(GSI[:, (og % 2) * 512:(og % 2 + 1) * 512], pg_[:, 0:512],
                        ACT.Silu, scale=USG[:, c0 + 1:c0 + 2], sem="SS")
                    b.wait(V, "v", "SS", b.cnt["SS"])
                    b.wait(V, "v", "TS", b.cnt["TS"])
                    vstt(YB[qb][:, obase:obase + 512], pu_[:, 0:512],
                         USG[:, c0 + 2:c0 + 3],
                         GSI[:, (og % 2) * 512:(og % 2 + 1) * 512], ALU.mult, ALU.mult,
                         sem="VS")
                    evac[g_] = (b.cnt["SS"], b.cnt["VS"])
                    ogg[0] += 1
                    pump(5)
                pump(50)
                pend[0] = quant_gen(qb, srow)
                pump(2)
            tguard[ti] = b.cnt["TS"]
        pump(50)
        barrier()
        ar.close()

    def emit_seg_c2():
        ar = Arena()
        WD = r3(ar.w(IKC * D), IKC, D)
        wd4 = r3(wqd, IKC, D)
        ldq(1, WD[:, 0:IKC // 2, :], wd4[:, 0:IKC // 2, :])
        ldq(2, WD[:, IKC // 2:IKC, :], wd4[:, IKC // 2:IKC, :])
        YQF = [r3(ar.bf16(IKC * TPC), IKC, TPC) for _ in range(2)]
        X1T = [r3(ar.f32(KC * TPC), KC, TPC) for _ in range(2)]
        RB = ar.f32(TPC)
        RROW = ar.f32(TPC)
        OF = r3(ar.f32(KC * TPC), KC, TPC)
        b.dbg = getattr(b, "dbg", {})
        b.dbg["OF1"] = OF.tensor.name
        b.dbg["X1T1"] = X1T[1].tensor.name
        b.dbg["RB1"] = RB.tensor.name
        OT = [ar.f32(D) for _ in range(TPC // 128)]
        USD = ar.f32(2)
        out3 = out_hbm.rearrange("(s p) d -> p s d", p=128)
        b.wait_pool(T, "t", LW)
        b.wait_pool(T, "t", LS)

        tguard = {}
        ot_tok = {}
        for ti in range(2 * NTC):
            cur = ti % 2
            t0 = (ti % 2) * SEQ + (ti // 2) * TPC
            if ti >= 2:
                b.wait(SY, "sy", "TS", tguard[ti - 2][0])
                b.wait(SY, "sy", "VS", tguard[ti - 2][1])
            for k in range(IKC):
                b.dma(SY, "sy", YQF[cur][:, k, :],
                      yqs[t0:t0 + TPC, k * 128:(k + 1) * 128], LX, transpose=True)
            ldx(X1T[cur][:, :, :], x1b[:, :, t0:t0 + TPC])
            NSUB = TPC // 128
            for sbi in range(NSUB):
                srow = t0 // 128 + sbi
                b.wait(V, "v", "TS")
                b.wait(V, "v", "SS")
                vts(USD[:, 0:1], AMY[:, srow:srow + 1], 1.0 / 127.0, None, ALU.mult)
                vtt(USD[:, 1:2], USD[:, 0:1], GLB[:, 2:3], ALU.mult, sem="VS")
                b.wait(T, "t", "VS")
                mm(PB[4][0:1, 0:128], USD[:, 1:2], IDF[:, 0:128], True, True, sem="TS")
                b.wait(V, "v", "TS")
                vcopy(RROW[0:1, sbi * 128:(sbi + 1) * 128], PB[4][0:1, 0:128], sem="VS")
            b.wait(T, "t", "VS")
            mm(PB[4][:, 0:TPC], ONF[0:1, 0:128], RROW[0:1, 0:TPC], True, True, sem="TS")
            b.wait(S, "s", "TS")
            scopy(RB[:, :], PB[4][:, 0:TPC], sem="SS")
            b.wait_pool(T, "t", LX)
            b.wait_pool(T, "t", LW)
            b.wait_pool(T, "t", LS)
            c2vs0 = b.cnt["VS"]
            oev = {}
            for o in range(KC):
                pp = PB[0] if o % 2 == 0 else PB[1]
                if o >= 2:
                    b.wait(T, "t", "VS", oev[o - 2])
                else:
                    b.wait(T, "t", "VS", c2vs0)
                for k in range(IKC):
                    mm(pp[:, 0:TPC], WD[:, k, o * 128:(o + 1) * 128], YQF[cur][:, k, :],
                       k == 0, k == IKC - 1)
                b.fence("t", "TS")
                b.wait(V, "v", "TS", b.cnt["TS"])
                b.wait(V, "v", "SS", b.cnt["SS"])
                vtt(OF[:, o, :], pp[:, 0:TPC], RB[:, :], ALU.mult)
                vtt(OF[:, o, :], OF[:, o, :], X1T[cur][:, o, :], ALU.add, sem="VS")
                oev[o] = b.cnt["VS"]
            tg_ts = b.cnt["TS"]
            b.wait(T, "t", "SS")
            for sbi in range(TPC // 128):
                if ti >= 1:
                    b.wait_tok(V, "v", ot_tok.get((ti - 1, sbi)))
                for k in range(KC):
                    pbx = PB[4 + (k % 2)]
                    b.wait(T, "t", "VS")
                    tr(pbx[0:128, 0:128], OF[:, k, sbi * 128:(sbi + 1) * 128], IDF[:, :])
                    b.fence("t", "TS")
                    b.wait(V, "v", "TS")
                    vcopy(OT[sbi][:, k * 128:(k + 1) * 128], pbx[0:128, 0:128], sem="VS")
                b.wait(SY, "sy", "VS")
                ot_tok[(ti, sbi)] = stx(out3[:, t0 // 128 + sbi, :],
                                        OT[sbi][:, :])
            tguard[ti] = (tg_ts, b.cnt["VS"])
        barrier()
        ar.close()

    b.dma(G, "g", DUM[:, :], zeros_in[:, 0:2], LW)
    b.pre("g", (), [DUM[:, :]])
    b.op("g", G.engine_nop(), "GS")
    ldx(BIAS[:, :, :], bias_in)
    ldx(IDF[:, :], ident_f)
    ldx(IDB[:, :], ident_b)
    ldx(MSK[:, :], maskd)
    ldx(ONB[:, :], ones_b)
    ldx(ONF[:, :], ones_f)
    ldx(CVS[:, :], cvals)
    for eng, en in ((T, "t"), (V, "v"), (S, "s"), (G, "g")):
        b.wait_pool(eng, en, LX)

    emit_wq_prep()
    emit_w0_prep()
    emit_seg_a()
    emit_seg_b()
    emit_seg_c1()
    emit_seg_c2()

    b.wait_pool(SY, "sy", STP)
    b.stack.close()
    return b


_cached = {}


def _build():
    if "b" not in _cached:
        _cached["b"] = build_program()
    return _cached["b"]


def kernel(L0_attn_x, L0_attn_Wz, L0_attn_bz, L0_attn_Wh, L0_attn_bh, L0_attn_Wg,
           L0_attn_bg, L0_ffn_weight, L0_ffn_aux_weight1, L0_ffn_aux_weight2,
           L0_ffn_bias, L1_attn_Wq, L1_attn_Wk, L1_attn_Wv, L1_attn_Wo, L1_attn_bo,
           L1_ffn_w_gate, L1_ffn_w_up, L1_ffn_w_down):
    x = np.ascontiguousarray(np.asarray(L0_attn_x, np.float32))

    def t(a):
        return np.ascontiguousarray(np.asarray(a, np.float32).T)

    def c(a):
        return np.ascontiguousarray(np.asarray(a, np.float32))

    wzt_h = np.ascontiguousarray(
        np.transpose(np.asarray(L0_attn_Wz, np.float32), (1, 0, 2)).reshape(D, D))
    wht_h = np.ascontiguousarray(
        np.transpose(np.asarray(L0_attn_Wh, np.float32), (1, 0, 2)).reshape(D, D))
    biases = np.zeros((128, 5, KC), np.float32)
    biases[:, 0, :] = np.asarray(L0_attn_bz, np.float32).reshape(KC, 128).T
    biases[:, 1, :] = np.asarray(L0_attn_bh, np.float32).reshape(KC, 128).T
    biases[:, 2, :] = np.asarray(L0_attn_bg, np.float32).reshape(KC, 128).T
    biases[:, 3, :] = np.asarray(L0_ffn_bias, np.float32).reshape(KC, 128).T
    biases[:, 4, :] = np.asarray(L1_attn_bo, np.float32).reshape(KC, 128).T

    common = dict(
        wzt=wzt_h, wht=wht_h, wg0=c(L0_attn_Wg), w0pt=t(L0_ffn_weight),
        a10=c(L0_ffn_aux_weight1), a20=c(L0_ffn_aux_weight2),
        wqt=t(L1_attn_Wq), wkt=t(L1_attn_Wk), wvt=t(L1_attn_Wv), wot=t(L1_attn_Wo),
        wgr=t(L1_ffn_w_gate), wur=t(L1_ffn_w_up), wdr=t(L1_ffn_w_down),
        biases=biases,
        ident_f=np.eye(128, dtype=np.float32),
        ident_b=np.eye(128, dtype=bfdt),
        maskd=np.ascontiguousarray(np.tril(np.ones((128, 128), np.float32)).T).astype(bfdt),
        ones_b=np.ones((128, 128), dtype=bfdt),
        ones_f=np.ones((128, 128), dtype=np.float32),
        cvals=np.tile(np.array([EPS_LN, EPS_ATT, -128.0, 0, 0, 0, 0, 0], np.float32), (128, 1)),
        zeros=np.zeros((128, 2 * NH * HD), np.float32),
    )

    in_maps = []
    for ci in range(NCORES):
        m = dict(common)
        m["x"] = np.ascontiguousarray(x[2 * ci:2 * ci + 2].reshape(2 * SEQ, D))
        in_maps.append(m)

    bb = _build()
    res = run_bass_kernel_spmd(bb.nc, in_maps, core_ids=list(range(NCORES)))
    out = np.empty((16, SEQ, D), np.float32)
    for ci in range(NCORES):
        out[2 * ci:2 * ci + 2] = np.asarray(res.results[ci]["out"]).reshape(2, SEQ, D)
    return out
